# version 03
# baseline (speedup 1.0000x reference)
import contextlib
import math

import numpy as np
import ml_dtypes

import concourse.bass as bass
import concourse.mybir as mybir
from concourse.bass_utils import run_bass_kernel_spmd

F32 = mybir.dt.float32
BF16 = mybir.dt.bfloat16
FP16 = mybir.dt.float16
FP8 = mybir.dt.float8e4
AF = mybir.ActivationFunctionType
ALU = mybir.AluOpType

N_CORES = 8
SEQ = 2048
T = 256
Q = 128
N_TILES = 16
TILES_PER_SEQ = SEQ // T
NCH = T // Q

MAGIC = 12582912.0
TWO_PI = float(2.0 * math.pi)
INV_2PI = float(1.0 / TWO_PI)
LN_EPS = 1e-5
RMS_EPS = 1e-6

STREAM_OF = {"T": "tensor", "V": "vector", "S": "scalar", "G": "gpsimd",
             "QS": "sync", "QG": "gpsimd", "QA": "scalar"}
DMA_ENGS = ("QS", "QG", "QA")


class Op:
    __slots__ = ("eng", "emit", "deps", "idx", "eidx", "needed", "inc_ord",
                 "tag", "lagged")

    def __init__(self, eng, emit, deps, idx, eidx, tag, lagged):
        self.eng, self.emit, self.deps = eng, emit, deps
        self.idx, self.eidx = idx, eidx
        self.needed, self.inc_ord, self.tag = False, 0, tag
        self.lagged = lagged


class Sched:
    def __init__(self):
        self.ops = []
        self.eng_ops = {e: [] for e in STREAM_OF}
        self.res = {}

    def add(self, eng, emit, reads=(), writes=(), tag="", lagged=False):
        deps = set()
        for t in reads:
            e = self.res.get(t)
            if e and e["w"] is not None:
                deps.add(e["w"])
        for t in writes:
            e = self.res.get(t)
            if e:
                if e["w"] is not None:
                    deps.add(e["w"])
                deps.update(e["r"])
        idx = len(self.ops)
        op = Op(eng, emit, deps, idx, len(self.eng_ops[eng]), tag, lagged)
        self.ops.append(op)
        self.eng_ops[eng].append(op)
        for t in reads:
            self.res.setdefault(t, {"w": None, "r": []})["r"].append(idx)
        for t in writes:
            self.res[t] = {"w": idx, "r": []}
        return idx

    def finalize(self):
        SAFE_DIST = 12
        for op in self.ops:
            best = {}
            for d in op.deps:
                dop = self.ops[d]
                same_stream = STREAM_OF[dop.eng] == STREAM_OF[op.eng]
                if same_stream and not dop.lagged:
                    if dop.eng in DMA_ENGS and op.eng in DMA_ENGS:
                        continue
                    if dop.eng == "T" and op.eng == "T":
                        continue
                    if (dop.eng == op.eng and dop.eng not in DMA_ENGS
                            and op.eidx - dop.eidx >= SAFE_DIST):
                        continue
                    if dop.eng not in DMA_ENGS and op.eng in DMA_ENGS:
                        pass
                e = dop.eng
                if e not in best or self.ops[best[e]].eidx < dop.eidx:
                    best[e] = d
            op.deps = set(best.values())
            for d in op.deps:
                self.ops[d].needed = True
        for e, ops in self.eng_ops.items():
            k = 0
            for op in ops:
                if e in DMA_ENGS or op.needed:
                    k += 1
                    op.inc_ord = k

    def emit_stream(self, stream, engobj, sems):
        hw = {}
        for op in self.ops:
            if STREAM_OF[op.eng] != stream:
                continue
            for d in sorted(op.deps, key=lambda i: self.ops[i].eng):
                dop = self.ops[d]
                step = 16 if dop.eng in DMA_ENGS else 1
                val = dop.inc_ord * step
                if hw.get(dop.eng, -1) >= val:
                    continue
                hw[dop.eng] = val
                engobj.wait_ge(sems[dop.eng], val)
            if op.eng in DMA_ENGS and op.inc_ord > 1:
                val = 16 * (op.inc_ord - 1)
                if hw.get(op.eng, -1) < val:
                    hw[op.eng] = val
                    engobj.wait_ge(sems[op.eng], val)
            inst = op.emit(engobj)
            if op.inc_ord:
                step = 16 if op.eng in DMA_ENGS else 1
                assert inst is not None, f"op {op.tag} must return its instruction"
                inst.then_inc(sems[op.eng], step)



def _perm_bc():
    p = np.empty(2048, dtype=np.int64)
    r = np.arange(1024)
    pc, m, i = r // 128, (r % 128) // 32, r % 32
    h = m * 8 + pc
    p[:1024] = h * 64 + 2 * i
    p[1024:] = h * 64 + 2 * i + 1
    return p


def _consts():
    c = {}
    ident = np.eye(128, dtype=np.float32)
    c["ident_bf"] = ident.astype(ml_dtypes.bfloat16)
    c["ident_f16"] = ident.astype(np.float16)
    c["ident_f32_32"] = np.eye(32, dtype=np.float32)
    tri = np.where(np.arange(128)[None, :] >= np.arange(128)[:, None],
                   0.0, -1e30).astype(np.float32)
    c["tri_rep4"] = np.tile(tri, (1, 4)).astype(ml_dtypes.bfloat16)
    c["anti_ident"] = (1.0 - np.eye(128, dtype=np.float32)).astype(np.float16)
    invfreq = (1.0 / (10000.0 ** (np.arange(32) / 32.0))).astype(np.float32)
    selth = np.zeros((32, 1024), dtype=np.float32)
    seldec = np.zeros((32, 1024), dtype=np.float32)
    rr = np.arange(1024)
    hh_of_r = ((rr % 128) // 32) * 8 + rr // 128
    selth[hh_of_r, rr] = invfreq[rr % 32]
    seldec[hh_of_r, rr] = 1.0
    c["selth"] = selth
    c["seldec"] = seldec
    seldch = np.zeros((32, 128), dtype=np.float32)
    for h in range(32):
        seldch[h, 32 * (h // 8):32 * (h // 8) + 32] = 1.0
    c["seldch"] = seldch
    i2d = np.zeros((32, 8), dtype=np.float32)
    i2d[np.arange(32), np.arange(32) % 8] = 1.0
    c["i2dch"] = i2d
    c["ones2_bf"] = np.ones((128, 128), dtype=ml_dtypes.bfloat16)
    c["ones1_f32"] = np.ones((1, 128), dtype=np.float32)
    c["onesdt"] = np.ones((32, 256), dtype=np.float32)
    return c


def setup_core_inputs(inputs):
    x = np.ascontiguousarray(
        np.asarray(inputs["L0_attn_hidden"]).reshape(16, 2048, 1024))
    W_in = np.asarray(inputs["L0_attn_W_in"])
    perm = _perm_bc()
    consts = _consts()
    shared = dict(
        Wz=np.ascontiguousarray(W_in[:, 0:2048]),
        Wxr=np.ascontiguousarray(W_in[:, 2048:4096]),
        WB=np.ascontiguousarray(W_in[:, 4096:6144][:, perm]),
        WC=np.ascontiguousarray(W_in[:, 6144:8192][:, perm]),
        Wdt=np.ascontiguousarray(
            np.concatenate([W_in[:, 8192:8224],
                            np.zeros((1024, 96), dtype=W_in.dtype)], axis=1)
            .reshape(8, 128, 128).transpose(1, 0, 2)),
        Wout=np.ascontiguousarray(np.asarray(inputs["L0_attn_W_out"])),
        ffnk=np.ascontiguousarray(
            np.asarray(inputs["L0_ffn_kernel"]).reshape(8, 128, 1024)
            .transpose(1, 0, 2)),
        ffnb_rep=np.ascontiguousarray(np.broadcast_to(
            np.asarray(inputs["L0_ffn_bias"])[None, :], (128, 1024))),
        s_in_col=np.ascontiguousarray(np.broadcast_to(
            np.asarray(inputs["L0_ffn_input_scale"])[:, None], (128, 1))),
        s_k_col=np.ascontiguousarray(np.broadcast_to(
            np.asarray(inputs["L0_ffn_kernel_scale"])[:, None], (128, 1))),
        dtb_col=np.ascontiguousarray(
            np.asarray(inputs["L0_attn_dt_bias"])[:, None]),
        alog_col=np.ascontiguousarray(
            np.asarray(inputs["L0_attn_A_log"])[:, None]),
        nw_fm=np.ascontiguousarray(
            np.asarray(inputs["L0_attn_norm_weight"]).reshape(16, 128).T),
        **consts,
    )
    return [dict(shared, x=np.ascontiguousarray(
        x[2 * ci:2 * ci + 2].reshape(4096, 1024))) for ci in range(N_CORES)]



class Ctx:
    pass


def build_nc(taps=(), n_tiles=N_TILES):
    nc = bass.Bass("TRN2", target_bir_lowering=False)
    P = Sched()
    X = Ctx()
    X.nc, X.P, X.n_tiles = nc, P, n_tiles
    X.taps = {}

    D = {}
    def din(name, shape, dt):
        D[name] = nc.dram_tensor(name, list(shape), dt, kind="ExternalInput").ap()
    def dout(name, shape, dt):
        D[name] = nc.dram_tensor(name, list(shape), dt, kind="ExternalOutput").ap()
    def dscr(name, shape, dt):
        D[name] = nc.dram_tensor(name, list(shape), dt, kind="Internal").ap()

    din("x", (4096, 1024), F32)
    for w in ("Wz", "Wxr", "WB", "WC"):
        din(w, (1024, 2048), F32)
    din("Wdt", (128, 8, 128), F32)
    din("Wout", (2048, 1024), F32)
    din("ffnk", (128, 8, 1024), BF16)
    din("ffnb_rep", (128, 1024), BF16)
    din("s_in_col", (128, 1), F32)
    din("s_k_col", (128, 1), F32)
    din("dtb_col", (32, 1), F32)
    din("alog_col", (32, 1), F32)
    din("nw_fm", (128, 16), F32)
    din("ident_bf", (128, 128), BF16)
    din("ident_f16", (128, 128), FP16)
    din("ident_f32_32", (32, 32), F32)
    din("tri_rep4", (128, 512), BF16)
    din("anti_ident", (128, 128), FP16)
    din("selth", (32, 1024), F32)
    din("seldec", (32, 1024), F32)
    din("seldch", (32, 128), F32)
    din("i2dch", (32, 8), F32)
    din("ones2_bf", (128, 128), BF16)
    din("ones1_f32", (1, 128), F32)
    din("onesdt", (32, 256), F32)
    dout("out", (4096, 1024), F32)
    for name, shape, dt in taps:
        dout("tap_" + name, list(shape), dt)
        X.taps[name] = D["tap_" + name]
    for s in ("scr_z", "scr_B", "scr_C", "scr_xr", "scr_out"):
        dscr(s, (128, 16384), BF16)

    es = contextlib.ExitStack()
    S = {}
    def asb(name, shape, dt):
        S[name] = es.enter_context(nc.sbuf_tensor(name, list(shape), dt))
    PS = {}
    def aps(name, shape, dt):
        PS[name] = es.enter_context(nc.psum_tensor(name, list(shape), dt))

    asb("Wdt_sb", (128, 8, 128), BF16)
    asb("ffn_q", (128, 8, 1024), FP8)
    asb("ffnb", (128, 1024), BF16)
    asb("H", (128, 2048), F32)
    asb("Hb", (128, 2048), FP16)
    asb("id_bf", (128, 128), BF16)
    asb("id_f16", (128, 128), FP16)
    asb("id32", (32, 32), F32)
    asb("tri4", (128, 512), BF16)
    asb("antiI", (128, 128), FP16)
    asb("selth_sb", (32, 1024), F32)
    asb("seldec_sb", (32, 1024), F32)
    asb("seldch_sb", (32, 128), F32)
    asb("i2dch_sb", (32, 8), F32)
    asb("ones2", (128, 128), BF16)
    asb("ones1f", (1, 128), F32)
    asb("onesdt_sb", (32, 256), F32)
    asb("nw_sb", (128, 16), F32)
    asb("scals", (128, 8), F32)
    asb("sc32", (32, 8), F32)
    asb("Wbuf", (128, 16384), BF16)
    asb("xt", (128, 2, 2, 1024), F32)
    asb("xn_tm", (128, 1024), BF16)
    asb("xn_fm", (128, 2, 8, 256), BF16)
    asb("xn_q", (128, 8, 256), FP8)
    asb("dts", (32, 2, 257), F32)
    asb("Lc", (32, 257), F32)
    asb("ldt", (32, 257), F32)
    asb("zs", (128, 2, 2048), FP16)
    asb("yg", (128, 2, 2048), FP16)
    asb("bufA", (128, 16, 256), FP16)
    asb("bufB", (128, 16, 256), FP16)
    asb("bufC", (128, 16, 256), FP16)
    asb("xr", (128, 2, 2048), FP16)
    asb("cth", (128, 8, 256), FP16)
    asb("sth", (128, 8, 256), FP16)
    asb("Btm", (128, 2048), FP16)
    asb("X2", (128, 2048), FP16)
    asb("e1b", (128, 1024), F32)
    asb("scr4k", (128, 1024), F32)
    asb("M1b", (128, 1024), FP16)
    asb("M2b", (128, 1024), FP16)
    asb("Sm1", (128, 1024), FP16)
    asb("Sm2", (128, 1024), FP16)
    asb("decr", (128, 8, 128), FP16)
    asb("Cdec", (128, 2, 8, 128), FP16)
    asb("Pq", (32, 128), F32)
    asb("Psplit", (64, 128), BF16)
    asb("Pflat", (128, 1024), BF16)
    asb("Mt", (128, 32), F32)
    asb("Mcol1", (128, 32), F32)
    asb("l1t", (128, 32), F32)
    asb("w1t", (128, 32), F32)
    asb("w2t", (128, 32), F32)
    asb("ratio", (128, 32), F32)
    asb("exS", (128, 32), F32)
    asb("ccol", (128, 32), F32)
    asb("rhsd", (32, 8), F32)
    asb("gbuf", (128, 2, 1024), BF16)
    asb("sgb", (128, 1024), F32)
    asb("outb", (128, 1024), F32)
    asb("lnstat", (128, 12), F32)
    asb("lnmv", (128, 2), F32)
    asb("lnsd", (128, 1), F32)
    asb("lnrs", (128, 1), F32)
    asb("lnmb", (128, 1), F32)
    asb("dtmp", (32, 256), F32)
    asb("dtmp2", (32, 256), F32)
    asb("dcol", (32, 2), F32)

    aps("pmm0", (128, 512), F32)
    aps("pmm1", (128, 512), F32)
    aps("pe", (128, 1024), F32)
    aps("psc", (128, 512), F32)
    aps("psc2", (128, 512), F32)
    aps("py", (128, 512), F32)
    aps("pt", (128, 1024), FP16)

    X.D, X.S, X.PS = D, S, PS
    _build(X)
    P.finalize()

    sems = {e: es.enter_context(nc.semaphore("sem_" + e)) for e in STREAM_OF}
    with es:
        blk = es.enter_context(nc.Block())
        @blk.sync
        def _(eng): P.emit_stream("sync", eng, sems)
        @blk.tensor
        def _(eng): P.emit_stream("tensor", eng, sems)
        @blk.vector
        def _(eng): P.emit_stream("vector", eng, sems)
        @blk.scalar
        def _(eng): P.emit_stream("scalar", eng, sems)
        @blk.gpsimd
        def _(eng): P.emit_stream("gpsimd", eng, sems)
    return nc



def _build(X):
    D, S, PS, add = X.D, X.S, X.PS, X.P.add

    def tap(name, ap_fn, reads):
        if name in X.taps:
            dst = X.taps[name]
            add("QG", lambda e: e.dma_start(out=dst, in_=ap_fn()),
                reads=list(reads), writes=["tapw_" + name], tag="tap_" + name)
    X.tap = tap

    for nm, dst in [("ident_bf", "id_bf"), ("ident_f16", "id_f16"),
                    ("ident_f32_32", "id32"), ("tri_rep4", "tri4"),
                    ("anti_ident", "antiI"), ("selth", "selth_sb"),
                    ("seldec", "seldec_sb"), ("seldch", "seldch_sb"),
                    ("i2dch", "i2dch_sb"), ("ones2_bf", "ones2"),
                    ("ones1_f32", "ones1f"), ("onesdt", "onesdt_sb"),
                    ("nw_fm", "nw_sb"), ("ffnb_rep", "ffnb")]:
        add("QS", lambda e, a=D[nm], b=S[dst]: e.dma_start(out=b[:], in_=a),
            writes=[dst], tag="ld_" + nm)
    add("QG", lambda e: e.dma_start(out=S["Wdt_sb"][:], in_=D["Wdt"]),
        writes=["Wdt_sb"], tag="ld_Wdt")
    add("QS", lambda e: e.dma_start(out=S["scals"][:, 0:1], in_=D["s_in_col"]),
        writes=["sin0"], tag="ld_sin")
    add("QS", lambda e: e.dma_start(out=S["scals"][:, 1:2], in_=D["s_k_col"]),
        writes=["sk0"], tag="ld_sk")
    add("QS", lambda e: e.dma_start(out=S["sc32"][:, 0:1], in_=D["dtb_col"]),
        writes=["dtb"], tag="ld_dtb")
    add("QS", lambda e: e.dma_start(out=S["sc32"][:, 1:2], in_=D["alog_col"]),
        writes=["alog"], tag="ld_alog")
    add("V", lambda e: e.memset(S["scals"][:, 5:6], LN_EPS), writes=["epsln"], tag="epsln")
    add("V", lambda e: e.memset(S["scals"][:, 6:7], RMS_EPS), writes=["epsrms"], tag="epsrms")
    add("V", lambda e: e.memset(S["scals"][:, 7:8], math.pi / 2), writes=["pih"], tag="pih")
    add("V", lambda e: e.reciprocal(out=S["scals"][:, 2:3], in_=S["scals"][:, 0:1]),
        reads=["sin0"], writes=["rsin"], tag="rsin")
    add("V", lambda e: e.reciprocal(out=S["scals"][:, 3:4], in_=S["scals"][:, 1:2]),
        reads=["sk0"], writes=["rsk"], tag="rsk")
    add("V", lambda e: e.tensor_tensor(out=S["scals"][:, 4:5], in0=S["scals"][:, 0:1],
                                       in1=S["scals"][:, 1:2], op=ALU.mult),
        reads=["sin0", "sk0"], writes=["sprod"], tag="sprod")
    add("S", lambda e: e.activation(out=S["sc32"][:, 3:4], in_=S["sc32"][:, 1:2],
                                    func=AF.Exp),
        reads=["alog"], writes=["aexp"], tag="aexp")
    add("V", lambda e: e.tensor_scalar(out=S["sc32"][:, 2:3], in0=S["sc32"][:, 3:4],
                                       scalar1=-1.0, scalar2=None, op0=ALU.mult),
        reads=["aexp"], writes=["A"], tag="A")
    add("QS", lambda e: e.dma_start(
            out=S["Wbuf"][:, 0:8192].rearrange("p (c n) -> p c n", c=8),
            in_=D["ffnk"]),
        writes=["WbufA"], tag="ld_ffnk")
    add("V", lambda e: e.tensor_scalar(
            out=S["ffn_q"][:].rearrange("p c n -> p (c n)"),
            in0=S["Wbuf"][:, 0:8192], scalar1=S["scals"][:, 3:4],
            scalar2=None, op0=ALU.mult),
        reads=["WbufA", "rsk"], writes=["ffn_q"], tag="q_ffnk")
    def _nw_fold():
        for kc2 in range(16):
            for hf in range(2):
                sl = slice(kc2 * 1024 + hf * 512, kc2 * 1024 + hf * 512 + 512)
                add("V", lambda e, s=sl, k=kc2: e.tensor_scalar(
                        out=S["xn_tm"][:, 0:512],
                        in0=S["Wbuf"][:, s], scalar1=S["nw_sb"][:, k:k + 1],
                        scalar2=None, op0=ALU.mult),
                    reads=["WbufA", "WbufB", "nw_sb", "xn_tm"], writes=["xn_tm"],
                    tag=f"nwf{kc2}_{hf}")
                add("V", lambda e, s=sl: e.tensor_copy(
                        out=S["Wbuf"][:, s], in_=S["xn_tm"][:, 0:512]),
                    reads=["xn_tm", "WbufA", "WbufB"],
                    writes=["WbufA" if kc2 < 8 else "WbufB"], tag=f"nwc{kc2}_{hf}")

    import os as _os2
    wlist = [] if _os2.environ.get("KLNONLY") else [
        ("Wz", "scr_z", 8), ("WB", "scr_B", 8),
        ("WC", "scr_C", 8), ("Wxr", "scr_xr", 8),
        ("Wout", "scr_out", 16)]
    for wsrc, wscr, kc in wlist:
        add("QG", lambda e, a=D[wsrc], kk=kc: e.dma_start(
                out=S["Wbuf"][:].rearrange("p (c n) -> p c n", c=kk),
                in_=a.rearrange("(c p) n -> p c n", p=128)),
            reads=["ffn_q"], writes=["WbufA", "WbufB"], tag="cast_" + wsrc)
        if wsrc == "Wout":
            _nw_fold()
        if kc == 8:
            for hf2 in range(2):
                add("QG", lambda e, b=D[wscr], h=hf2: e.dma_start(
                        out=b[:, h * 8192:(h + 1) * 8192],
                        in_=S["Wbuf"][:].rearrange("p (c n) -> p c n", c=8)
                        [:, :, h * 1024:(h + 1) * 1024]),
                    reads=["WbufA", "WbufB"], writes=[wscr], tag=f"st_{wscr}{hf2}")
        else:
            add("QG", lambda e, b=D[wscr]: e.dma_start(out=b, in_=S["Wbuf"][:]),
                reads=["WbufA", "WbufB"], writes=[wscr], tag="st_" + wscr)

    import os as _os
    if _os.environ.get("KLNONLY"):
        _ln_dt(X, 0)
        return
    for i in range(X.n_tiles):
        _tile(X, i)

    import os
    if int(os.environ.get("KPHASES", "99")) >= 6:
        add("G", lambda e: e.nop(),
            reads=[f"outdma{i}" for i in range(X.n_tiles)], tag="final")


def _ln_dt(X, j):
    S, PS, D, add = X.S, X.PS, X.D, X.P.add
    pj = j % 2
    xP = f"xtP{pj}"
    fmP = f"xnfmP{pj}"
    dtP = f"dtP{pj}"
    xtj = S["xt"][:, pj]
    add("QS", lambda e: e.dma_start(
            out=xtj, in_=D["x"][j * 256:(j + 1) * 256, :]
            .rearrange("(tc p) n -> p tc n", p=128)),
        writes=[xP], tag=f"ldx{j}")
    X.tap(f"xt{j}", lambda: xtj.rearrange("p a n -> p (a n)"), [xP])
    for tc in range(2):
        xs = xtj[:, tc, :]
        add("V", lambda e, a=xs: e.tensor_reduce(out=S["lnmv"][:, 0:1], in_=a,
                                                 axis=mybir.AxisListType.X,
                                                 op=ALU.add),
            reads=[xP, "lnmv"], writes=["lnmv"], tag=f"lsum_{j}_{tc}", lagged=True)
        add("S", lambda e, a=xs: e.activation(out=S["xn_tm"][:], in_=a,
                                              func=AF.Square),
            reads=[xP, "xn_tm"], writes=["xn_tm"], tag=f"lsq_{j}_{tc}")
        add("V", lambda e: e.tensor_reduce(out=S["lnsd"][:, 0:1],
                                           in_=S["xn_tm"][:].bitcast(BF16),
                                           axis=mybir.AxisListType.X, op=ALU.add),
            reads=["xn_tm", "lnsd"], writes=["lnsd"], tag=f"lsqr_{j}_{tc}",
            lagged=True)
        add("V", lambda e: e.tensor_scalar(out=S["lnstat"][:, 0:1],
                                           in0=S["lnmv"][:, 0:1],
                                           scalar1=1.0 / 1024, scalar2=None,
                                           op0=ALU.mult),
            reads=["lnmv", "lnstat"], writes=["lnstat"], tag=f"lmean_{j}_{tc}")
        add("V", lambda e: e.tensor_scalar(out=S["lnstat"][:, 1:2],
                                           in0=S["lnsd"][:, 0:1],
                                           scalar1=1.0 / 1024, scalar2=None,
                                           op0=ALU.mult),
            reads=["lnsd", "lnstat"], writes=["lnstat"], tag=f"lms_{j}_{tc}")
        add("V", lambda e: e.tensor_tensor(out=S["lnstat"][:, 2:3],
                                           in0=S["lnstat"][:, 0:1],
                                           in1=S["lnstat"][:, 0:1],
                                           op=ALU.mult),
            reads=["lnstat"], writes=["lnstat"], tag=f"lm2_{j}_{tc}")
        add("V", lambda e: e.tensor_tensor(out=S["lnmv"][:, 1:2],
                                           in0=S["lnstat"][:, 1:2],
                                           in1=S["lnstat"][:, 2:3],
                                           op=ALU.subtract),
            reads=["lnstat", "lnmv"], writes=["lnmv"], tag=f"lvar_{j}_{tc}")
        add("S", lambda e: e.activation(out=S["lnsd"][:], in_=S["lnmv"][:, 1:2],
                                        func=AF.Sqrt, bias=S["scals"][:, 5:6]),
            reads=["lnmv", "epsln", "lnsd"], writes=["lnsd"], tag=f"lnsq_{j}_{tc}")
        add("V", lambda e: e.reciprocal(out=S["lnrs"][:], in_=S["lnsd"][:]),
            reads=["lnsd", "lnrs"], writes=["lnrs"], tag=f"lnrs_{j}_{tc}")
        if tc == 0:
            X.tap(f"lnstat{j}", lambda: S["lnstat"][:, :], ["lnstat"])
            X.tap(f"lnsum{j}", lambda: S["lnmv"][:, :], ["lnmv"])
            X.tap(f"lnsq{j}", lambda: S["lnsd"][:, :], ["lnsd"])
        add("V", lambda e: e.tensor_tensor(out=S["lnmb"][:], in0=S["lnstat"][:, 0:1],
                                           in1=S["lnrs"][:], op=ALU.mult),
            reads=["lnstat", "lnrs", "lnmb"], writes=["lnmb"], tag=f"lnmb_{j}_{tc}")
        add("V", lambda e, a=xs: e.tensor_scalar(
                out=S["xn_tm"][:], in0=a, scalar1=S["lnrs"][:, 0:1],
                scalar2=S["lnmb"][:, 0:1], op0=ALU.mult, op1=ALU.subtract),
            reads=[xP, "lnrs", "lnmb"], writes=["xn_tm"], tag=f"lnap_{j}_{tc}")
        for kc in range(8):
            add("T", lambda e, k=kc: e.transpose(
                    out=PS["pt"][:, k * 128:(k + 1) * 128].bitcast(BF16),
                    in_=S["xn_tm"][:, k * 128:(k + 1) * 128],
                    identity=S["id_bf"][:]),
                reads=["xn_tm", "id_bf"], writes=["pt"], tag=f"xnt_{j}_{tc}_{kc}")
        add("S", lambda e, t=tc: e.copy(
                out=S["xn_fm"][:, pj, :, t * 128:(t + 1) * 128],
                in_=PS["pt"][:].bitcast(BF16).rearrange("p (c n) -> p c n", c=8)),
            reads=["pt"], writes=[fmP], tag=f"xnev_{j}_{tc}")
    for kc in range(8):
        add("T", lambda e, k=kc: e.matmul(
                out=PS["pmm0"][:, 0:256], lhsT=S["Wdt_sb"][:, k, :],
                rhs=S["xn_fm"][:, pj, k, :], start=(k == 0), stop=(k == 7)),
            reads=[fmP, "Wdt_sb"], writes=["pmm0"], tag=f"dtmm_{j}_{kc}")
    add("S", lambda e: e.activation(out=S["dtmp"][:], in_=PS["pmm0"][0:32, 0:256],
                                    func=AF.Abs, bias=S["sc32"][:, 0:1]),
        reads=["pmm0", "dtb"], writes=["dtmp"], tag=f"dta_{j}")
    add("S", lambda e: e.activation(out=S["dtmp2"][:], in_=S["dtmp"][:],
                                    func=AF.Exp, scale=-1.0),
        reads=["dtmp"], writes=["dtmp2"], tag=f"dte_{j}")
    add("V", lambda e: e.tensor_scalar(out=S["dtmp"][:], in0=S["dtmp2"][:],
                                       scalar1=1.0, scalar2=None, op0=ALU.add),
        reads=["dtmp2"], writes=["dtmp"], tag=f"dtp1_{j}")
    add("S", lambda e: e.activation(out=S["dtmp2"][:], in_=S["dtmp"][:], func=AF.Ln),
        reads=["dtmp"], writes=["dtmp2"], tag=f"dtl_{j}")
    add("V", lambda e: e.tensor_scalar(out=S["dtmp"][:], in0=PS["pmm0"][0:32, 0:256],
                                       scalar1=S["sc32"][:, 0:1], scalar2=0.0,
                                       op0=ALU.add, op1=ALU.max),
        reads=["pmm0", "dtb"], writes=["dtmp"], tag=f"dtr_{j}")
    add("V", lambda e: e.tensor_tensor(out=S["dts"][:, pj, 0:256], in0=S["dtmp"][:],
                                       in1=S["dtmp2"][:], op=ALU.add),
        reads=["dtmp", "dtmp2"], writes=[dtP], tag=f"dt_{j}")
    X.tap(f"dt{j}", lambda pj_=pj: S["dts"][:, pj_, :], [dtP])
    X.tap(f"xnfm{j}", lambda pj_=pj: S["xn_fm"][:, pj_].rearrange("p c n -> p (c n)"),
          [fmP])


def _tile(X, i):
    S, add = X.S, X.P.add
    par = i % 2
    seq_start = (i % TILES_PER_SEQ) == 0

    if i == 0:
        _ln_dt(X, 0)
    if i + 1 < X.n_tiles:
        _ln_dt(X, i + 1)
        add("V", lambda e: e.tensor_copy(out=S["dts"][:, par, 256:257],
                                         in_=S["dts"][:, 1 - par, 0:1]),
            reads=[f"dtP{1 - par}", f"dtP{par}"], writes=[f"dtP{par}"],
            tag=f"dtx{i}")
    else:
        add("V", lambda e: e.memset(S["dts"][:, par, 256:257], 1.0),
            reads=[f"dtP{par}"], writes=[f"dtP{par}"], tag=f"dtx{i}")

    add("S", lambda e: e.activation(out=S["ldt"][:], in_=S["dts"][:, par],
                                    func=AF.Ln, scale=0.5),
        reads=[f"dtP{par}"], writes=["ldt"], tag=f"ldt{i}")
    if seq_start:
        add("V", lambda e: e.memset(S["Lc"][:, 0:1], 0.0),
            reads=["Lc"], writes=["Lc"], tag=f"Lz{i}")
    else:
        add("V", lambda e: e.tensor_copy(out=S["Lc"][:, 0:1], in_=S["Lc"][:, 256:257]),
            reads=["Lc"], writes=["Lc"], tag=f"Lcp{i}")
    add("V", lambda e: e.tensor_tensor_scan(
            out=S["Lc"][:, 1:257], data0=S["onesdt_sb"][:],
            data1=S["dts"][:, par, 0:256], initial=S["Lc"][:, 0:1],
            op0=ALU.mult, op1=ALU.add),
        reads=[f"dtP{par}", "onesdt_sb", "Lc"], writes=["Lc"], tag=f"Lsc{i}")
    X.tap(f"L{i}", lambda: S["Lc"][:, :], ["Lc"])

    import os
    ph = int(os.environ.get("KPHASES", "99"))
    if ph >= 1:
        _ffn(X, i, par)
    if ph >= 2:
        _inproj(X, i, par)
    if ph >= 3:
        _angles(X, i, par)
    if ph >= 4:
        _rope(X, i, par)
    if ph >= 5:
        for c in range(NCH):
            _chunk(X, i, c, par, seq_start and c == 0)
    if ph >= 6:
        _epilogue(X, i, par)


def _ffn(X, i, par):
    S, PS, add = X.S, X.PS, X.P.add
    fmP = f"xnfmP{par}"
    add("V", lambda e: e.tensor_scalar(
            out=S["xn_q"][:].rearrange("p c n -> p (c n)"),
            in0=S["xn_fm"][:, par].rearrange("p c n -> p (c n)"),
            scalar1=S["scals"][:, 2:3], scalar2=None, op0=ALU.mult),
        reads=[fmP, "rsin"], writes=["xn_q"], tag=f"xq{i}")
    X.tap(f"xnq{i}", lambda: S["xn_q"][:].rearrange("p c n -> p (c n)"), ["xn_q"])
    for tc in range(2):
        for ng in range(2):
            pm = "pmm0" if ng == 0 else "pmm1"
            for kc in range(8):
                add("T", lambda e, k=kc, t=tc, n=ng, p=pm: e.matmul(
                        out=PS[p][:, 0:512],
                        lhsT=S["xn_q"][:, k, t * 128:(t + 1) * 128],
                        rhs=S["ffn_q"][:, k, n * 512:(n + 1) * 512],
                        start=(k == 0), stop=(k == 7)),
                    reads=["xn_q", "ffn_q"], writes=[pm], tag=f"ffmm{i}_{tc}_{ng}_{kc}")
            add("S", lambda e, p=pm: e.activation(
                    out=S["xn_tm"][:, 0:512].bitcast(BF16), in_=PS[p][:, 0:512],
                    func=AF.Copy, scale=S["scals"][:, 4:5]),
                reads=[pm, "sprod", "xn_tm"], writes=["xn_tm"], tag=f"ffc{i}_{tc}_{ng}")
            add("V", lambda e, n=ng: e.tensor_tensor(
                    out=S["xn_tm"][:, 512:1024].bitcast(BF16),
                    in0=S["xn_tm"][:, 0:512].bitcast(BF16),
                    in1=S["ffnb"][:, n * 512:(n + 1) * 512], op=ALU.add),
                reads=["xn_tm", "ffnb"], writes=["xn_tm"], tag=f"ffv{i}_{tc}_{ng}")
            add("S", lambda e, t=tc, n=ng: e.activation(
                    out=S["gbuf"][:, t, n * 512:(n + 1) * 512],
                    in_=S["xn_tm"][:, 512:1024].bitcast(BF16), func=AF.Silu),
                reads=["xn_tm"], writes=["gbuf"], tag=f"ffs{i}_{tc}_{ng}")
    X.tap(f"g{i}", lambda: S["gbuf"][:].rearrange("p c n -> p (c n)"), ["gbuf"])


def _inproj(X, i, par):
    S, PS, D, add = X.S, X.PS, X.D, X.P.add
    fmP = f"xnfmP{par}"
    for scr, dstb, kind in [("scr_B", "bufA", "B"), ("scr_C", "bufB", "C")]:
        for hf in range(2):
            add("QS", lambda e, a=D[scr], h=hf: e.dma_start(
                    out=S["Wbuf"][:, h * 8192:(h + 1) * 8192],
                    in_=a[:, h * 8192:(h + 1) * 8192]),
                writes=["WbufA" if hf == 0 else "WbufB"], tag=f"ldw{kind}{i}_{hf}")
        for mc in range(16):
            pm = "pmm0" if (mc % 2 == 0) else "pmm1"
            half = 0 if mc < 8 else 1
            for kc in range(8):
                add("T", lambda e, m=mc, k=kc, p=pm, h=half: e.matmul(
                        out=PS[p][:, 0:256],
                        lhsT=S["Wbuf"][:, h * 8192 + k * 1024 + (m % 8) * 128:
                                        h * 8192 + k * 1024 + (m % 8) * 128 + 128],
                        rhs=S["xn_fm"][:, par, k, :],
                        start=(k == 0), stop=(k == 7)),
                    reads=["WbufA" if half == 0 else "WbufB", fmP],
                    writes=[pm], tag=f"ip{kind}{i}_{mc}_{kc}")
            if mc % 2 == 0:
                add("S", lambda e, m=mc, p=pm, d=dstb: e.copy(
                        out=S[d][:, m, :], in_=PS[p][:, 0:256]),
                    reads=[pm], writes=[dstb, f"{dstb}{i}_{mc}"],
                    tag=f"{kind}ev{i}_{mc}")
            else:
                add("V", lambda e, m=mc, p=pm, d=dstb: e.tensor_copy(
                        out=S[d][:, m, :], in_=PS[p][:, 0:256]),
                    reads=[pm], writes=[dstb, f"{dstb}{i}_{mc}"],
                    tag=f"{kind}ev{i}_{mc}")
    for hf in range(2):
        add("QS", lambda e, h=hf: e.dma_start(
                out=S["Wbuf"][:, h * 8192:(h + 1) * 8192],
                in_=D["scr_z"][:, h * 8192:(h + 1) * 8192]),
            writes=["WbufA" if hf == 0 else "WbufB"], tag=f"ldwz{i}_{hf}")
    for tc in range(2):
        for ng in range(4):
            pm = "pmm0" if (ng % 2 == 0) else "pmm1"
            half = 0 if ng < 2 else 1
            for kc in range(8):
                add("T", lambda e, t=tc, n=ng, k=kc, p=pm, h=half: e.matmul(
                        out=PS[p][:, 0:512],
                        lhsT=S["xn_fm"][:, par, k, t * 128:(t + 1) * 128],
                        rhs=S["Wbuf"][:, h * 8192 + k * 1024 + (n % 2) * 512:
                                       h * 8192 + k * 1024 + (n % 2) * 512 + 512],
                        start=(k == 0), stop=(k == 7)),
                    reads=["WbufA" if half == 0 else "WbufB", fmP],
                    writes=[pm], tag=f"zm{i}_{tc}_{ng}_{kc}")
            add("S", lambda e, t=tc, n=ng, p=pm: e.activation(
                    out=S["zs"][:, t, n * 512:(n + 1) * 512], in_=PS[p][:, 0:512],
                    func=AF.Silu),
                reads=[pm], writes=["zs"], tag=f"zev{i}_{tc}_{ng}")
    for hf in range(2):
        add("QS", lambda e, h=hf: e.dma_start(
                out=S["Wbuf"][:, h * 8192:(h + 1) * 8192],
                in_=D["scr_xr"][:, h * 8192:(h + 1) * 8192]),
            writes=["WbufA" if hf == 0 else "WbufB"], tag=f"ldwxr{i}_{hf}")
    for tc in range(2):
        for ng in range(4):
            pm = "pmm0" if (ng % 2 == 0) else "pmm1"
            half = 0 if ng < 2 else 1
            for kc in range(8):
                add("T", lambda e, t=tc, n=ng, k=kc, p=pm, h=half: e.matmul(
                        out=PS[p][:, 0:512],
                        lhsT=S["xn_fm"][:, par, k, t * 128:(t + 1) * 128],
                        rhs=S["Wbuf"][:, h * 8192 + k * 1024 + (n % 2) * 512:
                                       h * 8192 + k * 1024 + (n % 2) * 512 + 512],
                        start=(k == 0), stop=(k == 7)),
                    reads=["WbufA" if half == 0 else "WbufB", fmP],
                    writes=[pm], tag=f"xrm{i}_{tc}_{ng}_{kc}")
            add("S", lambda e, t=tc, n=ng, p=pm: e.copy(
                    out=S["xr"][:, t, n * 512:(n + 1) * 512], in_=PS[p][:, 0:512]),
                reads=[pm], writes=["xr", f"xr{i}_{tc}"], tag=f"xrev{i}_{tc}_{ng}")
    X.tap(f"zs{i}", lambda: S["zs"][:].rearrange("p c n -> p (c n)"), ["zs"])
    X.tap(f"zsm{i}", lambda: S["zs"][:].rearrange("p c n -> p (c n)"), ["zs"])
    X.tap(f"Braw{i}", lambda: S["bufA"][:].rearrange("p c n -> p (c n)"),
          ["bufA"] + [f"bufA{i}_{m}" for m in range(16)])
    X.tap(f"xr{i}", lambda: S["xr"][:].rearrange("p c n -> p (c n)"),
          ["xr", f"xr{i}_0", f"xr{i}_1"])


def _angles(X, i, par):
    S, PS, add = X.S, X.PS, X.P.add
    for hf in range(2):
        for pc in range(4):
            g = 4 * hf + pc
            add("T", lambda e, gg=g, p=pc: e.matmul(
                    out=PS["pe"][:, p * 256:(p + 1) * 256],
                    lhsT=S["selth_sb"][:, gg * 128:(gg + 1) * 128],
                    rhs=S["Lc"][:, 1:257], start=True, stop=True),
                reads=["Lc", "selth_sb"], writes=["pe"], tag=f"thmm{i}_{g}")
        add("V", lambda e: e.tensor_scalar(out=S["e1b"][:], in0=PS["pe"][:],
                                           scalar1=INV_2PI, scalar2=MAGIC,
                                           op0=ALU.mult, op1=ALU.add),
            reads=["pe"], writes=["e1b"], tag=f"thu{i}_{hf}")
        add("V", lambda e: e.tensor_scalar(out=S["scr4k"][:], in0=S["e1b"][:],
                                           scalar1=MAGIC, scalar2=None,
                                           op0=ALU.subtract),
            reads=["e1b"], writes=["scr4k"], tag=f"thk{i}_{hf}")
        add("V", lambda e: e.scalar_tensor_tensor(
                out=S["e1b"][:], in0=S["scr4k"][:], scalar=-TWO_PI,
                in1=PS["pe"][:], op0=ALU.mult, op1=ALU.add),
            reads=["scr4k", "pe", "e1b"], writes=["e1b"], tag=f"thr{i}_{hf}")
        add("S", lambda e, h=hf: e.activation(
                out=S["sth"][:, 4 * h:4 * h + 4, :].rearrange("p c n -> p (c n)"),
                in_=S["e1b"][:], func=AF.Sin),
            reads=["e1b"], writes=["sth"], tag=f"sin{i}_{hf}")
        add("S", lambda e: e.activation(out=S["scr4k"][:], in_=S["e1b"][:],
                                        func=AF.Abs),
            reads=["e1b", "scr4k"], writes=["scr4k"], tag=f"tha{i}_{hf}")
        add("S", lambda e, h=hf: e.activation(
                out=S["cth"][:, 4 * h:4 * h + 4, :].rearrange("p c n -> p (c n)"),
                in_=S["scr4k"][:], func=AF.Sin, scale=-1.0, bias=S["scals"][:, 7:8]),
            reads=["scr4k", "pih"], writes=["cth"], tag=f"cos{i}_{hf}")
    X.tap(f"cth{i}", lambda: S["cth"][:].rearrange("p c n -> p (c n)"), ["cth"])
    X.tap(f"sth{i}", lambda: S["sth"][:].rearrange("p c n -> p (c n)"), ["sth"])


def _rope(X, i, par):
    S, add = X.S, X.P.add

    def rot(srcb, dstb, src_reads):
        src, dst = S[srcb], S[dstb]
        x0 = src[:, 0:8, :].rearrange("p c n -> p (c n)")
        x1 = src[:, 8:16, :].rearrange("p c n -> p (c n)")
        o0 = dst[:, 0:8, :].rearrange("p c n -> p (c n)")
        o1 = dst[:, 8:16, :].rearrange("p c n -> p (c n)")
        ct = S["cth"][:].rearrange("p c n -> p (c n)")
        st = S["sth"][:].rearrange("p c n -> p (c n)")
        t1, t2 = S["Btm"][:], S["X2"][:]
        sr = src_reads + ["cth", "sth"]
        add("V", lambda e: e.tensor_tensor(out=t1, in0=x0, in1=ct, op=ALU.mult),
            reads=sr + ["Btm"], writes=["Btm"], tag=f"rp1{dstb}{i}")
        add("G", lambda e: e.tensor_tensor(out=t2, in0=x1, in1=st, op=ALU.mult),
            reads=sr + ["X2"], writes=["X2"], tag=f"rp2{dstb}{i}")
        add("V", lambda e: e.tensor_tensor(out=o0, in0=t1, in1=t2, op=ALU.subtract),
            reads=["Btm", "X2", dstb], writes=[dstb], tag=f"rp3{dstb}{i}")
        add("V", lambda e: e.tensor_tensor(out=t1, in0=x0, in1=st, op=ALU.mult),
            reads=sr + ["Btm"], writes=["Btm"], tag=f"rp4{dstb}{i}")
        add("G", lambda e: e.tensor_tensor(out=t2, in0=x1, in1=ct, op=ALU.mult),
            reads=sr + ["X2"], writes=["X2"], tag=f"rp5{dstb}{i}")
        add("V", lambda e: e.tensor_tensor(out=o1, in0=t1, in1=t2, op=ALU.add),
            reads=["Btm", "X2", dstb], writes=[dstb], tag=f"rp6{dstb}{i}")

    rot("bufA", "bufC", ["bufA"] + [f"bufA{i}_{m}" for m in range(16)])
    rot("bufB", "bufA", ["bufB"] + [f"bufB{i}_{m}" for m in range(16)])
    X.tap(f"Brot{i}", lambda: S["bufC"][:].rearrange("p c n -> p (c n)"), ["bufC"])
    X.tap(f"Crot{i}", lambda: S["bufA"][:].rearrange("p c n -> p (c n)"), ["bufA"])


def _chunk(X, i, c, par, first_of_seq):
    import os
    sub = int(os.environ.get("KCHSUB", "9"))
    S, PS, add = X.S, X.PS, X.P.add
    cs = slice(c * 128, (c + 1) * 128)
    b_active = i // TILES_PER_SEQ
    boff = b_active * 1024
    dtP = f"dtP{par}"

    add("V", lambda e: e.tensor_scalar(
            out=S["Pq"][:], in0=S["Lc"][:, 1 + c * 128:1 + c * 128 + 128],
            scalar1=S["Lc"][:, c * 128 + 128:c * 128 + 129],
            scalar2=S["sc32"][:, 2:3], op0=ALU.subtract, op1=ALU.mult),
        reads=["Lc", "A", "Pq"], writes=["Pq"], tag=f"P{i}_{c}")
    add("V", lambda e: e.tensor_copy(out=S["Psplit"][0:32, :], in_=S["Pq"][:]),
        reads=["Pq", "Psplit"], writes=["Psplit"], tag=f"Phi{i}_{c}")
    add("V", lambda e: e.tensor_tensor(out=S["Psplit"][32:64, :], in0=S["Pq"][:],
                                       in1=S["Psplit"][0:32, :], op=ALU.subtract),
        reads=["Pq", "Psplit"], writes=["Psplit"], tag=f"Plo{i}_{c}")
    add("T", lambda e: e.transpose(out=PS["pmm0"][:, 384:416], in_=S["Pq"][:],
                                   identity=S["id32"][:]),
        reads=["Pq", "id32", "pmm0"], writes=["pmm0"], tag=f"tM{i}_{c}")
    add("T", lambda e: e.transpose(out=PS["pmm0"][:, 416:448],
                                   in_=S["ldt"][:, c * 128:c * 128 + 128],
                                   identity=S["id32"][:]),
        reads=["ldt", "id32"], writes=["pmm0"], tag=f"tl1{i}_{c}")
    add("T", lambda e: e.transpose(out=PS["pmm0"][:, 448:480],
                                   in_=S["dts"][:, par, c * 128:c * 128 + 128],
                                   identity=S["id32"][:]),
        reads=[dtP, "id32"], writes=["pmm0"], tag=f"tw1{i}_{c}")
    add("T", lambda e: e.transpose(out=PS["pmm0"][:, 480:512],
                                   in_=S["dts"][:, par, c * 128 + 1:c * 128 + 129],
                                   identity=S["id32"][:]),
        reads=[dtP, "id32"], writes=["pmm0"], tag=f"tw2{i}_{c}")
    add("V", lambda e: e.tensor_copy(out=S["Mt"][:], in_=PS["pmm0"][:, 384:416]),
        reads=["pmm0", "Mt"], writes=["Mt"], tag=f"Mt{i}_{c}")
    add("V", lambda e: e.tensor_copy(out=S["l1t"][:], in_=PS["pmm0"][:, 416:448]),
        reads=["pmm0", "l1t"], writes=["l1t"], tag=f"l1t{i}_{c}")
    add("S", lambda e: e.activation(out=S["w1t"][:], in_=PS["pmm0"][:, 448:480],
                                    func=AF.Copy, scale=0.5),
        reads=["pmm0", "w1t"], writes=["w1t"], tag=f"w1t{i}_{c}")
    add("S", lambda e: e.activation(out=S["w2t"][:], in_=PS["pmm0"][:, 480:512],
                                    func=AF.Copy, scale=0.5),
        reads=["pmm0", "w2t"], writes=["w2t"], tag=f"w2t{i}_{c}")
    add("V", lambda e: e.tensor_tensor(out=S["Mcol1"][:], in0=S["Mt"][:],
                                       in1=S["l1t"][:], op=ALU.subtract),
        reads=["Mt", "l1t", "Mcol1"], writes=["Mcol1"], tag=f"Mc1{i}_{c}")
    add("V", lambda e: e.reciprocal(out=S["exS"][:], in_=S["w1t"][:]),
        reads=["w1t", "exS"], writes=["exS"], tag=f"rw1{i}_{c}")
    add("V", lambda e: e.tensor_tensor(out=S["ratio"][:], in0=S["w2t"][:],
                                       in1=S["exS"][:], op=ALU.mult),
        reads=["w2t", "exS", "ratio"], writes=["ratio"], tag=f"ratio{i}_{c}")
    add("V", lambda e: e.tensor_tensor(out=S["l1t"][:], in0=S["w1t"][:],
                                       in1=S["w2t"][:], op=ALU.add),
        reads=["w1t", "w2t", "l1t"], writes=["l1t"], tag=f"wsum{i}_{c}")
    add("S", lambda e: e.activation(out=S["exS"][:], in_=S["Mt"][:], func=AF.Exp,
                                    scale=-1.0),
        reads=["Mt", "exS"], writes=["exS"], tag=f"exS{i}_{c}")
    add("V", lambda e: e.tensor_tensor(out=S["ccol"][:], in0=S["l1t"][:],
                                       in1=S["exS"][:], op=ALU.mult),
        reads=["l1t", "exS", "ccol"], writes=["ccol"], tag=f"ccol{i}_{c}")
    add("V", lambda e: e.tensor_tensor(
            out=S["X2"][:].rearrange("p (h q) -> p h q", h=32),
            in0=S["xr"][:, c, :].rearrange("p (h q) -> p h q", h=32),
            in1=S["ccol"][:].unsqueeze(2).broadcast_to([128, 32, 64]),
            op=ALU.mult),
        reads=["xr", f"xr{i}_{c}", "ccol", "X2"], writes=["X2"], tag=f"X2{i}_{c}")
    for rnd in range(2):
        for k in range(8):
            mc = rnd * 8 + k
            add("T", lambda e, m=mc, kk=k: e.transpose(
                    out=PS["pt"][:, kk * 128:(kk + 1) * 128],
                    in_=S["bufC"][:, m, cs], identity=S["id_f16"][:]),
                reads=["bufC", "id_f16", "pt"], writes=["pt"],
                tag=f"btm{i}_{c}_{mc}")
        add("S", lambda e, r=rnd: e.copy(out=S["Btm"][:, r * 1024:(r + 1) * 1024],
                                         in_=PS["pt"][:]),
            reads=["pt", "Btm"], writes=["Btm"], tag=f"btmev{i}_{c}_{rnd}")

    if sub < 2:
        return
    if not first_of_seq:
        add("V", lambda e: e.tensor_scalar(
                out=S["dcol"][:, 0:1], in0=S["Lc"][:, c * 128 + 128:c * 128 + 129],
                scalar1=S["Lc"][:, c * 128:c * 128 + 1], scalar2=S["sc32"][:, 2:3],
                op0=ALU.subtract, op1=ALU.mult),
            reads=["Lc", "A", "dcol"], writes=["dcol"], tag=f"lndch{i}_{c}")
        add("V", lambda e: e.tensor_scalar(out=S["dtmp"][:, 0:128], in0=S["Pq"][:],
                                           scalar1=S["dcol"][:, 0:1], scalar2=None,
                                           op0=ALU.add),
            reads=["Pq", "dcol", "dtmp"], writes=["dtmp"], tag=f"deca{i}_{c}")
        for pc in range(8):
            add("T", lambda e, p=pc: e.matmul(
                    out=PS["pe"][:, p * 128:(p + 1) * 128],
                    lhsT=S["seldec_sb"][:, p * 128:(p + 1) * 128],
                    rhs=S["dtmp"][:, 0:128], start=True, stop=True),
                reads=["dtmp", "seldec_sb", "pe"], writes=["pe"],
                tag=f"decmm{i}_{c}_{pc}")
        add("S", lambda e: e.activation(
                out=S["decr"][:].rearrange("p c n -> p (c n)"), in_=PS["pe"][:],
                func=AF.Exp),
            reads=["pe", "decr"], writes=["decr"], tag=f"decev{i}_{c}")
        add("V", lambda e: e.tensor_tensor(
                out=S["Cdec"][:],
                in0=S["bufA"][:, :, cs].rearrange("p (b c) n -> p b c n", b=2),
                in1=S["decr"][:].unsqueeze(1).broadcast_to([128, 2, 8, 128]),
                op=ALU.mult),
            reads=["bufA", "decr", "Cdec"], writes=["Cdec"], tag=f"cdec{i}_{c}")
        add("S", lambda e: e.activation(out=S["dcol"][:, 1:2], in_=S["dcol"][:, 0:1],
                                        func=AF.Exp),
            reads=["dcol"], writes=["dcol"], tag=f"dch{i}_{c}")
        add("V", lambda e: e.tensor_scalar(out=S["rhsd"][:], in0=S["i2dch_sb"][:],
                                           scalar1=S["dcol"][:, 1:2], scalar2=None,
                                           op0=ALU.mult),
            reads=["dcol", "i2dch_sb", "rhsd"], writes=["rhsd"], tag=f"rhsd{i}_{c}")
        add("T", lambda e: e.matmul(out=PS["pmm1"][:, 504:512],
                                    lhsT=S["seldch_sb"][:], rhs=S["rhsd"][:],
                                    start=True, stop=True),
            reads=["rhsd", "seldch_sb", "pmm1"], writes=["pmm1"],
            tag=f"dchmm{i}_{c}")

    if sub < 3:
        return
    for q in range(4):
        _quarter(X, i, c, q, par, first_of_seq, boff, cs, sub)
    if sub < 6:
        return

    hsl = S["H"][:, boff:boff + 1024]
    hbsl = S["Hb"][:, boff:boff + 1024]
    if not first_of_seq:
        add("V", lambda e: e.tensor_tensor(
                out=S["scr4k"][:].rearrange("p (a b) -> p a b", a=8),
                in0=hsl.rearrange("p (a b) -> p a b", a=8),
                in1=PS["pmm1"][:, 504:512].unsqueeze(2).broadcast_to([128, 8, 128]),
                op=ALU.mult),
            reads=["H", "pmm1", "scr4k"], writes=["scr4k"], tag=f"hdec{i}_{c}")
    for qq in range(4):
        for mc in range(8):
            h = qq * 8 + mc
            for piece in range(2):
                cb = mc * 128 + (piece * 1024)
                fo = mc * 128 + piece * 64
                add("T", lambda e, c0=cb, f=fo, hh_=h: e.matmul(
                        out=PS["psc" if f < 512 else "psc2"]
                        [:, f % 512:f % 512 + 64],
                        lhsT=S["Btm"][:, c0:c0 + 128],
                        rhs=S["X2"][:, hh_ * 64:hh_ * 64 + 64],
                        start=True, stop=True),
                    reads=["Btm", "X2", "psc", "psc2"], writes=["psc", "psc2"],
                    tag=f"st{i}_{c}_{h}_{piece}")
        if first_of_seq:
            add("V", lambda e, b=qq: e.tensor_copy(
                    out=hsl[32 * b:32 * b + 32, 0:512],
                    in_=PS["psc"][32 * b:32 * b + 32, :]),
                reads=["psc", "H"], writes=["H"], tag=f"hb0{i}_{c}_{qq}")
            add("S", lambda e, b=qq: e.copy(
                    out=hsl[32 * b:32 * b + 32, 512:1024],
                    in_=PS["psc2"][32 * b:32 * b + 32, :]),
                reads=["psc2", "H"], writes=["H"], tag=f"hb1{i}_{c}_{qq}")
        else:
            add("V", lambda e, b=qq: e.tensor_tensor(
                    out=hsl[32 * b:32 * b + 32, 0:512],
                    in0=S["scr4k"][32 * b:32 * b + 32, 0:512],
                    in1=PS["psc"][32 * b:32 * b + 32, :], op=ALU.add),
                reads=["psc", "H", "scr4k"], writes=["H"], tag=f"hb0{i}_{c}_{qq}")
            add("V", lambda e, b=qq: e.tensor_tensor(
                    out=hsl[32 * b:32 * b + 32, 512:1024],
                    in0=S["scr4k"][32 * b:32 * b + 32, 512:1024],
                    in1=PS["psc2"][32 * b:32 * b + 32, :], op=ALU.add),
                reads=["psc2", "H", "scr4k"], writes=["H"], tag=f"hb1{i}_{c}_{qq}")
    add("G", lambda e: e.tensor_copy(out=hbsl, in_=hsl),
        reads=["H", "Hb"], writes=["Hb"], tag=f"hb{i}_{c}")
    X.tap(f"H{i}_{c}", lambda: S["H"][:, :], ["H"])


def _quarter(X, i, c, q, par, first_of_seq, boff, cs, sub=9):
    S, PS, add = X.S, X.PS, X.P.add
    qb = 32 * q
    add("QA", lambda e: e.dma_start(out=S["Pflat"][qb:qb + 1, :],
                                    in_=S["Psplit"][q * 8:q * 8 + 8, :]),
        reads=["Psplit", "Pflat"], writes=["Pflat"], tag=f"pf0{i}_{c}_{q}")
    add("QA", lambda e: e.dma_start(out=S["Pflat"][qb + 1:qb + 2, :],
                                    in_=S["Psplit"][32 + q * 8:32 + q * 8 + 8, :]),
        reads=["Psplit", "Pflat"], writes=["Pflat"], tag=f"pf1{i}_{c}_{q}")
    for hf in range(2):
        sl = slice(hf * 512, (hf + 1) * 512)
        add("T", lambda e, s=sl: e.matmul(out=PS["pe"][:, s], lhsT=S["id_bf"][:],
                                          rhs=S["tri4"][:, 0:512], start=True,
                                          stop=False),
            reads=["tri4", "id_bf", "pe"], writes=["pe"], tag=f"et{i}_{c}_{q}_{hf}")
        add("T", lambda e, s=sl: e.matmul(out=PS["pe"][:, s],
                                          lhsT=S["ones2"][qb:qb + 2, :],
                                          rhs=S["Pflat"][qb:qb + 2, s], start=False,
                                          stop=True, tile_position=(qb, 0)),
            reads=["Pflat", "ones2", "pe"], writes=["pe"], tag=f"em{i}_{c}_{q}_{hf}")
    add("V", lambda e: e.tensor_tensor(
            out=S["e1b"][:].rearrange("p (h t) -> p h t", h=8),
            in0=PS["pe"][:].rearrange("p (h t) -> p h t", h=8),
            in1=S["Mcol1"][:, q * 8:q * 8 + 8].unsqueeze(2)
            .broadcast_to([128, 8, 128]),
            op=ALU.subtract),
        reads=["pe", "Mcol1", "e1b"], writes=["e1b"], tag=f"e1{i}_{c}_{q}")
    add("S", lambda e: e.activation(out=S["M1b"][:], in_=S["e1b"][:], func=AF.Exp),
        reads=["e1b", "M1b"], writes=["M1b"], tag=f"M1{i}_{c}_{q}")
    add("G", lambda e: e.tensor_tensor(
            out=S["Sm2"][:].rearrange("p (h t) -> p h t", h=8),
            in0=S["M1b"][:].rearrange("p (h t) -> p h t", h=8),
            in1=S["ratio"][:, q * 8:q * 8 + 8].unsqueeze(2)
            .broadcast_to([128, 8, 128]),
            op=ALU.mult),
        reads=["M1b", "ratio", "Sm2"], writes=["Sm2"], tag=f"M2a{i}_{c}_{q}")
    add("G", lambda e: e.tensor_tensor(
            out=S["M2b"][:].rearrange("p (h t) -> p h t", h=8),
            in0=S["Sm2"][:].rearrange("p (h t) -> p h t", h=8),
            in1=S["antiI"][:].unsqueeze(1).broadcast_to([128, 8, 128]),
            op=ALU.mult),
        reads=["Sm2", "antiI", "M2b"], writes=["M2b"], tag=f"M2b{i}_{c}_{q}")
    if q == 0 and c == 0:
        X.tap(f"M1_{i}", lambda: S["M1b"][:], ["M1b"])
    if sub < 4:
        return
    import os as _os
    hh4max = int(_os.environ.get("KHH4", "4"))
    for g in range(2):
        pg = "psc" if g == 0 else "psc2"
        for hh4 in range(hh4max):
            hh = g * 4 + hh4
            h = q * 8 + hh
            m = q
            mcx0 = h % 8
            fo = hh4 * 128
            add("T", lambda e, mm=m, mc=mcx0, f=fo, p=pg, s=cs: e.matmul(
                    out=PS[p][:, f:f + 128],
                    lhsT=S["bufC"][32 * mm:32 * mm + 32, mc, s],
                    rhs=S["bufA"][32 * mm:32 * mm + 32, mc, s],
                    start=True, stop=False, tile_position=(32 * mm, 0)),
                reads=["bufA", "bufC", pg], writes=[pg], tag=f"sc0{i}_{c}_{h}")
            add("T", lambda e, mm=m, mc=8 + mcx0, f=fo, p=pg, s=cs: e.matmul(
                    out=PS[p][:, f:f + 128],
                    lhsT=S["bufC"][32 * mm:32 * mm + 32, mc, s],
                    rhs=S["bufA"][32 * mm:32 * mm + 32, mc, s],
                    start=False, stop=True, tile_position=(32 * mm, 0)),
                reads=["bufA", "bufC"], writes=[pg], tag=f"sc1{i}_{c}_{h}")
        add("V", lambda e, p=pg, gg=g: e.tensor_tensor(
                out=S["Sm1"][:, gg * 512:(gg + 1) * 512], in0=PS[p][:],
                in1=S["M1b"][:, gg * 512:(gg + 1) * 512], op=ALU.mult),
            reads=[pg, "M1b", "Sm1"], writes=["Sm1"], tag=f"sm1{i}_{c}_{q}_{g}")
        add("V", lambda e, p=pg, gg=g: e.tensor_tensor(
                out=S["Sm2"][:, gg * 512:(gg + 1) * 512], in0=PS[p][:],
                in1=S["M2b"][:, gg * 512:(gg + 1) * 512], op=ALU.mult),
            reads=[pg, "M2b", "Sm2"], writes=["Sm2"], tag=f"sm2{i}_{c}_{q}_{g}")
    if q == 0 and c == 0:
        X.tap(f"Sm1_{i}", lambda: S["Sm1"][:], ["Sm1"])
        X.tap(f"Sm2_{i}", lambda: S["Sm2"][:], ["Sm2"])
    if sub < 5:
        return
    import os as _os3
    for hh in range(8):
        h = q * 8 + hh
        m = q
        mcx0 = h % 8
        fo = hh * 64
        smo = hh * 128
        add("T", lambda e, f=fo, so=smo, hh_=h: e.matmul(
                out=PS["py"][:, f:f + 64],
                lhsT=S["Sm1"][:, so:so + 128],
                rhs=S["xr"][:, c, hh_ * 64:hh_ * 64 + 64],
                start=True, stop=False),
            reads=["Sm1", "xr", f"xr{i}_{c}", "py"], writes=["py"],
            tag=f"yi1{i}_{c}_{h}")
        add("T", lambda e, f=fo, so=smo, hh_=h: e.matmul(
                out=PS["py"][:, f:f + 64],
                lhsT=S["Sm2"][:, so:so + 128],
                rhs=S["xr"][:, c, hh_ * 64:hh_ * 64 + 64],
                start=False, stop=first_of_seq),
            reads=["Sm2"], writes=["py"], tag=f"yi2{i}_{c}_{h}")
        if not first_of_seq:
            add("T", lambda e, f=fo, mm=m, mc=mcx0: e.matmul(
                    out=PS["py"][:, f:f + 64],
                    lhsT=S["Cdec"][32 * mm:32 * mm + 32, 0, mc, :],
                    rhs=S["Hb"][32 * mm:32 * mm + 32,
                                boff + mc * 128:boff + mc * 128 + 64],
                    start=False, stop=False, tile_position=(32 * mm, 0)),
                reads=["Hb", "Cdec"], writes=["py"], tag=f"yn0{i}_{c}_{h}")
            add("T", lambda e, f=fo, mm=m, mc=mcx0: e.matmul(
                    out=PS["py"][:, f:f + 64],
                    lhsT=S["Cdec"][32 * mm:32 * mm + 32, 1, mc, :],
                    rhs=S["Hb"][32 * mm:32 * mm + 32,
                                boff + mc * 128 + 64:boff + mc * 128 + 128],
                    start=False, stop=True, tile_position=(32 * mm, 0)),
                reads=["Hb", "Cdec"], writes=["py"], tag=f"yn1{i}_{c}_{h}")
    add("V", lambda e: e.tensor_tensor(
            out=S["yg"][:, c, q * 512:(q + 1) * 512], in0=PS["py"][:],
            in1=S["zs"][:, c, q * 512:(q + 1) * 512], op=ALU.mult),
        reads=["py", "zs", "yg"], writes=["yg"], tag=f"ygev{i}_{c}_{q}")


def _epilogue(X, i, par):
    S, PS, D, add = X.S, X.PS, X.D, X.P.add
    X.tap(f"yg{i}", lambda: S["yg"][:].rearrange("p c n -> p (c n)"), ["yg"])
    for c in range(NCH):
        add("S", lambda e, cc=c: e.activation(out=S["e1b"][:],
                                              in_=S["yg"][:, cc, 0:1024],
                                              func=AF.Square),
            reads=["yg", "e1b"], writes=["e1b"], tag=f"rsq0{i}_{c}")
        add("V", lambda e: e.tensor_reduce(out=S["lnmv"][:, 0:1], in_=S["e1b"][:],
                                           axis=mybir.AxisListType.X, op=ALU.add),
            reads=["e1b", "lnmv"], writes=["lnmv"], tag=f"rr0{i}_{c}", lagged=True)
        add("S", lambda e, cc=c: e.activation(out=S["e1b"][:],
                                              in_=S["yg"][:, cc, 1024:2048],
                                              func=AF.Square),
            reads=["yg", "e1b"], writes=["e1b"], tag=f"rsq1{i}_{c}")
        add("V", lambda e: e.tensor_reduce(out=S["lnmv"][:, 1:2], in_=S["e1b"][:],
                                           axis=mybir.AxisListType.X, op=ALU.add),
            reads=["e1b", "lnmv"], writes=["lnmv"], tag=f"rr1{i}_{c}", lagged=True)
        add("V", lambda e: e.tensor_tensor(out=S["lnsd"][:], in0=S["lnmv"][:, 0:1],
                                           in1=S["lnmv"][:, 1:2], op=ALU.add),
            reads=["lnmv", "lnsd"], writes=["lnsd"], tag=f"rms{i}_{c}")
        add("S", lambda e: e.activation(out=S["lnrs"][:], in_=S["lnsd"][:],
                                        func=AF.Ln, scale=1.0 / 2048.0,
                                        bias=S["scals"][:, 6:7]),
            reads=["lnsd", "epsrms", "lnrs"], writes=["lnrs"], tag=f"rln{i}_{c}")
        add("S", lambda e: e.activation(out=S["lnmb"][:], in_=S["lnrs"][:],
                                        func=AF.Exp, scale=-0.5),
            reads=["lnrs", "lnmb"], writes=["lnmb"], tag=f"rex{i}_{c}")
        add("V", lambda e, cc=c: e.tensor_scalar(out=S["Btm"][:],
                                                 in0=S["yg"][:, cc, :],
                                                 scalar1=S["lnmb"][:, 0:1],
                                                 scalar2=None, op0=ALU.mult),
            reads=["yg", "lnmb", "Btm"], writes=["Btm"], tag=f"ynm{i}_{c}")
        for rnd in range(2):
            for k in range(8):
                fb = rnd * 8 + k
                add("T", lambda e, f=fb, kk=k: e.transpose(
                        out=PS["pt"][:, kk * 128:(kk + 1) * 128],
                        in_=S["Btm"][:, f * 128:(f + 1) * 128],
                        identity=S["id_f16"][:]),
                    reads=["Btm", "id_f16", "pt"], writes=["pt"],
                    tag=f"ynt{i}_{c}_{fb}")
            add("S", lambda e, r=rnd, cc=c: e.copy(
                    out=S["bufB"][:, r * 8:(r + 1) * 8, cc * 128:(cc + 1) * 128],
                    in_=PS["pt"][:].rearrange("p (a b) -> p a b", a=8)),
                reads=["pt", "bufB"], writes=["bufB", "yn"], tag=f"ynev{i}_{c}_{rnd}")
    X.tap(f"yn{i}", lambda: S["bufB"][:].rearrange("p c n -> p (c n)"),
          ["bufB", "yn"])
    for hf in range(2):
        add("QS", lambda e, h=hf: e.dma_start(
                out=S["Wbuf"][:, h * 8192:(h + 1) * 8192],
                in_=D["scr_out"][:, h * 8192:(h + 1) * 8192]),
            writes=["WbufA" if hf == 0 else "WbufB"], tag=f"ldwo{i}_{hf}")
    for tc in range(2):
        for ng in range(2):
            pm = "pmm0" if ng == 0 else "pmm1"
            for kc in range(16):
                add("T", lambda e, t=tc, n=ng, k=kc, p=pm: e.matmul(
                        out=PS[p][:, 0:512],
                        lhsT=S["bufB"][:, k, t * 128:(t + 1) * 128],
                        rhs=S["Wbuf"][:, k * 1024 + n * 512:k * 1024 + n * 512 + 512],
                        start=(k == 0), stop=(k == 15)),
                    reads=["WbufA" if kc < 8 else "WbufB", "bufB", "yn", pm],
                    writes=[pm], tag=f"op{i}_{tc}_{ng}_{kc}")
            add("V", lambda e, t=tc, n=ng, p=pm: e.tensor_tensor(
                    out=S["sgb"][:, n * 512:(n + 1) * 512], in0=PS[p][:],
                    in1=S["gbuf"][:, t, n * 512:(n + 1) * 512], op=ALU.mult),
                reads=[pm, "gbuf", "sgb"], writes=["sgb"], tag=f"sg{i}_{tc}_{ng}")
        add("V", lambda e, t=tc: e.tensor_tensor(
                out=S["outb"][:], in0=S["sgb"][:],
                in1=S["xt"][:, par, t, :], op=ALU.add),
            reads=["sgb", f"xtP{par}", "outb"], writes=["outb"], tag=f"ob{i}_{tc}")
        add("QS", lambda e, t=tc: e.dma_start(
                out=D["out"][i * 256 + t * 128:i * 256 + t * 128 + 128, :],
                in_=S["outb"][:]),
            reads=["outb"], writes=[f"outdma{i}" if tc == 1 else f"outdmaa{i}"],
            tag=f"st_out{i}_{tc}")
    X.tap(f"s{i}", lambda: S["sgb"][:], ["sgb"])



def kernel(**inputs):
    in_maps = setup_core_inputs(inputs)
    nc = build_nc()
    res = run_bass_kernel_spmd(nc, in_maps, core_ids=list(range(N_CORES)))
    out = np.concatenate(
        [np.asarray(r["out"]).reshape(2, 2048, 1024) for r in res.results], axis=0)
    return np.ascontiguousarray(out.reshape(16, 2048, 1024)).astype(np.float32)


# version 04
# speedup vs baseline: 1.0919x; 1.0919x over previous
import contextlib
import math

import numpy as np
import ml_dtypes

import concourse.bass as bass
import concourse.mybir as mybir
from concourse.bass_utils import run_bass_kernel_spmd

F32 = mybir.dt.float32
BF16 = mybir.dt.bfloat16
FP16 = mybir.dt.float16
FP8 = mybir.dt.float8e4
AF = mybir.ActivationFunctionType
ALU = mybir.AluOpType

N_CORES = 8
SEQ = 2048
T = 256
Q = 128
N_TILES = 16
TILES_PER_SEQ = SEQ // T
NCH = T // Q

MAGIC = 12582912.0
TWO_PI = float(2.0 * math.pi)
INV_2PI = float(1.0 / TWO_PI)
LN_EPS = 1e-5
RMS_EPS = 1e-6

STREAM_OF = {"T": "tensor", "V": "vector", "S": "scalar", "G": "gpsimd",
             "QS": "sync", "QG": "gpsimd", "QA": "scalar"}
DMA_ENGS = ("QS", "QG", "QA")


class Op:
    __slots__ = ("eng", "emit", "deps", "idx", "eidx", "needed", "inc_ord",
                 "tag", "lagged")

    def __init__(self, eng, emit, deps, idx, eidx, tag, lagged):
        self.eng, self.emit, self.deps = eng, emit, deps
        self.idx, self.eidx = idx, eidx
        self.needed, self.inc_ord, self.tag = False, 0, tag
        self.lagged = lagged


class Sched:
    def __init__(self):
        self.ops = []
        self.eng_ops = {e: [] for e in STREAM_OF}
        self.res = {}

    def add(self, eng, emit, reads=(), writes=(), tag="", lagged=False):
        deps = set()
        for t in reads:
            e = self.res.get(t)
            if e and e["w"] is not None:
                deps.add(e["w"])
        for t in writes:
            e = self.res.get(t)
            if e:
                if e["w"] is not None:
                    deps.add(e["w"])
                deps.update(e["r"])
        idx = len(self.ops)
        op = Op(eng, emit, deps, idx, len(self.eng_ops[eng]), tag, lagged)
        self.ops.append(op)
        self.eng_ops[eng].append(op)
        for t in reads:
            self.res.setdefault(t, {"w": None, "r": []})["r"].append(idx)
        for t in writes:
            self.res[t] = {"w": idx, "r": []}
        return idx

    def finalize(self):
        SAFE_DIST = 12
        for op in self.ops:
            best = {}
            for d in op.deps:
                dop = self.ops[d]
                same_stream = STREAM_OF[dop.eng] == STREAM_OF[op.eng]
                if same_stream and not dop.lagged:
                    if dop.eng in DMA_ENGS and op.eng in DMA_ENGS:
                        continue
                    if dop.eng == "T" and op.eng == "T":
                        continue
                    if (dop.eng == op.eng and dop.eng not in DMA_ENGS
                            and op.eidx - dop.eidx >= SAFE_DIST):
                        continue
                    if dop.eng not in DMA_ENGS and op.eng in DMA_ENGS:
                        pass
                e = dop.eng
                if e not in best or self.ops[best[e]].eidx < dop.eidx:
                    best[e] = d
            op.deps = set(best.values())
            for d in op.deps:
                self.ops[d].needed = True
        for e, ops in self.eng_ops.items():
            k = 0
            for op in ops:
                if e in DMA_ENGS or op.needed:
                    k += 1
                    op.inc_ord = k

    def emit_stream(self, stream, engobj, sems):
        hw = {}
        for op in self.ops:
            if STREAM_OF[op.eng] != stream:
                continue
            for d in sorted(op.deps, key=lambda i: self.ops[i].eng):
                dop = self.ops[d]
                step = 16 if dop.eng in DMA_ENGS else 1
                val = dop.inc_ord * step
                if hw.get(dop.eng, -1) >= val:
                    continue
                hw[dop.eng] = val
                engobj.wait_ge(sems[dop.eng], val)
            if op.eng in DMA_ENGS and op.inc_ord > 1:
                val = 16 * (op.inc_ord - 1)
                if hw.get(op.eng, -1) < val:
                    hw[op.eng] = val
                    engobj.wait_ge(sems[op.eng], val)
            inst = op.emit(engobj)
            if op.inc_ord:
                step = 16 if op.eng in DMA_ENGS else 1
                assert inst is not None, f"op {op.tag} must return its instruction"
                inst.then_inc(sems[op.eng], step)



def _perm_bc():
    p = np.empty(2048, dtype=np.int64)
    r = np.arange(1024)
    pc, m, i = r // 128, (r % 128) // 32, r % 32
    h = m * 8 + pc
    p[:1024] = h * 64 + 2 * i
    p[1024:] = h * 64 + 2 * i + 1
    return p


def _consts():
    c = {}
    ident = np.eye(128, dtype=np.float32)
    c["ident_bf"] = ident.astype(ml_dtypes.bfloat16)
    c["ident_f16"] = ident.astype(np.float16)
    c["ident_f32_32"] = np.eye(32, dtype=np.float32)
    tri = np.where(np.arange(128)[None, :] >= np.arange(128)[:, None],
                   0.0, -1e30).astype(np.float32)
    c["tri_rep4"] = np.tile(tri, (1, 4)).astype(ml_dtypes.bfloat16)
    c["anti_ident"] = (1.0 - np.eye(128, dtype=np.float32)).astype(np.float16)
    invfreq = (1.0 / (10000.0 ** (np.arange(32) / 32.0))).astype(np.float32)
    selth = np.zeros((32, 1024), dtype=np.float32)
    seldec = np.zeros((32, 1024), dtype=np.float32)
    rr = np.arange(1024)
    hh_of_r = ((rr % 128) // 32) * 8 + rr // 128
    selth[hh_of_r, rr] = invfreq[rr % 32]
    seldec[hh_of_r, rr] = 1.0
    c["selth"] = selth
    c["seldec"] = seldec
    seldch = np.zeros((32, 128), dtype=np.float32)
    for h in range(32):
        seldch[h, 32 * (h // 8):32 * (h // 8) + 32] = 1.0
    c["seldch"] = seldch
    i2d = np.zeros((32, 8), dtype=np.float32)
    i2d[np.arange(32), np.arange(32) % 8] = 1.0
    c["i2dch"] = i2d
    c["ones2_bf"] = np.ones((128, 128), dtype=ml_dtypes.bfloat16)
    c["ones1_f32"] = np.ones((1, 128), dtype=np.float32)
    c["onesdt"] = np.ones((32, 256), dtype=np.float32)
    return c


def setup_core_inputs(inputs):
    x = np.ascontiguousarray(
        np.asarray(inputs["L0_attn_hidden"]).reshape(16, 2048, 1024))
    W_in = np.asarray(inputs["L0_attn_W_in"])
    perm = _perm_bc()
    consts = _consts()
    shared = dict(
        Wz=np.ascontiguousarray(W_in[:, 0:2048]),
        Wxr=np.ascontiguousarray(W_in[:, 2048:4096]),
        WB=np.ascontiguousarray(W_in[:, 4096:6144][:, perm]),
        WC=np.ascontiguousarray(W_in[:, 6144:8192][:, perm]),
        Wdt=np.ascontiguousarray(
            np.concatenate([W_in[:, 8192:8224],
                            np.zeros((1024, 96), dtype=W_in.dtype)], axis=1)
            .reshape(8, 128, 128).transpose(1, 0, 2)),
        Wout=np.ascontiguousarray(np.asarray(inputs["L0_attn_W_out"])),
        ffnk=np.ascontiguousarray(
            np.asarray(inputs["L0_ffn_kernel"]).reshape(8, 128, 1024)
            .transpose(1, 0, 2)),
        ffnb_rep=np.ascontiguousarray(np.broadcast_to(
            np.asarray(inputs["L0_ffn_bias"])[None, :], (128, 1024))),
        s_in_col=np.ascontiguousarray(np.broadcast_to(
            np.asarray(inputs["L0_ffn_input_scale"])[:, None], (128, 1))),
        s_k_col=np.ascontiguousarray(np.broadcast_to(
            np.asarray(inputs["L0_ffn_kernel_scale"])[:, None], (128, 1))),
        dtb_col=np.ascontiguousarray(
            np.asarray(inputs["L0_attn_dt_bias"])[:, None]),
        alog_col=np.ascontiguousarray(
            np.asarray(inputs["L0_attn_A_log"])[:, None]),
        nw_fm=np.ascontiguousarray(
            np.asarray(inputs["L0_attn_norm_weight"]).reshape(16, 128).T),
        **consts,
    )
    return [dict(shared, x=np.ascontiguousarray(
        x[2 * ci:2 * ci + 2].reshape(4096, 1024))) for ci in range(N_CORES)]



class Ctx:
    pass


def build_nc(taps=(), n_tiles=N_TILES):
    nc = bass.Bass("TRN2", target_bir_lowering=False)
    P = Sched()
    X = Ctx()
    X.nc, X.P, X.n_tiles = nc, P, n_tiles
    X.taps = {}

    D = {}
    def din(name, shape, dt):
        D[name] = nc.dram_tensor(name, list(shape), dt, kind="ExternalInput").ap()
    def dout(name, shape, dt):
        D[name] = nc.dram_tensor(name, list(shape), dt, kind="ExternalOutput").ap()
    def dscr(name, shape, dt):
        D[name] = nc.dram_tensor(name, list(shape), dt, kind="Internal").ap()

    din("x", (4096, 1024), F32)
    for w in ("Wz", "Wxr", "WB", "WC"):
        din(w, (1024, 2048), F32)
    din("Wdt", (128, 8, 128), F32)
    din("Wout", (2048, 1024), F32)
    din("ffnk", (128, 8, 1024), BF16)
    din("ffnb_rep", (128, 1024), BF16)
    din("s_in_col", (128, 1), F32)
    din("s_k_col", (128, 1), F32)
    din("dtb_col", (32, 1), F32)
    din("alog_col", (32, 1), F32)
    din("nw_fm", (128, 16), F32)
    din("ident_bf", (128, 128), BF16)
    din("ident_f16", (128, 128), FP16)
    din("ident_f32_32", (32, 32), F32)
    din("tri_rep4", (128, 512), BF16)
    din("anti_ident", (128, 128), FP16)
    din("selth", (32, 1024), F32)
    din("seldec", (32, 1024), F32)
    din("seldch", (32, 128), F32)
    din("i2dch", (32, 8), F32)
    din("ones2_bf", (128, 128), BF16)
    din("ones1_f32", (1, 128), F32)
    din("onesdt", (32, 256), F32)
    dout("out", (4096, 1024), F32)
    for name, shape, dt in taps:
        dout("tap_" + name, list(shape), dt)
        X.taps[name] = D["tap_" + name]
    for s in ("scr_z", "scr_B", "scr_C", "scr_xr", "scr_out"):
        dscr(s, (128, 16384), BF16)

    es = contextlib.ExitStack()
    S = {}
    def asb(name, shape, dt):
        S[name] = es.enter_context(nc.sbuf_tensor(name, list(shape), dt))
    PS = {}
    def aps(name, shape, dt):
        PS[name] = es.enter_context(nc.psum_tensor(name, list(shape), dt))

    asb("Wdt_sb", (128, 8, 128), BF16)
    asb("ffn_q", (128, 8, 1024), FP8)
    asb("ffnb", (128, 1024), BF16)
    asb("H", (128, 2048), F32)
    asb("Hb", (128, 2048), FP16)
    asb("id_bf", (128, 128), BF16)
    asb("id_f16", (128, 128), FP16)
    asb("id32", (32, 32), F32)
    asb("tri4", (128, 512), BF16)
    asb("antiI", (128, 128), FP16)
    asb("selth_sb", (32, 1024), F32)
    asb("seldec_sb", (32, 1024), F32)
    asb("seldch_sb", (32, 128), F32)
    asb("i2dch_sb", (32, 8), F32)
    asb("ones2", (128, 128), BF16)
    asb("ones1f", (1, 128), F32)
    asb("onesdt_sb", (32, 256), F32)
    asb("nw_sb", (128, 16), F32)
    asb("scals", (128, 8), F32)
    asb("sc32", (32, 8), F32)
    asb("Wbuf", (128, 16384), BF16)
    asb("xt", (128, 2, 2, 1024), F32)
    asb("xn_tm", (128, 1024), BF16)
    asb("xn_fm", (128, 2, 8, 256), BF16)
    asb("xn_q", (128, 8, 256), FP8)
    asb("dts", (32, 2, 257), F32)
    asb("Lc", (32, 257), F32)
    asb("ldt", (32, 257), F32)
    asb("zs", (128, 2, 2048), FP16)
    asb("yg", (128, 2, 2048), FP16)
    asb("bufA", (128, 16, 256), FP16)
    asb("bufB", (128, 16, 256), FP16)
    asb("bufC", (128, 16, 256), FP16)
    asb("xr", (128, 2, 2048), FP16)
    asb("cth", (128, 8, 256), FP16)
    asb("sth", (128, 8, 256), FP16)
    asb("Btm", (128, 2048), FP16)
    asb("X2", (128, 2048), FP16)
    asb("e1b", (128, 1024), F32)
    asb("scr4k", (128, 1024), F32)
    asb("M1b", (128, 1024), FP16)
    asb("M2b", (128, 1024), FP16)
    asb("Sm1", (128, 1024), FP16)
    asb("Sm2", (128, 1024), FP16)
    asb("decr", (128, 8, 128), FP16)
    asb("Cdec", (128, 2, 8, 128), FP16)
    asb("Pq", (32, 128), F32)
    asb("Psplit", (64, 128), BF16)
    asb("Pflat", (128, 1024), BF16)
    asb("Mt", (128, 32), F32)
    asb("Mcol1", (128, 32), F32)
    asb("l1t", (128, 32), F32)
    asb("w1t", (128, 32), F32)
    asb("w2t", (128, 32), F32)
    asb("ratio", (128, 32), F32)
    asb("exS", (128, 32), F32)
    asb("ccol", (128, 32), F32)
    asb("rhsd", (32, 8), F32)
    asb("gbuf", (128, 2, 1024), BF16)
    asb("sgb", (128, 1024), F32)
    asb("outb", (128, 1024), F32)
    asb("lnstat", (128, 12), F32)
    asb("lnmv", (128, 2), F32)
    asb("lnsd", (128, 1), F32)
    asb("lnrs", (128, 1), F32)
    asb("lnmb", (128, 1), F32)
    asb("dtmp", (32, 256), F32)
    asb("dtmp2", (32, 256), F32)
    asb("dcol", (32, 2), F32)

    aps("pmm0", (128, 512), F32)
    aps("pmm1", (128, 512), F32)
    aps("pe", (128, 1024), F32)
    aps("psc", (128, 512), F32)
    aps("psc2", (128, 512), F32)
    aps("py", (128, 512), F32)
    aps("pt", (128, 1024), FP16)

    X.D, X.S, X.PS = D, S, PS
    _build(X)
    P.finalize()

    sems = {e: es.enter_context(nc.semaphore("sem_" + e)) for e in STREAM_OF}
    with es:
        blk = es.enter_context(nc.Block())
        @blk.sync
        def _(eng): P.emit_stream("sync", eng, sems)
        @blk.tensor
        def _(eng): P.emit_stream("tensor", eng, sems)
        @blk.vector
        def _(eng): P.emit_stream("vector", eng, sems)
        @blk.scalar
        def _(eng): P.emit_stream("scalar", eng, sems)
        @blk.gpsimd
        def _(eng): P.emit_stream("gpsimd", eng, sems)
    return nc



def _build(X):
    D, S, PS, add = X.D, X.S, X.PS, X.P.add

    def tap(name, ap_fn, reads):
        if name in X.taps:
            dst = X.taps[name]
            add("QG", lambda e: e.dma_start(out=dst, in_=ap_fn()),
                reads=list(reads), writes=["tapw_" + name], tag="tap_" + name)
    X.tap = tap

    for nm, dst in [("ident_bf", "id_bf"), ("ident_f16", "id_f16"),
                    ("ident_f32_32", "id32"), ("tri_rep4", "tri4"),
                    ("anti_ident", "antiI"), ("selth", "selth_sb"),
                    ("seldec", "seldec_sb"), ("seldch", "seldch_sb"),
                    ("i2dch", "i2dch_sb"), ("ones2_bf", "ones2"),
                    ("ones1_f32", "ones1f"), ("onesdt", "onesdt_sb"),
                    ("nw_fm", "nw_sb"), ("ffnb_rep", "ffnb")]:
        add("QS", lambda e, a=D[nm], b=S[dst]: e.dma_start(out=b[:], in_=a),
            writes=[dst], tag="ld_" + nm)
    add("QG", lambda e: e.dma_start(out=S["Wdt_sb"][:], in_=D["Wdt"]),
        writes=["Wdt_sb"], tag="ld_Wdt")
    add("QS", lambda e: e.dma_start(out=S["scals"][:, 0:1], in_=D["s_in_col"]),
        writes=["sin0"], tag="ld_sin")
    add("QS", lambda e: e.dma_start(out=S["scals"][:, 1:2], in_=D["s_k_col"]),
        writes=["sk0"], tag="ld_sk")
    add("QS", lambda e: e.dma_start(out=S["sc32"][:, 0:1], in_=D["dtb_col"]),
        writes=["dtb"], tag="ld_dtb")
    add("QS", lambda e: e.dma_start(out=S["sc32"][:, 1:2], in_=D["alog_col"]),
        writes=["alog"], tag="ld_alog")
    add("V", lambda e: e.memset(S["scals"][:, 5:6], LN_EPS), writes=["epsln"], tag="epsln")
    add("V", lambda e: e.memset(S["scals"][:, 6:7], RMS_EPS), writes=["epsrms"], tag="epsrms")
    add("V", lambda e: e.memset(S["scals"][:, 7:8], math.pi / 2), writes=["pih"], tag="pih")
    add("V", lambda e: e.reciprocal(out=S["scals"][:, 2:3], in_=S["scals"][:, 0:1]),
        reads=["sin0"], writes=["rsin"], tag="rsin")
    add("V", lambda e: e.reciprocal(out=S["scals"][:, 3:4], in_=S["scals"][:, 1:2]),
        reads=["sk0"], writes=["rsk"], tag="rsk")
    add("V", lambda e: e.tensor_tensor(out=S["scals"][:, 4:5], in0=S["scals"][:, 0:1],
                                       in1=S["scals"][:, 1:2], op=ALU.mult),
        reads=["sin0", "sk0"], writes=["sprod"], tag="sprod")
    add("S", lambda e: e.activation(out=S["sc32"][:, 3:4], in_=S["sc32"][:, 1:2],
                                    func=AF.Exp),
        reads=["alog"], writes=["aexp"], tag="aexp")
    add("V", lambda e: e.tensor_scalar(out=S["sc32"][:, 2:3], in0=S["sc32"][:, 3:4],
                                       scalar1=-1.0, scalar2=None, op0=ALU.mult),
        reads=["aexp"], writes=["A"], tag="A")
    add("QS", lambda e: e.dma_start(
            out=S["Wbuf"][:, 0:8192].rearrange("p (c n) -> p c n", c=8),
            in_=D["ffnk"]),
        writes=["WbufA"], tag="ld_ffnk")
    add("V", lambda e: e.tensor_scalar(
            out=S["ffn_q"][:].rearrange("p c n -> p (c n)"),
            in0=S["Wbuf"][:, 0:8192], scalar1=S["scals"][:, 3:4],
            scalar2=None, op0=ALU.mult),
        reads=["WbufA", "rsk"], writes=["ffn_q"], tag="q_ffnk")
    def _nw_fold():
        for kc2 in range(16):
            for hf in range(2):
                sl = slice(kc2 * 1024 + hf * 512, kc2 * 1024 + hf * 512 + 512)
                add("V", lambda e, s=sl, k=kc2: e.tensor_scalar(
                        out=S["xn_tm"][:, 0:512],
                        in0=S["Wbuf"][:, s], scalar1=S["nw_sb"][:, k:k + 1],
                        scalar2=None, op0=ALU.mult),
                    reads=["WbufA", "WbufB", "nw_sb", "xn_tm"], writes=["xn_tm"],
                    tag=f"nwf{kc2}_{hf}")
                add("V", lambda e, s=sl: e.tensor_copy(
                        out=S["Wbuf"][:, s], in_=S["xn_tm"][:, 0:512]),
                    reads=["xn_tm", "WbufA", "WbufB"],
                    writes=["WbufA" if kc2 < 8 else "WbufB"], tag=f"nwc{kc2}_{hf}")

    import os as _os2
    wlist = [] if _os2.environ.get("KLNONLY") else [
        ("Wz", "scr_z", 8), ("WB", "scr_B", 8),
        ("WC", "scr_C", 8), ("Wxr", "scr_xr", 8),
        ("Wout", "scr_out", 16)]
    for wsrc, wscr, kc in wlist:
        add("QG", lambda e, a=D[wsrc], kk=kc: e.dma_start(
                out=S["Wbuf"][:].rearrange("p (c n) -> p c n", c=kk),
                in_=a.rearrange("(c p) n -> p c n", p=128)),
            reads=["ffn_q"], writes=["WbufA", "WbufB"], tag="cast_" + wsrc)
        if wsrc == "Wout":
            _nw_fold()
        if kc == 8:
            for hf2 in range(2):
                add("QG", lambda e, b=D[wscr], h=hf2: e.dma_start(
                        out=b[:, h * 8192:(h + 1) * 8192],
                        in_=S["Wbuf"][:].rearrange("p (c n) -> p c n", c=8)
                        [:, :, h * 1024:(h + 1) * 1024]),
                    reads=["WbufA", "WbufB"], writes=[wscr], tag=f"st_{wscr}{hf2}")
        else:
            add("QG", lambda e, b=D[wscr]: e.dma_start(out=b, in_=S["Wbuf"][:]),
                reads=["WbufA", "WbufB"], writes=[wscr], tag="st_" + wscr)

    import os as _os
    if _os.environ.get("KLNONLY"):
        _ln_dt(X, 0)
        return
    for i in range(X.n_tiles):
        _tile(X, i)

    import os
    if int(os.environ.get("KPHASES", "99")) >= 6:
        add("G", lambda e: e.nop(),
            reads=[f"outdma{i}" for i in range(X.n_tiles)], tag="final")


def _ln_dt(X, j):
    S, PS, D, add = X.S, X.PS, X.D, X.P.add
    pj = j % 2
    xP = f"xtP{pj}"
    fmP = f"xnfmP{pj}"
    dtP = f"dtP{pj}"
    xtj = S["xt"][:, pj]
    add("QS", lambda e: e.dma_start(
            out=xtj, in_=D["x"][j * 256:(j + 1) * 256, :]
            .rearrange("(tc p) n -> p tc n", p=128)),
        writes=[xP], tag=f"ldx{j}")
    X.tap(f"xt{j}", lambda: xtj.rearrange("p a n -> p (a n)"), [xP])
    for tc in range(2):
        xs = xtj[:, tc, :]
        add("V", lambda e, a=xs: e.tensor_reduce(out=S["lnmv"][:, 0:1], in_=a,
                                                 axis=mybir.AxisListType.X,
                                                 op=ALU.add),
            reads=[xP, "lnmv"], writes=["lnmv"], tag=f"lsum_{j}_{tc}", lagged=True)
        add("S", lambda e, a=xs: e.activation(out=S["xn_tm"][:], in_=a,
                                              func=AF.Square),
            reads=[xP, "xn_tm"], writes=["xn_tm"], tag=f"lsq_{j}_{tc}")
        add("V", lambda e: e.tensor_reduce(out=S["lnsd"][:, 0:1],
                                           in_=S["xn_tm"][:].bitcast(BF16),
                                           axis=mybir.AxisListType.X, op=ALU.add),
            reads=["xn_tm", "lnsd"], writes=["lnsd"], tag=f"lsqr_{j}_{tc}",
            lagged=True)
        add("V", lambda e: e.tensor_scalar(out=S["lnstat"][:, 0:1],
                                           in0=S["lnmv"][:, 0:1],
                                           scalar1=1.0 / 1024, scalar2=None,
                                           op0=ALU.mult),
            reads=["lnmv", "lnstat"], writes=["lnstat"], tag=f"lmean_{j}_{tc}")
        add("V", lambda e: e.tensor_scalar(out=S["lnstat"][:, 1:2],
                                           in0=S["lnsd"][:, 0:1],
                                           scalar1=1.0 / 1024, scalar2=None,
                                           op0=ALU.mult),
            reads=["lnsd", "lnstat"], writes=["lnstat"], tag=f"lms_{j}_{tc}")
        add("V", lambda e: e.tensor_tensor(out=S["lnstat"][:, 2:3],
                                           in0=S["lnstat"][:, 0:1],
                                           in1=S["lnstat"][:, 0:1],
                                           op=ALU.mult),
            reads=["lnstat"], writes=["lnstat"], tag=f"lm2_{j}_{tc}")
        add("V", lambda e: e.tensor_tensor(out=S["lnmv"][:, 1:2],
                                           in0=S["lnstat"][:, 1:2],
                                           in1=S["lnstat"][:, 2:3],
                                           op=ALU.subtract),
            reads=["lnstat", "lnmv"], writes=["lnmv"], tag=f"lvar_{j}_{tc}")
        add("S", lambda e: e.activation(out=S["lnsd"][:], in_=S["lnmv"][:, 1:2],
                                        func=AF.Sqrt, bias=S["scals"][:, 5:6]),
            reads=["lnmv", "epsln", "lnsd"], writes=["lnsd"], tag=f"lnsq_{j}_{tc}")
        add("V", lambda e: e.reciprocal(out=S["lnrs"][:], in_=S["lnsd"][:]),
            reads=["lnsd", "lnrs"], writes=["lnrs"], tag=f"lnrs_{j}_{tc}")
        if tc == 0:
            X.tap(f"lnstat{j}", lambda: S["lnstat"][:, :], ["lnstat"])
            X.tap(f"lnsum{j}", lambda: S["lnmv"][:, :], ["lnmv"])
            X.tap(f"lnsq{j}", lambda: S["lnsd"][:, :], ["lnsd"])
        add("V", lambda e: e.tensor_tensor(out=S["lnmb"][:], in0=S["lnstat"][:, 0:1],
                                           in1=S["lnrs"][:], op=ALU.mult),
            reads=["lnstat", "lnrs", "lnmb"], writes=["lnmb"], tag=f"lnmb_{j}_{tc}")
        add("V", lambda e, a=xs: e.tensor_scalar(
                out=S["xn_tm"][:], in0=a, scalar1=S["lnrs"][:, 0:1],
                scalar2=S["lnmb"][:, 0:1], op0=ALU.mult, op1=ALU.subtract),
            reads=[xP, "lnrs", "lnmb"], writes=["xn_tm"], tag=f"lnap_{j}_{tc}")
        for kc in range(8):
            add("T", lambda e, k=kc: e.transpose(
                    out=PS["pt"][:, k * 128:(k + 1) * 128].bitcast(BF16),
                    in_=S["xn_tm"][:, k * 128:(k + 1) * 128],
                    identity=S["id_bf"][:]),
                reads=["xn_tm", "id_bf"], writes=["pt"], tag=f"xnt_{j}_{tc}_{kc}")
        add("S", lambda e, t=tc: e.copy(
                out=S["xn_fm"][:, pj, :, t * 128:(t + 1) * 128],
                in_=PS["pt"][:].bitcast(BF16).rearrange("p (c n) -> p c n", c=8)),
            reads=["pt"], writes=[fmP], tag=f"xnev_{j}_{tc}")
    for kc in range(8):
        add("T", lambda e, k=kc: e.matmul(
                out=PS["pmm0"][:, 0:256], lhsT=S["Wdt_sb"][:, k, :],
                rhs=S["xn_fm"][:, pj, k, :], start=(k == 0), stop=(k == 7)),
            reads=[fmP, "Wdt_sb"], writes=["pmm0"], tag=f"dtmm_{j}_{kc}")
    add("S", lambda e: e.activation(out=S["dtmp"][:], in_=PS["pmm0"][0:32, 0:256],
                                    func=AF.Abs, bias=S["sc32"][:, 0:1]),
        reads=["pmm0", "dtb"], writes=["dtmp"], tag=f"dta_{j}")
    add("S", lambda e: e.activation(out=S["dtmp2"][:], in_=S["dtmp"][:],
                                    func=AF.Exp, scale=-1.0),
        reads=["dtmp"], writes=["dtmp2"], tag=f"dte_{j}")
    add("V", lambda e: e.tensor_scalar(out=S["dtmp"][:], in0=S["dtmp2"][:],
                                       scalar1=1.0, scalar2=None, op0=ALU.add),
        reads=["dtmp2"], writes=["dtmp"], tag=f"dtp1_{j}")
    add("S", lambda e: e.activation(out=S["dtmp2"][:], in_=S["dtmp"][:], func=AF.Ln),
        reads=["dtmp"], writes=["dtmp2"], tag=f"dtl_{j}")
    add("V", lambda e: e.tensor_scalar(out=S["dtmp"][:], in0=PS["pmm0"][0:32, 0:256],
                                       scalar1=S["sc32"][:, 0:1], scalar2=0.0,
                                       op0=ALU.add, op1=ALU.max),
        reads=["pmm0", "dtb"], writes=["dtmp"], tag=f"dtr_{j}")
    add("V", lambda e: e.tensor_tensor(out=S["dts"][:, pj, 0:256], in0=S["dtmp"][:],
                                       in1=S["dtmp2"][:], op=ALU.add),
        reads=["dtmp", "dtmp2"], writes=[dtP], tag=f"dt_{j}")
    X.tap(f"dt{j}", lambda pj_=pj: S["dts"][:, pj_, :], [dtP])
    X.tap(f"xnfm{j}", lambda pj_=pj: S["xn_fm"][:, pj_].rearrange("p c n -> p (c n)"),
          [fmP])


def _tile(X, i):
    S, add = X.S, X.P.add
    par = i % 2
    seq_start = (i % TILES_PER_SEQ) == 0

    if i == 0:
        _ln_dt(X, 0)
    if i + 1 < X.n_tiles:
        _ln_dt(X, i + 1)
        add("V", lambda e: e.tensor_copy(out=S["dts"][:, par, 256:257],
                                         in_=S["dts"][:, 1 - par, 0:1]),
            reads=[f"dtP{1 - par}", f"dtP{par}"], writes=[f"dtP{par}"],
            tag=f"dtx{i}")
    else:
        add("V", lambda e: e.memset(S["dts"][:, par, 256:257], 1.0),
            reads=[f"dtP{par}"], writes=[f"dtP{par}"], tag=f"dtx{i}")

    add("S", lambda e: e.activation(out=S["ldt"][:], in_=S["dts"][:, par],
                                    func=AF.Ln, scale=0.5),
        reads=[f"dtP{par}"], writes=["ldt"], tag=f"ldt{i}")
    if seq_start:
        add("V", lambda e: e.memset(S["Lc"][:, 0:1], 0.0),
            reads=["Lc"], writes=["Lc"], tag=f"Lz{i}")
    else:
        add("V", lambda e: e.tensor_copy(out=S["Lc"][:, 0:1], in_=S["Lc"][:, 256:257]),
            reads=["Lc"], writes=["Lc"], tag=f"Lcp{i}")
    add("V", lambda e: e.tensor_tensor_scan(
            out=S["Lc"][:, 1:257], data0=S["onesdt_sb"][:],
            data1=S["dts"][:, par, 0:256], initial=S["Lc"][:, 0:1],
            op0=ALU.mult, op1=ALU.add),
        reads=[f"dtP{par}", "onesdt_sb", "Lc"], writes=["Lc"], tag=f"Lsc{i}")
    X.tap(f"L{i}", lambda: S["Lc"][:, :], ["Lc"])

    import os
    ph = int(os.environ.get("KPHASES", "99"))
    if ph >= 1:
        _ffn(X, i, par)
    if ph >= 2:
        _inproj(X, i, par)
    if ph >= 3:
        _angles(X, i, par)
    if ph >= 4:
        _rope(X, i, par)
    if ph >= 5:
        for c in range(NCH):
            _chunk(X, i, c, par, seq_start and c == 0)
    if ph >= 6:
        _epilogue(X, i, par)


def _ffn(X, i, par):
    S, PS, add = X.S, X.PS, X.P.add
    fmP = f"xnfmP{par}"
    add("V", lambda e: e.tensor_scalar(
            out=S["xn_q"][:].rearrange("p c n -> p (c n)"),
            in0=S["xn_fm"][:, par].rearrange("p c n -> p (c n)"),
            scalar1=S["scals"][:, 2:3], scalar2=None, op0=ALU.mult),
        reads=[fmP, "rsin"], writes=["xn_q"], tag=f"xq{i}")
    X.tap(f"xnq{i}", lambda: S["xn_q"][:].rearrange("p c n -> p (c n)"), ["xn_q"])
    for tc in range(2):
        for ng in range(2):
            pm = "pmm0" if ng == 0 else "pmm1"
            for kc in range(8):
                add("T", lambda e, k=kc, t=tc, n=ng, p=pm: e.matmul(
                        out=PS[p][:, 0:512],
                        lhsT=S["xn_q"][:, k, t * 128:(t + 1) * 128],
                        rhs=S["ffn_q"][:, k, n * 512:(n + 1) * 512],
                        start=(k == 0), stop=(k == 7)),
                    reads=["xn_q", "ffn_q"], writes=[pm], tag=f"ffmm{i}_{tc}_{ng}_{kc}")
            add("S", lambda e, p=pm: e.activation(
                    out=S["xn_tm"][:, 0:512].bitcast(BF16), in_=PS[p][:, 0:512],
                    func=AF.Copy, scale=S["scals"][:, 4:5]),
                reads=[pm, "sprod", "xn_tm"], writes=["xn_tm"], tag=f"ffc{i}_{tc}_{ng}")
            add("V", lambda e, n=ng: e.tensor_tensor(
                    out=S["xn_tm"][:, 512:1024].bitcast(BF16),
                    in0=S["xn_tm"][:, 0:512].bitcast(BF16),
                    in1=S["ffnb"][:, n * 512:(n + 1) * 512], op=ALU.add),
                reads=["xn_tm", "ffnb"], writes=["xn_tm"], tag=f"ffv{i}_{tc}_{ng}")
            add("S", lambda e, t=tc, n=ng: e.activation(
                    out=S["gbuf"][:, t, n * 512:(n + 1) * 512],
                    in_=S["xn_tm"][:, 512:1024].bitcast(BF16), func=AF.Silu),
                reads=["xn_tm"], writes=["gbuf"], tag=f"ffs{i}_{tc}_{ng}")
    X.tap(f"g{i}", lambda: S["gbuf"][:].rearrange("p c n -> p (c n)"), ["gbuf"])


def _inproj(X, i, par):
    S, PS, D, add = X.S, X.PS, X.D, X.P.add
    fmP = f"xnfmP{par}"
    for scr, dstb, kind in [("scr_B", "bufA", "B"), ("scr_C", "bufB", "C")]:
        for hf in range(2):
            add("QS", lambda e, a=D[scr], h=hf: e.dma_start(
                    out=S["Wbuf"][:, h * 8192:(h + 1) * 8192],
                    in_=a[:, h * 8192:(h + 1) * 8192]),
                writes=["WbufA" if hf == 0 else "WbufB"], tag=f"ldw{kind}{i}_{hf}")
        for mc in range(16):
            pm = "pmm0" if (mc % 2 == 0) else "pmm1"
            half = 0 if mc < 8 else 1
            for kc in range(8):
                add("T", lambda e, m=mc, k=kc, p=pm, h=half: e.matmul(
                        out=PS[p][:, 0:256],
                        lhsT=S["Wbuf"][:, h * 8192 + k * 1024 + (m % 8) * 128:
                                        h * 8192 + k * 1024 + (m % 8) * 128 + 128],
                        rhs=S["xn_fm"][:, par, k, :],
                        start=(k == 0), stop=(k == 7)),
                    reads=["WbufA" if half == 0 else "WbufB", fmP],
                    writes=[pm], tag=f"ip{kind}{i}_{mc}_{kc}")
            if mc % 2 == 0:
                add("S", lambda e, m=mc, p=pm, d=dstb: e.copy(
                        out=S[d][:, m, :], in_=PS[p][:, 0:256]),
                    reads=[pm], writes=[dstb, f"{dstb}{i}_{mc}"],
                    tag=f"{kind}ev{i}_{mc}")
            else:
                add("V", lambda e, m=mc, p=pm, d=dstb: e.tensor_copy(
                        out=S[d][:, m, :], in_=PS[p][:, 0:256]),
                    reads=[pm], writes=[dstb, f"{dstb}{i}_{mc}"],
                    tag=f"{kind}ev{i}_{mc}")
    for hf in range(2):
        add("QS", lambda e, h=hf: e.dma_start(
                out=S["Wbuf"][:, h * 8192:(h + 1) * 8192],
                in_=D["scr_z"][:, h * 8192:(h + 1) * 8192]),
            writes=["WbufA" if hf == 0 else "WbufB"], tag=f"ldwz{i}_{hf}")
    for tc in range(2):
        for ng in range(4):
            pm = "pmm0" if (ng % 2 == 0) else "pmm1"
            half = 0 if ng < 2 else 1
            for kc in range(8):
                add("T", lambda e, t=tc, n=ng, k=kc, p=pm, h=half: e.matmul(
                        out=PS[p][:, 0:512],
                        lhsT=S["xn_fm"][:, par, k, t * 128:(t + 1) * 128],
                        rhs=S["Wbuf"][:, h * 8192 + k * 1024 + (n % 2) * 512:
                                       h * 8192 + k * 1024 + (n % 2) * 512 + 512],
                        start=(k == 0), stop=(k == 7)),
                    reads=["WbufA" if half == 0 else "WbufB", fmP],
                    writes=[pm], tag=f"zm{i}_{tc}_{ng}_{kc}")
            add("S", lambda e, t=tc, n=ng, p=pm: e.activation(
                    out=S["zs"][:, t, n * 512:(n + 1) * 512], in_=PS[p][:, 0:512],
                    func=AF.Silu),
                reads=[pm], writes=["zs"], tag=f"zev{i}_{tc}_{ng}")
    for hf in range(2):
        add("QS", lambda e, h=hf: e.dma_start(
                out=S["Wbuf"][:, h * 8192:(h + 1) * 8192],
                in_=D["scr_xr"][:, h * 8192:(h + 1) * 8192]),
            writes=["WbufA" if hf == 0 else "WbufB"], tag=f"ldwxr{i}_{hf}")
    for tc in range(2):
        for ng in range(4):
            pm = "pmm0" if (ng % 2 == 0) else "pmm1"
            half = 0 if ng < 2 else 1
            for kc in range(8):
                add("T", lambda e, t=tc, n=ng, k=kc, p=pm, h=half: e.matmul(
                        out=PS[p][:, 0:512],
                        lhsT=S["xn_fm"][:, par, k, t * 128:(t + 1) * 128],
                        rhs=S["Wbuf"][:, h * 8192 + k * 1024 + (n % 2) * 512:
                                       h * 8192 + k * 1024 + (n % 2) * 512 + 512],
                        start=(k == 0), stop=(k == 7)),
                    reads=["WbufA" if half == 0 else "WbufB", fmP],
                    writes=[pm], tag=f"xrm{i}_{tc}_{ng}_{kc}")
            add("S", lambda e, t=tc, n=ng, p=pm: e.copy(
                    out=S["xr"][:, t, n * 512:(n + 1) * 512], in_=PS[p][:, 0:512]),
                reads=[pm], writes=["xr", f"xr{i}_{tc}"], tag=f"xrev{i}_{tc}_{ng}")
    X.tap(f"zs{i}", lambda: S["zs"][:].rearrange("p c n -> p (c n)"), ["zs"])
    X.tap(f"zsm{i}", lambda: S["zs"][:].rearrange("p c n -> p (c n)"), ["zs"])
    X.tap(f"Braw{i}", lambda: S["bufA"][:].rearrange("p c n -> p (c n)"),
          ["bufA"] + [f"bufA{i}_{m}" for m in range(16)])
    X.tap(f"xr{i}", lambda: S["xr"][:].rearrange("p c n -> p (c n)"),
          ["xr", f"xr{i}_0", f"xr{i}_1"])


def _angles(X, i, par):
    S, PS, add = X.S, X.PS, X.P.add
    for hf in range(2):
        for pc in range(4):
            g = 4 * hf + pc
            add("T", lambda e, gg=g, p=pc: e.matmul(
                    out=PS["pe"][:, p * 256:(p + 1) * 256],
                    lhsT=S["selth_sb"][:, gg * 128:(gg + 1) * 128],
                    rhs=S["Lc"][:, 1:257], start=True, stop=True),
                reads=["Lc", "selth_sb"], writes=["pe"], tag=f"thmm{i}_{g}")
        add("V", lambda e: e.tensor_scalar(out=S["e1b"][:], in0=PS["pe"][:],
                                           scalar1=INV_2PI, scalar2=MAGIC,
                                           op0=ALU.mult, op1=ALU.add),
            reads=["pe"], writes=["e1b"], tag=f"thu{i}_{hf}")
        add("V", lambda e: e.tensor_scalar(out=S["scr4k"][:], in0=S["e1b"][:],
                                           scalar1=MAGIC, scalar2=None,
                                           op0=ALU.subtract),
            reads=["e1b"], writes=["scr4k"], tag=f"thk{i}_{hf}")
        add("V", lambda e: e.scalar_tensor_tensor(
                out=S["e1b"][:], in0=S["scr4k"][:], scalar=-TWO_PI,
                in1=PS["pe"][:], op0=ALU.mult, op1=ALU.add),
            reads=["scr4k", "pe", "e1b"], writes=["e1b"], tag=f"thr{i}_{hf}")
        add("S", lambda e, h=hf: e.activation(
                out=S["sth"][:, 4 * h:4 * h + 4, :].rearrange("p c n -> p (c n)"),
                in_=S["e1b"][:], func=AF.Sin),
            reads=["e1b"], writes=["sth"], tag=f"sin{i}_{hf}")
        add("S", lambda e: e.activation(out=S["scr4k"][:], in_=S["e1b"][:],
                                        func=AF.Abs),
            reads=["e1b", "scr4k"], writes=["scr4k"], tag=f"tha{i}_{hf}")
        add("S", lambda e, h=hf: e.activation(
                out=S["cth"][:, 4 * h:4 * h + 4, :].rearrange("p c n -> p (c n)"),
                in_=S["scr4k"][:], func=AF.Sin, scale=-1.0, bias=S["scals"][:, 7:8]),
            reads=["scr4k", "pih"], writes=["cth"], tag=f"cos{i}_{hf}")
    X.tap(f"cth{i}", lambda: S["cth"][:].rearrange("p c n -> p (c n)"), ["cth"])
    X.tap(f"sth{i}", lambda: S["sth"][:].rearrange("p c n -> p (c n)"), ["sth"])


def _rope(X, i, par):
    S, add = X.S, X.P.add

    def rot(srcb, dstb, src_reads):
        src, dst = S[srcb], S[dstb]
        x0 = src[:, 0:8, :].rearrange("p c n -> p (c n)")
        x1 = src[:, 8:16, :].rearrange("p c n -> p (c n)")
        o0 = dst[:, 0:8, :].rearrange("p c n -> p (c n)")
        o1 = dst[:, 8:16, :].rearrange("p c n -> p (c n)")
        ct = S["cth"][:].rearrange("p c n -> p (c n)")
        st = S["sth"][:].rearrange("p c n -> p (c n)")
        t1, t2 = S["Btm"][:], S["X2"][:]
        sr = src_reads + ["cth", "sth"]
        add("V", lambda e: e.tensor_tensor(out=t1, in0=x0, in1=ct, op=ALU.mult),
            reads=sr + ["Btm"], writes=["Btm"], tag=f"rp1{dstb}{i}")
        add("G", lambda e: e.tensor_tensor(out=t2, in0=x1, in1=st, op=ALU.mult),
            reads=sr + ["X2"], writes=["X2"], tag=f"rp2{dstb}{i}")
        add("V", lambda e: e.tensor_tensor(out=o0, in0=t1, in1=t2, op=ALU.subtract),
            reads=["Btm", "X2", dstb], writes=[dstb], tag=f"rp3{dstb}{i}")
        add("V", lambda e: e.tensor_tensor(out=t1, in0=x0, in1=st, op=ALU.mult),
            reads=sr + ["Btm"], writes=["Btm"], tag=f"rp4{dstb}{i}")
        add("G", lambda e: e.tensor_tensor(out=t2, in0=x1, in1=ct, op=ALU.mult),
            reads=sr + ["X2"], writes=["X2"], tag=f"rp5{dstb}{i}")
        add("V", lambda e: e.tensor_tensor(out=o1, in0=t1, in1=t2, op=ALU.add),
            reads=["Btm", "X2", dstb], writes=[dstb], tag=f"rp6{dstb}{i}")

    rot("bufA", "bufC", ["bufA"] + [f"bufA{i}_{m}" for m in range(16)])
    rot("bufB", "bufA", ["bufB"] + [f"bufB{i}_{m}" for m in range(16)])
    X.tap(f"Brot{i}", lambda: S["bufC"][:].rearrange("p c n -> p (c n)"), ["bufC"])
    X.tap(f"Crot{i}", lambda: S["bufA"][:].rearrange("p c n -> p (c n)"), ["bufA"])


def _chunk(X, i, c, par, first_of_seq):
    import os
    sub = int(os.environ.get("KCHSUB", "9"))
    S, PS, add = X.S, X.PS, X.P.add
    cs = slice(c * 128, (c + 1) * 128)
    b_active = i // TILES_PER_SEQ
    boff = b_active * 1024
    dtP = f"dtP{par}"

    add("V", lambda e: e.tensor_scalar(
            out=S["Pq"][:], in0=S["Lc"][:, 1 + c * 128:1 + c * 128 + 128],
            scalar1=S["Lc"][:, c * 128 + 128:c * 128 + 129],
            scalar2=S["sc32"][:, 2:3], op0=ALU.subtract, op1=ALU.mult),
        reads=["Lc", "A", "Pq"], writes=["Pq"], tag=f"P{i}_{c}")
    add("V", lambda e: e.tensor_copy(out=S["Psplit"][0:32, :], in_=S["Pq"][:]),
        reads=["Pq", "Psplit"], writes=["Psplit"], tag=f"Phi{i}_{c}")
    add("V", lambda e: e.tensor_tensor(out=S["Psplit"][32:64, :], in0=S["Pq"][:],
                                       in1=S["Psplit"][0:32, :], op=ALU.subtract),
        reads=["Pq", "Psplit"], writes=["Psplit"], tag=f"Plo{i}_{c}")
    add("T", lambda e: e.transpose(out=PS["pmm0"][:, 384:416], in_=S["Pq"][:],
                                   identity=S["id32"][:]),
        reads=["Pq", "id32", "pmm0"], writes=["pmm0"], tag=f"tM{i}_{c}")
    add("T", lambda e: e.transpose(out=PS["pmm0"][:, 416:448],
                                   in_=S["ldt"][:, c * 128:c * 128 + 128],
                                   identity=S["id32"][:]),
        reads=["ldt", "id32"], writes=["pmm0"], tag=f"tl1{i}_{c}")
    add("T", lambda e: e.transpose(out=PS["pmm0"][:, 448:480],
                                   in_=S["dts"][:, par, c * 128:c * 128 + 128],
                                   identity=S["id32"][:]),
        reads=[dtP, "id32"], writes=["pmm0"], tag=f"tw1{i}_{c}")
    add("T", lambda e: e.transpose(out=PS["pmm0"][:, 480:512],
                                   in_=S["dts"][:, par, c * 128 + 1:c * 128 + 129],
                                   identity=S["id32"][:]),
        reads=[dtP, "id32"], writes=["pmm0"], tag=f"tw2{i}_{c}")
    add("V", lambda e: e.tensor_copy(out=S["Mt"][:], in_=PS["pmm0"][:, 384:416]),
        reads=["pmm0", "Mt"], writes=["Mt"], tag=f"Mt{i}_{c}")
    add("V", lambda e: e.tensor_copy(out=S["l1t"][:], in_=PS["pmm0"][:, 416:448]),
        reads=["pmm0", "l1t"], writes=["l1t"], tag=f"l1t{i}_{c}")
    add("S", lambda e: e.activation(out=S["w1t"][:], in_=PS["pmm0"][:, 448:480],
                                    func=AF.Copy, scale=0.5),
        reads=["pmm0", "w1t"], writes=["w1t"], tag=f"w1t{i}_{c}")
    add("S", lambda e: e.activation(out=S["w2t"][:], in_=PS["pmm0"][:, 480:512],
                                    func=AF.Copy, scale=0.5),
        reads=["pmm0", "w2t"], writes=["w2t"], tag=f"w2t{i}_{c}")
    add("V", lambda e: e.tensor_tensor(out=S["Mcol1"][:], in0=S["Mt"][:],
                                       in1=S["l1t"][:], op=ALU.subtract),
        reads=["Mt", "l1t", "Mcol1"], writes=["Mcol1"], tag=f"Mc1{i}_{c}")
    add("V", lambda e: e.reciprocal(out=S["exS"][:], in_=S["w1t"][:]),
        reads=["w1t", "exS"], writes=["exS"], tag=f"rw1{i}_{c}")
    add("V", lambda e: e.tensor_tensor(out=S["ratio"][:], in0=S["w2t"][:],
                                       in1=S["exS"][:], op=ALU.mult),
        reads=["w2t", "exS", "ratio"], writes=["ratio"], tag=f"ratio{i}_{c}")
    add("V", lambda e: e.tensor_tensor(out=S["l1t"][:], in0=S["w1t"][:],
                                       in1=S["w2t"][:], op=ALU.add),
        reads=["w1t", "w2t", "l1t"], writes=["l1t"], tag=f"wsum{i}_{c}")
    add("S", lambda e: e.activation(out=S["exS"][:], in_=S["Mt"][:], func=AF.Exp,
                                    scale=-1.0),
        reads=["Mt", "exS"], writes=["exS"], tag=f"exS{i}_{c}")
    add("V", lambda e: e.tensor_tensor(out=S["ccol"][:], in0=S["l1t"][:],
                                       in1=S["exS"][:], op=ALU.mult),
        reads=["l1t", "exS", "ccol"], writes=["ccol"], tag=f"ccol{i}_{c}")
    add("V", lambda e: e.tensor_tensor(
            out=S["X2"][:].rearrange("p (h q) -> p h q", h=32),
            in0=S["xr"][:, c, :].rearrange("p (h q) -> p h q", h=32),
            in1=S["ccol"][:].unsqueeze(2).broadcast_to([128, 32, 64]),
            op=ALU.mult),
        reads=["xr", f"xr{i}_{c}", "ccol", "X2"], writes=["X2"], tag=f"X2{i}_{c}")
    for rnd in range(2):
        for k in range(8):
            mc = rnd * 8 + k
            add("T", lambda e, m=mc, kk=k: e.transpose(
                    out=PS["pt"][:, kk * 128:(kk + 1) * 128],
                    in_=S["bufC"][:, m, cs], identity=S["id_f16"][:]),
                reads=["bufC", "id_f16", "pt"], writes=["pt"],
                tag=f"btm{i}_{c}_{mc}")
        add("S", lambda e, r=rnd: e.copy(out=S["Btm"][:, r * 1024:(r + 1) * 1024],
                                         in_=PS["pt"][:]),
            reads=["pt", "Btm"], writes=["Btm"], tag=f"btmev{i}_{c}_{rnd}")

    if sub < 2:
        return
    if not first_of_seq:
        add("V", lambda e: e.tensor_scalar(
                out=S["dcol"][:, 0:1], in0=S["Lc"][:, c * 128 + 128:c * 128 + 129],
                scalar1=S["Lc"][:, c * 128:c * 128 + 1], scalar2=S["sc32"][:, 2:3],
                op0=ALU.subtract, op1=ALU.mult),
            reads=["Lc", "A", "dcol"], writes=["dcol"], tag=f"lndch{i}_{c}")
        add("V", lambda e: e.tensor_scalar(out=S["dtmp"][:, 0:128], in0=S["Pq"][:],
                                           scalar1=S["dcol"][:, 0:1], scalar2=None,
                                           op0=ALU.add),
            reads=["Pq", "dcol", "dtmp"], writes=["dtmp"], tag=f"deca{i}_{c}")
        for pc in range(8):
            add("T", lambda e, p=pc: e.matmul(
                    out=PS["pe"][:, p * 128:(p + 1) * 128],
                    lhsT=S["seldec_sb"][:, p * 128:(p + 1) * 128],
                    rhs=S["dtmp"][:, 0:128], start=True, stop=True),
                reads=["dtmp", "seldec_sb", "pe"], writes=["pe"],
                tag=f"decmm{i}_{c}_{pc}")
        add("S", lambda e: e.activation(
                out=S["decr"][:].rearrange("p c n -> p (c n)"), in_=PS["pe"][:],
                func=AF.Exp),
            reads=["pe", "decr"], writes=["decr"], tag=f"decev{i}_{c}")
        add("V", lambda e: e.tensor_tensor(
                out=S["Cdec"][:],
                in0=S["bufA"][:, :, cs].rearrange("p (b c) n -> p b c n", b=2),
                in1=S["decr"][:].unsqueeze(1).broadcast_to([128, 2, 8, 128]),
                op=ALU.mult),
            reads=["bufA", "decr", "Cdec"], writes=["Cdec"], tag=f"cdec{i}_{c}")
        add("S", lambda e: e.activation(out=S["dcol"][:, 1:2], in_=S["dcol"][:, 0:1],
                                        func=AF.Exp),
            reads=["dcol"], writes=["dcol"], tag=f"dch{i}_{c}")
        add("V", lambda e: e.tensor_scalar(out=S["rhsd"][:], in0=S["i2dch_sb"][:],
                                           scalar1=S["dcol"][:, 1:2], scalar2=None,
                                           op0=ALU.mult),
            reads=["dcol", "i2dch_sb", "rhsd"], writes=["rhsd"], tag=f"rhsd{i}_{c}")
        add("T", lambda e: e.matmul(out=PS["pmm1"][:, 504:512],
                                    lhsT=S["seldch_sb"][:], rhs=S["rhsd"][:],
                                    start=True, stop=True),
            reads=["rhsd", "seldch_sb", "pmm1"], writes=["pmm1"],
            tag=f"dchmm{i}_{c}")

    if sub < 3:
        return
    for q in range(4):
        _qA(X, i, c, q)
        if sub >= 4 and q > 0:
            _qD(X, i, c, q - 1)
        _qB(X, i, c, q)
        if sub >= 4:
            _qC(X, i, c, q, cs)
        if sub >= 5 and q > 0:
            _qE(X, i, c, q - 1, first_of_seq, boff)
    if sub >= 4:
        _qD(X, i, c, 3)
    if sub >= 5:
        _qE(X, i, c, 3, first_of_seq, boff)
    if sub < 6:
        return

    hsl = S["H"][:, boff:boff + 1024]
    hbsl = S["Hb"][:, boff:boff + 1024]
    if not first_of_seq:
        add("V", lambda e: e.tensor_tensor(
                out=S["scr4k"][:].rearrange("p (a b) -> p a b", a=8),
                in0=hsl.rearrange("p (a b) -> p a b", a=8),
                in1=PS["pmm1"][:, 504:512].unsqueeze(2).broadcast_to([128, 8, 128]),
                op=ALU.mult),
            reads=["H", "pmm1", "scr4k"], writes=["scr4k"], tag=f"hdec{i}_{c}")
    for qq in range(4):
        for mc in range(8):
            h = qq * 8 + mc
            for piece in range(2):
                cb = mc * 128 + (piece * 1024)
                fo = mc * 128 + piece * 64
                add("T", lambda e, c0=cb, f=fo, hh_=h: e.matmul(
                        out=PS["psc" if f < 512 else "psc2"]
                        [:, f % 512:f % 512 + 64],
                        lhsT=S["Btm"][:, c0:c0 + 128],
                        rhs=S["X2"][:, hh_ * 64:hh_ * 64 + 64],
                        start=True, stop=True),
                    reads=["Btm", "X2", "psc", "psc2"], writes=["psc", "psc2"],
                    tag=f"st{i}_{c}_{h}_{piece}")
        if first_of_seq:
            add("V", lambda e, b=qq: e.tensor_copy(
                    out=hsl[32 * b:32 * b + 32, 0:512],
                    in_=PS["psc"][32 * b:32 * b + 32, :]),
                reads=["psc", "H"], writes=["H"], tag=f"hb0{i}_{c}_{qq}")
            add("S", lambda e, b=qq: e.copy(
                    out=hsl[32 * b:32 * b + 32, 512:1024],
                    in_=PS["psc2"][32 * b:32 * b + 32, :]),
                reads=["psc2", "H"], writes=["H"], tag=f"hb1{i}_{c}_{qq}")
        else:
            add("V", lambda e, b=qq: e.tensor_tensor(
                    out=hsl[32 * b:32 * b + 32, 0:512],
                    in0=S["scr4k"][32 * b:32 * b + 32, 0:512],
                    in1=PS["psc"][32 * b:32 * b + 32, :], op=ALU.add),
                reads=["psc", "H", "scr4k"], writes=["H"], tag=f"hb0{i}_{c}_{qq}")
            add("V", lambda e, b=qq: e.tensor_tensor(
                    out=hsl[32 * b:32 * b + 32, 512:1024],
                    in0=S["scr4k"][32 * b:32 * b + 32, 512:1024],
                    in1=PS["psc2"][32 * b:32 * b + 32, :], op=ALU.add),
                reads=["psc2", "H", "scr4k"], writes=["H"], tag=f"hb1{i}_{c}_{qq}")
    add("G", lambda e: e.tensor_copy(out=hbsl, in_=hsl),
        reads=["H", "Hb"], writes=["Hb"], tag=f"hb{i}_{c}")
    X.tap(f"H{i}_{c}", lambda: S["H"][:, :], ["H"])


def _qA(X, i, c, q):
    S, PS, add = X.S, X.PS, X.P.add
    qb = 32 * q
    add("QA", lambda e: e.dma_start(out=S["Pflat"][qb:qb + 1, :],
                                    in_=S["Psplit"][q * 8:q * 8 + 8, :]),
        reads=["Psplit", "Pflat"], writes=["Pflat"], tag=f"pf0{i}_{c}_{q}")
    add("QA", lambda e: e.dma_start(out=S["Pflat"][qb + 1:qb + 2, :],
                                    in_=S["Psplit"][32 + q * 8:32 + q * 8 + 8, :]),
        reads=["Psplit", "Pflat"], writes=["Pflat"], tag=f"pf1{i}_{c}_{q}")
    for hf in range(2):
        sl = slice(hf * 512, (hf + 1) * 512)
        add("T", lambda e, s=sl: e.matmul(out=PS["pe"][:, s], lhsT=S["id_bf"][:],
                                          rhs=S["tri4"][:, 0:512], start=True,
                                          stop=False),
            reads=["tri4", "id_bf", "pe"], writes=["pe"], tag=f"et{i}_{c}_{q}_{hf}")
        add("T", lambda e, s=sl: e.matmul(out=PS["pe"][:, s],
                                          lhsT=S["ones2"][qb:qb + 2, :],
                                          rhs=S["Pflat"][qb:qb + 2, s], start=False,
                                          stop=True, tile_position=(qb, 0)),
            reads=["Pflat", "ones2", "pe"], writes=["pe"], tag=f"em{i}_{c}_{q}_{hf}")


def _qB(X, i, c, q):
    S, PS, add = X.S, X.PS, X.P.add
    add("V", lambda e: e.tensor_tensor(
            out=S["e1b"][:].rearrange("p (h t) -> p h t", h=8),
            in0=PS["pe"][:].rearrange("p (h t) -> p h t", h=8),
            in1=S["Mcol1"][:, q * 8:q * 8 + 8].unsqueeze(2)
            .broadcast_to([128, 8, 128]),
            op=ALU.subtract),
        reads=["pe", "Mcol1", "e1b"], writes=["e1b"], tag=f"e1{i}_{c}_{q}")
    add("S", lambda e: e.activation(out=S["M1b"][:], in_=S["e1b"][:], func=AF.Exp),
        reads=["e1b", "M1b"], writes=["M1b"], tag=f"M1{i}_{c}_{q}")
    if q == 0 and c == 0:
        X.tap(f"M1_{i}", lambda: S["M1b"][:], ["M1b"])


def _qC(X, i, c, q, cs):
    S, PS, add = X.S, X.PS, X.P.add
    import os as _os
    hh4max = int(_os.environ.get("KHH4", "4"))
    for g in range(2):
        pg = "psc" if g == 0 else "psc2"
        for hh4 in range(hh4max):
            hh = g * 4 + hh4
            h = q * 8 + hh
            m = q
            mcx0 = h % 8
            fo = hh4 * 128
            add("T", lambda e, mm=m, mc=mcx0, f=fo, p=pg, s=cs: e.matmul(
                    out=PS[p][:, f:f + 128],
                    lhsT=S["bufC"][32 * mm:32 * mm + 32, mc, s],
                    rhs=S["bufA"][32 * mm:32 * mm + 32, mc, s],
                    start=True, stop=False, tile_position=(32 * mm, 0)),
                reads=["bufA", "bufC", pg], writes=[pg], tag=f"sc0{i}_{c}_{h}")
            add("T", lambda e, mm=m, mc=8 + mcx0, f=fo, p=pg, s=cs: e.matmul(
                    out=PS[p][:, f:f + 128],
                    lhsT=S["bufC"][32 * mm:32 * mm + 32, mc, s],
                    rhs=S["bufA"][32 * mm:32 * mm + 32, mc, s],
                    start=False, stop=True, tile_position=(32 * mm, 0)),
                reads=["bufA", "bufC"], writes=[pg], tag=f"sc1{i}_{c}_{h}")


def _qD(X, i, c, q):
    S, PS, add = X.S, X.PS, X.P.add
    m2t = S["e1b"][:, 0:512].bitcast(FP16)
    add("V", lambda e: e.tensor_tensor(
            out=m2t.rearrange("p (h t) -> p h t", h=8),
            in0=S["M1b"][:].rearrange("p (h t) -> p h t", h=8),
            in1=S["ratio"][:, q * 8:q * 8 + 8].unsqueeze(2)
            .broadcast_to([128, 8, 128]),
            op=ALU.mult),
        reads=["M1b", "ratio"], writes=["e1b"], tag=f"M2a{i}_{c}_{q}")
    add("V", lambda e: e.tensor_tensor(
            out=S["M2b"][:].rearrange("p (h t) -> p h t", h=8),
            in0=m2t.rearrange("p (h t) -> p h t", h=8),
            in1=S["antiI"][:].unsqueeze(1).broadcast_to([128, 8, 128]),
            op=ALU.mult),
        reads=["e1b", "antiI", "M2b"], writes=["M2b"], tag=f"M2b{i}_{c}_{q}")
    for g in range(2):
        pg = "psc" if g == 0 else "psc2"
        add("V", lambda e, p=pg, gg=g: e.tensor_tensor(
                out=S["Sm1"][:, gg * 512:(gg + 1) * 512], in0=PS[p][:],
                in1=S["M1b"][:, gg * 512:(gg + 1) * 512], op=ALU.mult),
            reads=[pg, "M1b", "Sm1"], writes=["Sm1"], tag=f"sm1{i}_{c}_{q}_{g}")
        add("V", lambda e, p=pg, gg=g: e.tensor_tensor(
                out=S["Sm2"][:, gg * 512:(gg + 1) * 512], in0=PS[p][:],
                in1=S["M2b"][:, gg * 512:(gg + 1) * 512], op=ALU.mult),
            reads=[pg, "M2b", "Sm2"], writes=["Sm2"], tag=f"sm2{i}_{c}_{q}_{g}")
    if q == 0 and c == 0:
        X.tap(f"Sm1_{i}", lambda: S["Sm1"][:], ["Sm1"])
        X.tap(f"Sm2_{i}", lambda: S["Sm2"][:], ["Sm2"])


def _qE(X, i, c, q, first_of_seq, boff):
    S, PS, add = X.S, X.PS, X.P.add
    m = q
    for hh in range(8):
        h = q * 8 + hh
        m = q
        mcx0 = h % 8
        fo = hh * 64
        smo = hh * 128
        add("T", lambda e, f=fo, so=smo, hh_=h: e.matmul(
                out=PS["py"][:, f:f + 64],
                lhsT=S["Sm1"][:, so:so + 128],
                rhs=S["xr"][:, c, hh_ * 64:hh_ * 64 + 64],
                start=True, stop=False),
            reads=["Sm1", "xr", f"xr{i}_{c}", "py"], writes=["py"],
            tag=f"yi1{i}_{c}_{h}")
        add("T", lambda e, f=fo, so=smo, hh_=h: e.matmul(
                out=PS["py"][:, f:f + 64],
                lhsT=S["Sm2"][:, so:so + 128],
                rhs=S["xr"][:, c, hh_ * 64:hh_ * 64 + 64],
                start=False, stop=first_of_seq),
            reads=["Sm2"], writes=["py"], tag=f"yi2{i}_{c}_{h}")
        if not first_of_seq:
            add("T", lambda e, f=fo, mm=m, mc=mcx0: e.matmul(
                    out=PS["py"][:, f:f + 64],
                    lhsT=S["Cdec"][32 * mm:32 * mm + 32, 0, mc, :],
                    rhs=S["Hb"][32 * mm:32 * mm + 32,
                                boff + mc * 128:boff + mc * 128 + 64],
                    start=False, stop=False, tile_position=(32 * mm, 0)),
                reads=["Hb", "Cdec"], writes=["py"], tag=f"yn0{i}_{c}_{h}")
            add("T", lambda e, f=fo, mm=m, mc=mcx0: e.matmul(
                    out=PS["py"][:, f:f + 64],
                    lhsT=S["Cdec"][32 * mm:32 * mm + 32, 1, mc, :],
                    rhs=S["Hb"][32 * mm:32 * mm + 32,
                                boff + mc * 128 + 64:boff + mc * 128 + 128],
                    start=False, stop=True, tile_position=(32 * mm, 0)),
                reads=["Hb", "Cdec"], writes=["py"], tag=f"yn1{i}_{c}_{h}")
    add("V", lambda e: e.tensor_tensor(
            out=S["yg"][:, c, q * 512:(q + 1) * 512], in0=PS["py"][:],
            in1=S["zs"][:, c, q * 512:(q + 1) * 512], op=ALU.mult),
        reads=["py", "zs", "yg"], writes=["yg"], tag=f"ygev{i}_{c}_{q}")


def _epilogue(X, i, par):
    S, PS, D, add = X.S, X.PS, X.D, X.P.add
    X.tap(f"yg{i}", lambda: S["yg"][:].rearrange("p c n -> p (c n)"), ["yg"])
    for c in range(NCH):
        add("S", lambda e, cc=c: e.activation(out=S["e1b"][:],
                                              in_=S["yg"][:, cc, 0:1024],
                                              func=AF.Square),
            reads=["yg", "e1b"], writes=["e1b"], tag=f"rsq0{i}_{c}")
        add("V", lambda e: e.tensor_reduce(out=S["lnmv"][:, 0:1], in_=S["e1b"][:],
                                           axis=mybir.AxisListType.X, op=ALU.add),
            reads=["e1b", "lnmv"], writes=["lnmv"], tag=f"rr0{i}_{c}", lagged=True)
        add("S", lambda e, cc=c: e.activation(out=S["e1b"][:],
                                              in_=S["yg"][:, cc, 1024:2048],
                                              func=AF.Square),
            reads=["yg", "e1b"], writes=["e1b"], tag=f"rsq1{i}_{c}")
        add("V", lambda e: e.tensor_reduce(out=S["lnmv"][:, 1:2], in_=S["e1b"][:],
                                           axis=mybir.AxisListType.X, op=ALU.add),
            reads=["e1b", "lnmv"], writes=["lnmv"], tag=f"rr1{i}_{c}", lagged=True)
        add("V", lambda e: e.tensor_tensor(out=S["lnsd"][:], in0=S["lnmv"][:, 0:1],
                                           in1=S["lnmv"][:, 1:2], op=ALU.add),
            reads=["lnmv", "lnsd"], writes=["lnsd"], tag=f"rms{i}_{c}")
        add("S", lambda e: e.activation(out=S["lnrs"][:], in_=S["lnsd"][:],
                                        func=AF.Ln, scale=1.0 / 2048.0,
                                        bias=S["scals"][:, 6:7]),
            reads=["lnsd", "epsrms", "lnrs"], writes=["lnrs"], tag=f"rln{i}_{c}")
        add("S", lambda e: e.activation(out=S["lnmb"][:], in_=S["lnrs"][:],
                                        func=AF.Exp, scale=-0.5),
            reads=["lnrs", "lnmb"], writes=["lnmb"], tag=f"rex{i}_{c}")
        add("V", lambda e, cc=c: e.tensor_scalar(out=S["Btm"][:],
                                                 in0=S["yg"][:, cc, :],
                                                 scalar1=S["lnmb"][:, 0:1],
                                                 scalar2=None, op0=ALU.mult),
            reads=["yg", "lnmb", "Btm"], writes=["Btm"], tag=f"ynm{i}_{c}")
        for rnd in range(2):
            for k in range(8):
                fb = rnd * 8 + k
                add("T", lambda e, f=fb, kk=k: e.transpose(
                        out=PS["pt"][:, kk * 128:(kk + 1) * 128],
                        in_=S["Btm"][:, f * 128:(f + 1) * 128],
                        identity=S["id_f16"][:]),
                    reads=["Btm", "id_f16", "pt"], writes=["pt"],
                    tag=f"ynt{i}_{c}_{fb}")
            add("S", lambda e, r=rnd, cc=c: e.copy(
                    out=S["bufB"][:, r * 8:(r + 1) * 8, cc * 128:(cc + 1) * 128],
                    in_=PS["pt"][:].rearrange("p (a b) -> p a b", a=8)),
                reads=["pt", "bufB"], writes=["bufB", "yn"], tag=f"ynev{i}_{c}_{rnd}")
    X.tap(f"yn{i}", lambda: S["bufB"][:].rearrange("p c n -> p (c n)"),
          ["bufB", "yn"])
    for hf in range(2):
        add("QS", lambda e, h=hf: e.dma_start(
                out=S["Wbuf"][:, h * 8192:(h + 1) * 8192],
                in_=D["scr_out"][:, h * 8192:(h + 1) * 8192]),
            writes=["WbufA" if hf == 0 else "WbufB"], tag=f"ldwo{i}_{hf}")
    for tc in range(2):
        for ng in range(2):
            pm = "pmm0" if ng == 0 else "pmm1"
            for kc in range(16):
                add("T", lambda e, t=tc, n=ng, k=kc, p=pm: e.matmul(
                        out=PS[p][:, 0:512],
                        lhsT=S["bufB"][:, k, t * 128:(t + 1) * 128],
                        rhs=S["Wbuf"][:, k * 1024 + n * 512:k * 1024 + n * 512 + 512],
                        start=(k == 0), stop=(k == 15)),
                    reads=["WbufA" if kc < 8 else "WbufB", "bufB", "yn", pm],
                    writes=[pm], tag=f"op{i}_{tc}_{ng}_{kc}")
            add("V", lambda e, t=tc, n=ng, p=pm: e.tensor_tensor(
                    out=S["sgb"][:, n * 512:(n + 1) * 512], in0=PS[p][:],
                    in1=S["gbuf"][:, t, n * 512:(n + 1) * 512], op=ALU.mult),
                reads=[pm, "gbuf", "sgb"], writes=["sgb"], tag=f"sg{i}_{tc}_{ng}")
        add("V", lambda e, t=tc: e.tensor_tensor(
                out=S["outb"][:], in0=S["sgb"][:],
                in1=S["xt"][:, par, t, :], op=ALU.add),
            reads=["sgb", f"xtP{par}", "outb"], writes=["outb"], tag=f"ob{i}_{tc}")
        add("QS", lambda e, t=tc: e.dma_start(
                out=D["out"][i * 256 + t * 128:i * 256 + t * 128 + 128, :],
                in_=S["outb"][:]),
            reads=["outb"], writes=[f"outdma{i}" if tc == 1 else f"outdmaa{i}"],
            tag=f"st_out{i}_{tc}")
    X.tap(f"s{i}", lambda: S["sgb"][:], ["sgb"])



def kernel(**inputs):
    in_maps = setup_core_inputs(inputs)
    nc = build_nc()
    res = run_bass_kernel_spmd(nc, in_maps, core_ids=list(range(N_CORES)))
    out = np.concatenate(
        [np.asarray(r["out"]).reshape(2, 2048, 1024) for r in res.results], axis=0)
    return np.ascontiguousarray(out.reshape(16, 2048, 1024)).astype(np.float32)


# version 05
# speedup vs baseline: 1.0976x; 1.0052x over previous
import contextlib
import math

import numpy as np
import ml_dtypes

import concourse.bass as bass
import concourse.mybir as mybir
from concourse.bass_utils import run_bass_kernel_spmd

F32 = mybir.dt.float32
BF16 = mybir.dt.bfloat16
FP16 = mybir.dt.float16
FP8 = mybir.dt.float8e4
AF = mybir.ActivationFunctionType
ALU = mybir.AluOpType

N_CORES = 8
SEQ = 2048
T = 256
Q = 128
N_TILES = 16
TILES_PER_SEQ = SEQ // T
NCH = T // Q

MAGIC = 12582912.0
TWO_PI = float(2.0 * math.pi)
INV_2PI = float(1.0 / TWO_PI)
LN_EPS = 1e-5
RMS_EPS = 1e-6

STREAM_OF = {"T": "tensor", "V": "vector", "S": "scalar", "G": "gpsimd",
             "QS": "sync", "QG": "gpsimd", "QA": "scalar"}
DMA_ENGS = ("QS", "QG", "QA")


class Op:
    __slots__ = ("eng", "emit", "deps", "idx", "eidx", "needed", "inc_ord",
                 "tag", "lagged")

    def __init__(self, eng, emit, deps, idx, eidx, tag, lagged):
        self.eng, self.emit, self.deps = eng, emit, deps
        self.idx, self.eidx = idx, eidx
        self.needed, self.inc_ord, self.tag = False, 0, tag
        self.lagged = lagged


class Sched:
    def __init__(self):
        self.ops = []
        self.eng_ops = {e: [] for e in STREAM_OF}
        self.res = {}

    def add(self, eng, emit, reads=(), writes=(), tag="", lagged=False):
        deps = set()
        for t in reads:
            e = self.res.get(t)
            if e and e["w"] is not None:
                deps.add(e["w"])
        for t in writes:
            e = self.res.get(t)
            if e:
                if e["w"] is not None:
                    deps.add(e["w"])
                deps.update(e["r"])
        idx = len(self.ops)
        op = Op(eng, emit, deps, idx, len(self.eng_ops[eng]), tag, lagged)
        self.ops.append(op)
        self.eng_ops[eng].append(op)
        for t in reads:
            self.res.setdefault(t, {"w": None, "r": []})["r"].append(idx)
        for t in writes:
            self.res[t] = {"w": idx, "r": []}
        return idx

    def finalize(self):
        SAFE_DIST = 12
        for op in self.ops:
            best = {}
            for d in op.deps:
                dop = self.ops[d]
                same_stream = STREAM_OF[dop.eng] == STREAM_OF[op.eng]
                if same_stream and not dop.lagged:
                    if dop.eng in DMA_ENGS and op.eng in DMA_ENGS:
                        continue
                    if dop.eng == "T" and op.eng == "T":
                        continue
                    if (dop.eng == op.eng and dop.eng not in DMA_ENGS
                            and op.eidx - dop.eidx >= SAFE_DIST):
                        continue
                    if dop.eng not in DMA_ENGS and op.eng in DMA_ENGS:
                        pass
                e = dop.eng
                if e not in best or self.ops[best[e]].eidx < dop.eidx:
                    best[e] = d
            op.deps = set(best.values())
            for d in op.deps:
                self.ops[d].needed = True
        for e, ops in self.eng_ops.items():
            k = 0
            for op in ops:
                if e in DMA_ENGS or op.needed:
                    k += 1
                    op.inc_ord = k

    @staticmethod
    def _dma_sem(eng, ord_):
        if eng == "QS":
            return f"QS{(ord_ - 1) % 4}", 16 * ((ord_ - 1) // 4 + 1)
        return eng, 16 * ord_

    def emit_stream(self, stream, engobj, sems):
        hw = {}
        for op in self.ops:
            if STREAM_OF[op.eng] != stream:
                continue
            for d in sorted(op.deps, key=lambda i: self.ops[i].eng):
                dop = self.ops[d]
                if dop.eng in DMA_ENGS:
                    key, val = self._dma_sem(dop.eng, dop.inc_ord)
                else:
                    key, val = dop.eng, dop.inc_ord
                if hw.get(key, -1) >= val:
                    continue
                hw[key] = val
                engobj.wait_ge(sems[key], val)
            if op.eng in DMA_ENGS:
                if op.eng == "QS":
                    if op.inc_ord > 4:
                        key, val = self._dma_sem("QS", op.inc_ord - 4)
                    else:
                        key, val = None, 0
                elif op.inc_ord > 1:
                    key, val = op.eng, 16 * (op.inc_ord - 1)
                else:
                    key, val = None, 0
                if key is not None and hw.get(key, -1) < val:
                    hw[key] = val
                    engobj.wait_ge(sems[key], val)
            inst = op.emit(engobj)
            if op.inc_ord:
                assert inst is not None, f"op {op.tag} must return its instruction"
                if op.eng in DMA_ENGS:
                    key, _ = self._dma_sem(op.eng, op.inc_ord)
                    inst.then_inc(sems[key], 16)
                else:
                    inst.then_inc(sems[op.eng], 1)



def _perm_bc():
    p = np.empty(2048, dtype=np.int64)
    r = np.arange(1024)
    pc, m, i = r // 128, (r % 128) // 32, r % 32
    h = m * 8 + pc
    p[:1024] = h * 64 + 2 * i
    p[1024:] = h * 64 + 2 * i + 1
    return p


def _consts():
    c = {}
    ident = np.eye(128, dtype=np.float32)
    c["ident_bf"] = ident.astype(ml_dtypes.bfloat16)
    c["ident_f16"] = ident.astype(np.float16)
    c["ident_f32_32"] = np.eye(32, dtype=np.float32)
    tri = np.where(np.arange(128)[None, :] >= np.arange(128)[:, None],
                   0.0, -1e30).astype(np.float32)
    c["tri_rep4"] = np.tile(tri, (1, 4)).astype(ml_dtypes.bfloat16)
    c["anti_ident"] = (1.0 - np.eye(128, dtype=np.float32)).astype(np.float16)
    invfreq = (1.0 / (10000.0 ** (np.arange(32) / 32.0))).astype(np.float32)
    selth = np.zeros((32, 1024), dtype=np.float32)
    seldec = np.zeros((32, 1024), dtype=np.float32)
    rr = np.arange(1024)
    hh_of_r = ((rr % 128) // 32) * 8 + rr // 128
    selth[hh_of_r, rr] = invfreq[rr % 32]
    seldec[hh_of_r, rr] = 1.0
    c["selth"] = selth
    c["seldec"] = seldec
    seldch = np.zeros((32, 128), dtype=np.float32)
    for h in range(32):
        seldch[h, 32 * (h // 8):32 * (h // 8) + 32] = 1.0
    c["seldch"] = seldch
    i2d = np.zeros((32, 8), dtype=np.float32)
    i2d[np.arange(32), np.arange(32) % 8] = 1.0
    c["i2dch"] = i2d
    c["ones2_bf"] = np.ones((128, 128), dtype=ml_dtypes.bfloat16)
    c["ones1_f32"] = np.ones((1, 128), dtype=np.float32)
    c["onesdt"] = np.ones((32, 256), dtype=np.float32)
    return c


def setup_core_inputs(inputs):
    x = np.ascontiguousarray(
        np.asarray(inputs["L0_attn_hidden"]).reshape(16, 2048, 1024))
    W_in = np.asarray(inputs["L0_attn_W_in"])
    perm = _perm_bc()
    consts = _consts()
    shared = dict(
        Wz=np.ascontiguousarray(W_in[:, 0:2048]),
        Wxr=np.ascontiguousarray(W_in[:, 2048:4096]),
        WB=np.ascontiguousarray(W_in[:, 4096:6144][:, perm]),
        WC=np.ascontiguousarray(W_in[:, 6144:8192][:, perm]),
        Wdt=np.ascontiguousarray(
            np.concatenate([W_in[:, 8192:8224],
                            np.zeros((1024, 96), dtype=W_in.dtype)], axis=1)
            .reshape(8, 128, 128).transpose(1, 0, 2)),
        Wout=np.ascontiguousarray(np.asarray(inputs["L0_attn_W_out"])),
        ffnk=np.ascontiguousarray(
            np.asarray(inputs["L0_ffn_kernel"]).reshape(8, 128, 1024)
            .transpose(1, 0, 2)),
        ffnb_rep=np.ascontiguousarray(np.broadcast_to(
            np.asarray(inputs["L0_ffn_bias"])[None, :], (128, 1024))),
        s_in_col=np.ascontiguousarray(np.broadcast_to(
            np.asarray(inputs["L0_ffn_input_scale"])[:, None], (128, 1))),
        s_k_col=np.ascontiguousarray(np.broadcast_to(
            np.asarray(inputs["L0_ffn_kernel_scale"])[:, None], (128, 1))),
        dtb_col=np.ascontiguousarray(
            np.asarray(inputs["L0_attn_dt_bias"])[:, None]),
        alog_col=np.ascontiguousarray(
            np.asarray(inputs["L0_attn_A_log"])[:, None]),
        nw_fm=np.ascontiguousarray(
            np.asarray(inputs["L0_attn_norm_weight"]).reshape(16, 128).T),
        **consts,
    )
    return [dict(shared, x=np.ascontiguousarray(
        x[2 * ci:2 * ci + 2].reshape(4096, 1024))) for ci in range(N_CORES)]



class Ctx:
    pass


def build_nc(taps=(), n_tiles=N_TILES):
    nc = bass.Bass("TRN2", target_bir_lowering=False)
    P = Sched()
    X = Ctx()
    X.nc, X.P, X.n_tiles = nc, P, n_tiles
    X.taps = {}

    D = {}
    def din(name, shape, dt):
        D[name] = nc.dram_tensor(name, list(shape), dt, kind="ExternalInput").ap()
    def dout(name, shape, dt):
        D[name] = nc.dram_tensor(name, list(shape), dt, kind="ExternalOutput").ap()
    def dscr(name, shape, dt):
        D[name] = nc.dram_tensor(name, list(shape), dt, kind="Internal").ap()

    din("x", (4096, 1024), F32)
    for w in ("Wz", "Wxr", "WB", "WC"):
        din(w, (1024, 2048), F32)
    din("Wdt", (128, 8, 128), F32)
    din("Wout", (2048, 1024), F32)
    din("ffnk", (128, 8, 1024), BF16)
    din("ffnb_rep", (128, 1024), BF16)
    din("s_in_col", (128, 1), F32)
    din("s_k_col", (128, 1), F32)
    din("dtb_col", (32, 1), F32)
    din("alog_col", (32, 1), F32)
    din("nw_fm", (128, 16), F32)
    din("ident_bf", (128, 128), BF16)
    din("ident_f16", (128, 128), FP16)
    din("ident_f32_32", (32, 32), F32)
    din("tri_rep4", (128, 512), BF16)
    din("anti_ident", (128, 128), FP16)
    din("selth", (32, 1024), F32)
    din("seldec", (32, 1024), F32)
    din("seldch", (32, 128), F32)
    din("i2dch", (32, 8), F32)
    din("ones2_bf", (128, 128), BF16)
    din("ones1_f32", (1, 128), F32)
    din("onesdt", (32, 256), F32)
    dout("out", (4096, 1024), F32)
    for name, shape, dt in taps:
        dout("tap_" + name, list(shape), dt)
        X.taps[name] = D["tap_" + name]
    for s in ("scr_z", "scr_B", "scr_C", "scr_xr", "scr_out"):
        dscr(s, (128, 16384), BF16)

    es = contextlib.ExitStack()
    S = {}
    def asb(name, shape, dt):
        S[name] = es.enter_context(nc.sbuf_tensor(name, list(shape), dt))
    PS = {}
    def aps(name, shape, dt):
        PS[name] = es.enter_context(nc.psum_tensor(name, list(shape), dt))

    asb("Wdt_sb", (128, 8, 128), BF16)
    asb("ffn_q", (128, 8, 1024), FP8)
    asb("ffnb", (128, 1024), BF16)
    asb("H", (128, 2048), F32)
    asb("Hb", (128, 2048), FP16)
    asb("id_bf", (128, 128), BF16)
    asb("id_f16", (128, 128), FP16)
    asb("id32", (32, 32), F32)
    asb("tri4", (128, 512), BF16)
    asb("antiI", (128, 128), FP16)
    asb("selth_sb", (32, 1024), F32)
    asb("seldec_sb", (32, 1024), F32)
    asb("seldch_sb", (32, 128), F32)
    asb("i2dch_sb", (32, 8), F32)
    asb("ones2", (128, 128), BF16)
    asb("ones1f", (1, 128), F32)
    asb("onesdt_sb", (32, 256), F32)
    asb("nw_sb", (128, 16), F32)
    asb("scals", (128, 8), F32)
    asb("sc32", (32, 8), F32)
    asb("Wbuf", (128, 16384), BF16)
    asb("xt", (128, 2, 2, 1024), F32)
    asb("xn_tm", (128, 1024), BF16)
    asb("xn_fm", (128, 2, 8, 256), BF16)
    asb("xn_q", (128, 8, 256), FP8)
    asb("dts", (32, 2, 257), F32)
    asb("Lc", (32, 257), F32)
    asb("ldt", (32, 257), F32)
    asb("zs", (128, 2, 2048), FP16)
    asb("yg", (128, 2, 2048), FP16)
    asb("bufA", (128, 16, 256), FP16)
    asb("bufB", (128, 16, 256), FP16)
    asb("bufC", (128, 16, 256), FP16)
    asb("xr", (128, 2, 2048), FP16)
    asb("cth", (128, 8, 256), FP16)
    asb("sth", (128, 8, 256), FP16)
    asb("Btm", (128, 2048), FP16)
    asb("X2", (128, 2048), FP16)
    asb("e1b", (128, 1024), F32)
    asb("scr4k", (128, 1024), F32)
    asb("M1b", (128, 1024), FP16)
    asb("M2b", (128, 1024), FP16)
    asb("Sm1", (128, 1024), FP16)
    asb("Sm2", (128, 1024), FP16)
    asb("decr", (128, 8, 128), FP16)
    asb("Cdec", (128, 2, 8, 128), FP16)
    asb("Pq", (32, 128), F32)
    asb("Psplit", (64, 128), BF16)
    asb("Pflat", (128, 1024), BF16)
    asb("Mt", (128, 32), F32)
    asb("Mcol1", (128, 32), F32)
    asb("l1t", (128, 32), F32)
    asb("w1t", (128, 32), F32)
    asb("w2t", (128, 32), F32)
    asb("ratio", (128, 32), F32)
    asb("exS", (128, 32), F32)
    asb("ccol", (128, 32), F32)
    asb("rhsd", (32, 8), F32)
    asb("gbuf", (128, 2, 1024), BF16)
    asb("sgb", (128, 1024), F32)
    asb("outb", (128, 1024), F32)
    asb("lnstat", (128, 12), F32)
    asb("lnmv", (128, 2), F32)
    asb("lnsd", (128, 1), F32)
    asb("lnrs", (128, 1), F32)
    asb("lnmb", (128, 1), F32)
    asb("dtmp", (32, 256), F32)
    asb("dtmp2", (32, 256), F32)
    asb("dcol", (32, 2), F32)

    aps("pmm0", (128, 512), F32)
    aps("pmm1", (128, 512), F32)
    aps("pe", (128, 1024), F32)
    aps("psc", (128, 512), F32)
    aps("psc2", (128, 512), F32)
    aps("py", (128, 512), F32)
    aps("pt", (128, 1024), FP16)

    X.D, X.S, X.PS = D, S, PS
    _build(X)
    P.finalize()

    sems = {e: es.enter_context(nc.semaphore("sem_" + e)) for e in STREAM_OF}
    for j in range(4):
        sems[f"QS{j}"] = es.enter_context(nc.semaphore(f"sem_QS{j}"))
    with es:
        blk = es.enter_context(nc.Block())
        @blk.sync
        def _(eng): P.emit_stream("sync", eng, sems)
        @blk.tensor
        def _(eng): P.emit_stream("tensor", eng, sems)
        @blk.vector
        def _(eng): P.emit_stream("vector", eng, sems)
        @blk.scalar
        def _(eng): P.emit_stream("scalar", eng, sems)
        @blk.gpsimd
        def _(eng): P.emit_stream("gpsimd", eng, sems)
    return nc



def _build(X):
    D, S, PS, add = X.D, X.S, X.PS, X.P.add

    def tap(name, ap_fn, reads):
        if name in X.taps:
            dst = X.taps[name]
            add("QG", lambda e: e.dma_start(out=dst, in_=ap_fn()),
                reads=list(reads), writes=["tapw_" + name], tag="tap_" + name)
    X.tap = tap

    for nm, dst in [("ident_bf", "id_bf"), ("ident_f16", "id_f16"),
                    ("ident_f32_32", "id32"), ("tri_rep4", "tri4"),
                    ("anti_ident", "antiI"), ("selth", "selth_sb"),
                    ("seldec", "seldec_sb"), ("seldch", "seldch_sb"),
                    ("i2dch", "i2dch_sb"), ("ones2_bf", "ones2"),
                    ("ones1_f32", "ones1f"), ("onesdt", "onesdt_sb"),
                    ("nw_fm", "nw_sb"), ("ffnb_rep", "ffnb")]:
        add("QS", lambda e, a=D[nm], b=S[dst]: e.dma_start(out=b[:], in_=a),
            writes=[dst], tag="ld_" + nm)
    add("QG", lambda e: e.dma_start(out=S["Wdt_sb"][:], in_=D["Wdt"]),
        writes=["Wdt_sb"], tag="ld_Wdt")
    add("QS", lambda e: e.dma_start(out=S["scals"][:, 0:1], in_=D["s_in_col"]),
        writes=["sin0"], tag="ld_sin")
    add("QS", lambda e: e.dma_start(out=S["scals"][:, 1:2], in_=D["s_k_col"]),
        writes=["sk0"], tag="ld_sk")
    add("QS", lambda e: e.dma_start(out=S["sc32"][:, 0:1], in_=D["dtb_col"]),
        writes=["dtb"], tag="ld_dtb")
    add("QS", lambda e: e.dma_start(out=S["sc32"][:, 1:2], in_=D["alog_col"]),
        writes=["alog"], tag="ld_alog")
    add("V", lambda e: e.memset(S["scals"][:, 5:6], LN_EPS), writes=["epsln"], tag="epsln")
    add("V", lambda e: e.memset(S["scals"][:, 6:7], RMS_EPS), writes=["epsrms"], tag="epsrms")
    add("V", lambda e: e.memset(S["scals"][:, 7:8], math.pi / 2), writes=["pih"], tag="pih")
    add("V", lambda e: e.reciprocal(out=S["scals"][:, 2:3], in_=S["scals"][:, 0:1]),
        reads=["sin0"], writes=["rsin"], tag="rsin")
    add("V", lambda e: e.reciprocal(out=S["scals"][:, 3:4], in_=S["scals"][:, 1:2]),
        reads=["sk0"], writes=["rsk"], tag="rsk")
    add("V", lambda e: e.tensor_tensor(out=S["scals"][:, 4:5], in0=S["scals"][:, 0:1],
                                       in1=S["scals"][:, 1:2], op=ALU.mult),
        reads=["sin0", "sk0"], writes=["sprod"], tag="sprod")
    add("S", lambda e: e.activation(out=S["sc32"][:, 3:4], in_=S["sc32"][:, 1:2],
                                    func=AF.Exp),
        reads=["alog"], writes=["aexp"], tag="aexp")
    add("V", lambda e: e.tensor_scalar(out=S["sc32"][:, 2:3], in0=S["sc32"][:, 3:4],
                                       scalar1=-1.0, scalar2=None, op0=ALU.mult),
        reads=["aexp"], writes=["A"], tag="A")
    add("QS", lambda e: e.dma_start(
            out=S["Wbuf"][:, 0:8192].rearrange("p (c n) -> p c n", c=8),
            in_=D["ffnk"]),
        writes=["WbufA"], tag="ld_ffnk")
    add("V", lambda e: e.tensor_scalar(
            out=S["ffn_q"][:].rearrange("p c n -> p (c n)"),
            in0=S["Wbuf"][:, 0:8192], scalar1=S["scals"][:, 3:4],
            scalar2=None, op0=ALU.mult),
        reads=["WbufA", "rsk"], writes=["ffn_q"], tag="q_ffnk")
    def _nw_fold():
        for kc2 in range(16):
            for hf in range(2):
                sl = slice(kc2 * 1024 + hf * 512, kc2 * 1024 + hf * 512 + 512)
                add("V", lambda e, s=sl, k=kc2: e.tensor_scalar(
                        out=S["xn_tm"][:, 0:512],
                        in0=S["Wbuf"][:, s], scalar1=S["nw_sb"][:, k:k + 1],
                        scalar2=None, op0=ALU.mult),
                    reads=["WbufA", "WbufB", "nw_sb", "xn_tm"], writes=["xn_tm"],
                    tag=f"nwf{kc2}_{hf}")
                add("V", lambda e, s=sl: e.tensor_copy(
                        out=S["Wbuf"][:, s], in_=S["xn_tm"][:, 0:512]),
                    reads=["xn_tm", "WbufA", "WbufB"],
                    writes=["WbufA" if kc2 < 8 else "WbufB"], tag=f"nwc{kc2}_{hf}")

    import os as _os2
    wlist = [] if _os2.environ.get("KLNONLY") else [
        ("Wz", "scr_z", 8), ("WB", "scr_B", 8),
        ("WC", "scr_C", 8), ("Wxr", "scr_xr", 8),
        ("Wout", "scr_out", 16)]
    for wsrc, wscr, kc in wlist:
        add("QG", lambda e, a=D[wsrc], kk=kc: e.dma_start(
                out=S["Wbuf"][:].rearrange("p (c n) -> p c n", c=kk),
                in_=a.rearrange("(c p) n -> p c n", p=128)),
            reads=["ffn_q"], writes=["WbufA", "WbufB"], tag="cast_" + wsrc)
        if wsrc == "Wout":
            _nw_fold()
        if kc == 8:
            for hf2 in range(2):
                add("QG", lambda e, b=D[wscr], h=hf2: e.dma_start(
                        out=b[:, h * 8192:(h + 1) * 8192],
                        in_=S["Wbuf"][:].rearrange("p (c n) -> p c n", c=8)
                        [:, :, h * 1024:(h + 1) * 1024]),
                    reads=["WbufA", "WbufB"], writes=[wscr], tag=f"st_{wscr}{hf2}")
        else:
            add("QG", lambda e, b=D[wscr]: e.dma_start(out=b, in_=S["Wbuf"][:]),
                reads=["WbufA", "WbufB"], writes=[wscr], tag="st_" + wscr)

    import os as _os
    if _os.environ.get("KLNONLY"):
        _ln_dt(X, 0)
        return
    for i in range(X.n_tiles):
        _tile(X, i)

    import os
    if int(os.environ.get("KPHASES", "99")) >= 6:
        add("G", lambda e: e.nop(),
            reads=[f"outdma{i}" for i in range(X.n_tiles)], tag="final")


def _ln_dt(X, j):
    S, PS, D, add = X.S, X.PS, X.D, X.P.add
    pj = j % 2
    xP = f"xtP{pj}"
    fmP = f"xnfmP{pj}"
    dtP = f"dtP{pj}"
    xtj = S["xt"][:, pj]
    add("QS", lambda e: e.dma_start(
            out=xtj, in_=D["x"][j * 256:(j + 1) * 256, :]
            .rearrange("(tc p) n -> p tc n", p=128)),
        writes=[xP], tag=f"ldx{j}")
    X.tap(f"xt{j}", lambda: xtj.rearrange("p a n -> p (a n)"), [xP])
    for tc in range(2):
        xs = xtj[:, tc, :]
        add("V", lambda e, a=xs: e.tensor_reduce(out=S["lnmv"][:, 0:1], in_=a,
                                                 axis=mybir.AxisListType.X,
                                                 op=ALU.add),
            reads=[xP, "lnmv"], writes=["lnmv"], tag=f"lsum_{j}_{tc}", lagged=True)
        add("S", lambda e, a=xs: e.activation(out=S["xn_tm"][:], in_=a,
                                              func=AF.Square),
            reads=[xP, "xn_tm"], writes=["xn_tm"], tag=f"lsq_{j}_{tc}")
        add("V", lambda e: e.tensor_reduce(out=S["lnsd"][:, 0:1],
                                           in_=S["xn_tm"][:].bitcast(BF16),
                                           axis=mybir.AxisListType.X, op=ALU.add),
            reads=["xn_tm", "lnsd"], writes=["lnsd"], tag=f"lsqr_{j}_{tc}",
            lagged=True)
        add("V", lambda e: e.tensor_scalar(out=S["lnstat"][:, 0:1],
                                           in0=S["lnmv"][:, 0:1],
                                           scalar1=1.0 / 1024, scalar2=None,
                                           op0=ALU.mult),
            reads=["lnmv", "lnstat"], writes=["lnstat"], tag=f"lmean_{j}_{tc}")
        add("V", lambda e: e.tensor_scalar(out=S["lnstat"][:, 1:2],
                                           in0=S["lnsd"][:, 0:1],
                                           scalar1=1.0 / 1024, scalar2=None,
                                           op0=ALU.mult),
            reads=["lnsd", "lnstat"], writes=["lnstat"], tag=f"lms_{j}_{tc}")
        add("V", lambda e: e.tensor_tensor(out=S["lnstat"][:, 2:3],
                                           in0=S["lnstat"][:, 0:1],
                                           in1=S["lnstat"][:, 0:1],
                                           op=ALU.mult),
            reads=["lnstat"], writes=["lnstat"], tag=f"lm2_{j}_{tc}")
        add("V", lambda e: e.tensor_tensor(out=S["lnmv"][:, 1:2],
                                           in0=S["lnstat"][:, 1:2],
                                           in1=S["lnstat"][:, 2:3],
                                           op=ALU.subtract),
            reads=["lnstat", "lnmv"], writes=["lnmv"], tag=f"lvar_{j}_{tc}")
        add("S", lambda e: e.activation(out=S["lnsd"][:], in_=S["lnmv"][:, 1:2],
                                        func=AF.Sqrt, bias=S["scals"][:, 5:6]),
            reads=["lnmv", "epsln", "lnsd"], writes=["lnsd"], tag=f"lnsq_{j}_{tc}")
        add("V", lambda e: e.reciprocal(out=S["lnrs"][:], in_=S["lnsd"][:]),
            reads=["lnsd", "lnrs"], writes=["lnrs"], tag=f"lnrs_{j}_{tc}")
        if tc == 0:
            X.tap(f"lnstat{j}", lambda: S["lnstat"][:, :], ["lnstat"])
            X.tap(f"lnsum{j}", lambda: S["lnmv"][:, :], ["lnmv"])
            X.tap(f"lnsq{j}", lambda: S["lnsd"][:, :], ["lnsd"])
        add("V", lambda e: e.tensor_tensor(out=S["lnmb"][:], in0=S["lnstat"][:, 0:1],
                                           in1=S["lnrs"][:], op=ALU.mult),
            reads=["lnstat", "lnrs", "lnmb"], writes=["lnmb"], tag=f"lnmb_{j}_{tc}")
        add("V", lambda e, a=xs: e.tensor_scalar(
                out=S["xn_tm"][:], in0=a, scalar1=S["lnrs"][:, 0:1],
                scalar2=S["lnmb"][:, 0:1], op0=ALU.mult, op1=ALU.subtract),
            reads=[xP, "lnrs", "lnmb"], writes=["xn_tm"], tag=f"lnap_{j}_{tc}")
        for kc in range(8):
            add("T", lambda e, k=kc: e.transpose(
                    out=PS["pt"][:, k * 128:(k + 1) * 128].bitcast(BF16),
                    in_=S["xn_tm"][:, k * 128:(k + 1) * 128],
                    identity=S["id_bf"][:]),
                reads=["xn_tm", "id_bf"], writes=["pt"], tag=f"xnt_{j}_{tc}_{kc}")
        add("S", lambda e, t=tc: e.copy(
                out=S["xn_fm"][:, pj, :, t * 128:(t + 1) * 128],
                in_=PS["pt"][:].bitcast(BF16).rearrange("p (c n) -> p c n", c=8)),
            reads=["pt"], writes=[fmP], tag=f"xnev_{j}_{tc}")
    for kc in range(8):
        add("T", lambda e, k=kc: e.matmul(
                out=PS["pmm0"][:, 0:256], lhsT=S["Wdt_sb"][:, k, :],
                rhs=S["xn_fm"][:, pj, k, :], start=(k == 0), stop=(k == 7)),
            reads=[fmP, "Wdt_sb"], writes=["pmm0"], tag=f"dtmm_{j}_{kc}")
    add("S", lambda e: e.activation(out=S["dtmp"][:], in_=PS["pmm0"][0:32, 0:256],
                                    func=AF.Abs, bias=S["sc32"][:, 0:1]),
        reads=["pmm0", "dtb"], writes=["dtmp"], tag=f"dta_{j}")
    add("S", lambda e: e.activation(out=S["dtmp2"][:], in_=S["dtmp"][:],
                                    func=AF.Exp, scale=-1.0),
        reads=["dtmp"], writes=["dtmp2"], tag=f"dte_{j}")
    add("V", lambda e: e.tensor_scalar(out=S["dtmp"][:], in0=S["dtmp2"][:],
                                       scalar1=1.0, scalar2=None, op0=ALU.add),
        reads=["dtmp2"], writes=["dtmp"], tag=f"dtp1_{j}")
    add("S", lambda e: e.activation(out=S["dtmp2"][:], in_=S["dtmp"][:], func=AF.Ln),
        reads=["dtmp"], writes=["dtmp2"], tag=f"dtl_{j}")
    add("V", lambda e: e.tensor_scalar(out=S["dtmp"][:], in0=PS["pmm0"][0:32, 0:256],
                                       scalar1=S["sc32"][:, 0:1], scalar2=0.0,
                                       op0=ALU.add, op1=ALU.max),
        reads=["pmm0", "dtb"], writes=["dtmp"], tag=f"dtr_{j}")
    add("V", lambda e: e.tensor_tensor(out=S["dts"][:, pj, 0:256], in0=S["dtmp"][:],
                                       in1=S["dtmp2"][:], op=ALU.add),
        reads=["dtmp", "dtmp2"], writes=[dtP], tag=f"dt_{j}")
    X.tap(f"dt{j}", lambda pj_=pj: S["dts"][:, pj_, :], [dtP])
    X.tap(f"xnfm{j}", lambda pj_=pj: S["xn_fm"][:, pj_].rearrange("p c n -> p (c n)"),
          [fmP])


def _tile(X, i):
    S, add = X.S, X.P.add
    par = i % 2
    seq_start = (i % TILES_PER_SEQ) == 0

    if i == 0:
        _ln_dt(X, 0)
    if i + 1 < X.n_tiles:
        _ln_dt(X, i + 1)
        add("V", lambda e: e.tensor_copy(out=S["dts"][:, par, 256:257],
                                         in_=S["dts"][:, 1 - par, 0:1]),
            reads=[f"dtP{1 - par}", f"dtP{par}"], writes=[f"dtP{par}"],
            tag=f"dtx{i}")
    else:
        add("V", lambda e: e.memset(S["dts"][:, par, 256:257], 1.0),
            reads=[f"dtP{par}"], writes=[f"dtP{par}"], tag=f"dtx{i}")

    add("S", lambda e: e.activation(out=S["ldt"][:], in_=S["dts"][:, par],
                                    func=AF.Ln, scale=0.5),
        reads=[f"dtP{par}"], writes=["ldt"], tag=f"ldt{i}")
    if seq_start:
        add("V", lambda e: e.memset(S["Lc"][:, 0:1], 0.0),
            reads=["Lc"], writes=["Lc"], tag=f"Lz{i}")
    else:
        add("V", lambda e: e.tensor_copy(out=S["Lc"][:, 0:1], in_=S["Lc"][:, 256:257]),
            reads=["Lc"], writes=["Lc"], tag=f"Lcp{i}")
    add("V", lambda e: e.tensor_tensor_scan(
            out=S["Lc"][:, 1:257], data0=S["onesdt_sb"][:],
            data1=S["dts"][:, par, 0:256], initial=S["Lc"][:, 0:1],
            op0=ALU.mult, op1=ALU.add),
        reads=[f"dtP{par}", "onesdt_sb", "Lc"], writes=["Lc"], tag=f"Lsc{i}")
    X.tap(f"L{i}", lambda: S["Lc"][:, :], ["Lc"])

    import os
    ph = int(os.environ.get("KPHASES", "99"))
    if ph >= 1:
        _ffn(X, i, par)
    if ph >= 2:
        _inproj(X, i, par)
    if ph >= 3:
        _angles(X, i, par)
    if ph >= 4:
        _rope(X, i, par)
    if ph >= 5:
        for c in range(NCH):
            _chunk(X, i, c, par, seq_start and c == 0)
    if ph >= 6:
        _epilogue(X, i, par)


def _ffn(X, i, par):
    S, PS, add = X.S, X.PS, X.P.add
    fmP = f"xnfmP{par}"
    add("V", lambda e: e.tensor_scalar(
            out=S["xn_q"][:].rearrange("p c n -> p (c n)"),
            in0=S["xn_fm"][:, par].rearrange("p c n -> p (c n)"),
            scalar1=S["scals"][:, 2:3], scalar2=None, op0=ALU.mult),
        reads=[fmP, "rsin"], writes=["xn_q"], tag=f"xq{i}")
    X.tap(f"xnq{i}", lambda: S["xn_q"][:].rearrange("p c n -> p (c n)"), ["xn_q"])
    for tc in range(2):
        for ng in range(2):
            pm = "pmm0" if ng == 0 else "pmm1"
            for kc in range(8):
                add("T", lambda e, k=kc, t=tc, n=ng, p=pm: e.matmul(
                        out=PS[p][:, 0:512],
                        lhsT=S["xn_q"][:, k, t * 128:(t + 1) * 128],
                        rhs=S["ffn_q"][:, k, n * 512:(n + 1) * 512],
                        start=(k == 0), stop=(k == 7)),
                    reads=["xn_q", "ffn_q"], writes=[pm], tag=f"ffmm{i}_{tc}_{ng}_{kc}")
            add("S", lambda e, p=pm: e.activation(
                    out=S["xn_tm"][:, 0:512].bitcast(BF16), in_=PS[p][:, 0:512],
                    func=AF.Copy, scale=S["scals"][:, 4:5]),
                reads=[pm, "sprod", "xn_tm"], writes=["xn_tm"], tag=f"ffc{i}_{tc}_{ng}")
            add("V", lambda e, n=ng: e.tensor_tensor(
                    out=S["xn_tm"][:, 512:1024].bitcast(BF16),
                    in0=S["xn_tm"][:, 0:512].bitcast(BF16),
                    in1=S["ffnb"][:, n * 512:(n + 1) * 512], op=ALU.add),
                reads=["xn_tm", "ffnb"], writes=["xn_tm"], tag=f"ffv{i}_{tc}_{ng}")
            add("S", lambda e, t=tc, n=ng: e.activation(
                    out=S["gbuf"][:, t, n * 512:(n + 1) * 512],
                    in_=S["xn_tm"][:, 512:1024].bitcast(BF16), func=AF.Silu),
                reads=["xn_tm"], writes=["gbuf"], tag=f"ffs{i}_{tc}_{ng}")
    X.tap(f"g{i}", lambda: S["gbuf"][:].rearrange("p c n -> p (c n)"), ["gbuf"])


def _inproj(X, i, par):
    S, PS, D, add = X.S, X.PS, X.D, X.P.add
    fmP = f"xnfmP{par}"
    for scr, dstb, kind in [("scr_B", "bufA", "B"), ("scr_C", "bufB", "C")]:
        for hf in range(2):
            add("QS", lambda e, a=D[scr], h=hf: e.dma_start(
                    out=S["Wbuf"][:, h * 8192:(h + 1) * 8192],
                    in_=a[:, h * 8192:(h + 1) * 8192]),
                writes=["WbufA" if hf == 0 else "WbufB"], tag=f"ldw{kind}{i}_{hf}")
        for mc in range(16):
            pm = "pmm0" if (mc % 2 == 0) else "pmm1"
            half = 0 if mc < 8 else 1
            for kc in range(8):
                add("T", lambda e, m=mc, k=kc, p=pm, h=half: e.matmul(
                        out=PS[p][:, 0:256],
                        lhsT=S["Wbuf"][:, h * 8192 + k * 1024 + (m % 8) * 128:
                                        h * 8192 + k * 1024 + (m % 8) * 128 + 128],
                        rhs=S["xn_fm"][:, par, k, :],
                        start=(k == 0), stop=(k == 7)),
                    reads=["WbufA" if half == 0 else "WbufB", fmP],
                    writes=[pm], tag=f"ip{kind}{i}_{mc}_{kc}")
            if mc % 2 == 0:
                add("S", lambda e, m=mc, p=pm, d=dstb: e.copy(
                        out=S[d][:, m, :], in_=PS[p][:, 0:256]),
                    reads=[pm], writes=[dstb, f"{dstb}{i}_{mc}"],
                    tag=f"{kind}ev{i}_{mc}")
            else:
                add("V", lambda e, m=mc, p=pm, d=dstb: e.tensor_copy(
                        out=S[d][:, m, :], in_=PS[p][:, 0:256]),
                    reads=[pm], writes=[dstb, f"{dstb}{i}_{mc}"],
                    tag=f"{kind}ev{i}_{mc}")
    for hf in range(2):
        add("QS", lambda e, h=hf: e.dma_start(
                out=S["Wbuf"][:, h * 8192:(h + 1) * 8192],
                in_=D["scr_z"][:, h * 8192:(h + 1) * 8192]),
            writes=["WbufA" if hf == 0 else "WbufB"], tag=f"ldwz{i}_{hf}")
    for tc in range(2):
        for ng in range(4):
            pm = "pmm0" if (ng % 2 == 0) else "pmm1"
            half = 0 if ng < 2 else 1
            for kc in range(8):
                add("T", lambda e, t=tc, n=ng, k=kc, p=pm, h=half: e.matmul(
                        out=PS[p][:, 0:512],
                        lhsT=S["xn_fm"][:, par, k, t * 128:(t + 1) * 128],
                        rhs=S["Wbuf"][:, h * 8192 + k * 1024 + (n % 2) * 512:
                                       h * 8192 + k * 1024 + (n % 2) * 512 + 512],
                        start=(k == 0), stop=(k == 7)),
                    reads=["WbufA" if half == 0 else "WbufB", fmP],
                    writes=[pm], tag=f"zm{i}_{tc}_{ng}_{kc}")
            add("S", lambda e, t=tc, n=ng, p=pm: e.activation(
                    out=S["zs"][:, t, n * 512:(n + 1) * 512], in_=PS[p][:, 0:512],
                    func=AF.Silu),
                reads=[pm], writes=["zs"], tag=f"zev{i}_{tc}_{ng}")
    for hf in range(2):
        add("QS", lambda e, h=hf: e.dma_start(
                out=S["Wbuf"][:, h * 8192:(h + 1) * 8192],
                in_=D["scr_xr"][:, h * 8192:(h + 1) * 8192]),
            writes=["WbufA" if hf == 0 else "WbufB"], tag=f"ldwxr{i}_{hf}")
    for tc in range(2):
        for ng in range(4):
            pm = "pmm0" if (ng % 2 == 0) else "pmm1"
            half = 0 if ng < 2 else 1
            for kc in range(8):
                add("T", lambda e, t=tc, n=ng, k=kc, p=pm, h=half: e.matmul(
                        out=PS[p][:, 0:512],
                        lhsT=S["xn_fm"][:, par, k, t * 128:(t + 1) * 128],
                        rhs=S["Wbuf"][:, h * 8192 + k * 1024 + (n % 2) * 512:
                                       h * 8192 + k * 1024 + (n % 2) * 512 + 512],
                        start=(k == 0), stop=(k == 7)),
                    reads=["WbufA" if half == 0 else "WbufB", fmP],
                    writes=[pm], tag=f"xrm{i}_{tc}_{ng}_{kc}")
            add("S", lambda e, t=tc, n=ng, p=pm: e.copy(
                    out=S["xr"][:, t, n * 512:(n + 1) * 512], in_=PS[p][:, 0:512]),
                reads=[pm], writes=["xr", f"xr{i}_{tc}"], tag=f"xrev{i}_{tc}_{ng}")
    X.tap(f"zs{i}", lambda: S["zs"][:].rearrange("p c n -> p (c n)"), ["zs"])
    X.tap(f"zsm{i}", lambda: S["zs"][:].rearrange("p c n -> p (c n)"), ["zs"])
    X.tap(f"Braw{i}", lambda: S["bufA"][:].rearrange("p c n -> p (c n)"),
          ["bufA"] + [f"bufA{i}_{m}" for m in range(16)])
    X.tap(f"xr{i}", lambda: S["xr"][:].rearrange("p c n -> p (c n)"),
          ["xr", f"xr{i}_0", f"xr{i}_1"])


def _angles(X, i, par):
    S, PS, add = X.S, X.PS, X.P.add
    for hf in range(2):
        for pc in range(4):
            g = 4 * hf + pc
            add("T", lambda e, gg=g, p=pc: e.matmul(
                    out=PS["pe"][:, p * 256:(p + 1) * 256],
                    lhsT=S["selth_sb"][:, gg * 128:(gg + 1) * 128],
                    rhs=S["Lc"][:, 1:257], start=True, stop=True),
                reads=["Lc", "selth_sb"], writes=["pe"], tag=f"thmm{i}_{g}")
        add("V", lambda e: e.tensor_scalar(out=S["e1b"][:], in0=PS["pe"][:],
                                           scalar1=INV_2PI, scalar2=MAGIC,
                                           op0=ALU.mult, op1=ALU.add),
            reads=["pe"], writes=["e1b"], tag=f"thu{i}_{hf}")
        add("V", lambda e: e.tensor_scalar(out=S["scr4k"][:], in0=S["e1b"][:],
                                           scalar1=MAGIC, scalar2=None,
                                           op0=ALU.subtract),
            reads=["e1b"], writes=["scr4k"], tag=f"thk{i}_{hf}")
        add("V", lambda e: e.scalar_tensor_tensor(
                out=S["e1b"][:], in0=S["scr4k"][:], scalar=-TWO_PI,
                in1=PS["pe"][:], op0=ALU.mult, op1=ALU.add),
            reads=["scr4k", "pe", "e1b"], writes=["e1b"], tag=f"thr{i}_{hf}")
        add("S", lambda e, h=hf: e.activation(
                out=S["sth"][:, 4 * h:4 * h + 4, :].rearrange("p c n -> p (c n)"),
                in_=S["e1b"][:], func=AF.Sin),
            reads=["e1b"], writes=["sth"], tag=f"sin{i}_{hf}")
        add("S", lambda e: e.activation(out=S["scr4k"][:], in_=S["e1b"][:],
                                        func=AF.Abs),
            reads=["e1b", "scr4k"], writes=["scr4k"], tag=f"tha{i}_{hf}")
        add("S", lambda e, h=hf: e.activation(
                out=S["cth"][:, 4 * h:4 * h + 4, :].rearrange("p c n -> p (c n)"),
                in_=S["scr4k"][:], func=AF.Sin, scale=-1.0, bias=S["scals"][:, 7:8]),
            reads=["scr4k", "pih"], writes=["cth"], tag=f"cos{i}_{hf}")
    X.tap(f"cth{i}", lambda: S["cth"][:].rearrange("p c n -> p (c n)"), ["cth"])
    X.tap(f"sth{i}", lambda: S["sth"][:].rearrange("p c n -> p (c n)"), ["sth"])


def _rope(X, i, par):
    S, add = X.S, X.P.add

    def rot(srcb, dstb, src_reads):
        src, dst = S[srcb], S[dstb]
        x0 = src[:, 0:8, :].rearrange("p c n -> p (c n)")
        x1 = src[:, 8:16, :].rearrange("p c n -> p (c n)")
        o0 = dst[:, 0:8, :].rearrange("p c n -> p (c n)")
        o1 = dst[:, 8:16, :].rearrange("p c n -> p (c n)")
        ct = S["cth"][:].rearrange("p c n -> p (c n)")
        st = S["sth"][:].rearrange("p c n -> p (c n)")
        t1, t2 = S["Btm"][:], S["X2"][:]
        sr = src_reads + ["cth", "sth"]
        add("V", lambda e: e.tensor_tensor(out=t1, in0=x0, in1=ct, op=ALU.mult),
            reads=sr + ["Btm"], writes=["Btm"], tag=f"rp1{dstb}{i}")
        add("G", lambda e: e.tensor_tensor(out=t2, in0=x1, in1=st, op=ALU.mult),
            reads=sr + ["X2"], writes=["X2"], tag=f"rp2{dstb}{i}")
        add("V", lambda e: e.tensor_tensor(out=o0, in0=t1, in1=t2, op=ALU.subtract),
            reads=["Btm", "X2", dstb], writes=[dstb], tag=f"rp3{dstb}{i}")
        add("V", lambda e: e.tensor_tensor(out=t1, in0=x0, in1=st, op=ALU.mult),
            reads=sr + ["Btm"], writes=["Btm"], tag=f"rp4{dstb}{i}")
        add("G", lambda e: e.tensor_tensor(out=t2, in0=x1, in1=ct, op=ALU.mult),
            reads=sr + ["X2"], writes=["X2"], tag=f"rp5{dstb}{i}")
        add("V", lambda e: e.tensor_tensor(out=o1, in0=t1, in1=t2, op=ALU.add),
            reads=["Btm", "X2", dstb], writes=[dstb], tag=f"rp6{dstb}{i}")

    rot("bufA", "bufC", ["bufA"] + [f"bufA{i}_{m}" for m in range(16)])
    rot("bufB", "bufA", ["bufB"] + [f"bufB{i}_{m}" for m in range(16)])
    X.tap(f"Brot{i}", lambda: S["bufC"][:].rearrange("p c n -> p (c n)"), ["bufC"])
    X.tap(f"Crot{i}", lambda: S["bufA"][:].rearrange("p c n -> p (c n)"), ["bufA"])


def _chunk(X, i, c, par, first_of_seq):
    import os
    sub = int(os.environ.get("KCHSUB", "9"))
    S, PS, add = X.S, X.PS, X.P.add
    cs = slice(c * 128, (c + 1) * 128)
    b_active = i // TILES_PER_SEQ
    boff = b_active * 1024
    dtP = f"dtP{par}"

    add("V", lambda e: e.tensor_scalar(
            out=S["Pq"][:], in0=S["Lc"][:, 1 + c * 128:1 + c * 128 + 128],
            scalar1=S["Lc"][:, c * 128 + 128:c * 128 + 129],
            scalar2=S["sc32"][:, 2:3], op0=ALU.subtract, op1=ALU.mult),
        reads=["Lc", "A", "Pq"], writes=["Pq"], tag=f"P{i}_{c}")
    add("V", lambda e: e.tensor_copy(out=S["Psplit"][0:32, :], in_=S["Pq"][:]),
        reads=["Pq", "Psplit"], writes=["Psplit"], tag=f"Phi{i}_{c}")
    add("V", lambda e: e.tensor_tensor(out=S["Psplit"][32:64, :], in0=S["Pq"][:],
                                       in1=S["Psplit"][0:32, :], op=ALU.subtract),
        reads=["Pq", "Psplit"], writes=["Psplit"], tag=f"Plo{i}_{c}")
    add("T", lambda e: e.transpose(out=PS["pmm0"][:, 384:416], in_=S["Pq"][:],
                                   identity=S["id32"][:]),
        reads=["Pq", "id32", "pmm0"], writes=["pmm0"], tag=f"tM{i}_{c}")
    add("T", lambda e: e.transpose(out=PS["pmm0"][:, 416:448],
                                   in_=S["ldt"][:, c * 128:c * 128 + 128],
                                   identity=S["id32"][:]),
        reads=["ldt", "id32"], writes=["pmm0"], tag=f"tl1{i}_{c}")
    add("T", lambda e: e.transpose(out=PS["pmm0"][:, 448:480],
                                   in_=S["dts"][:, par, c * 128:c * 128 + 128],
                                   identity=S["id32"][:]),
        reads=[dtP, "id32"], writes=["pmm0"], tag=f"tw1{i}_{c}")
    add("T", lambda e: e.transpose(out=PS["pmm0"][:, 480:512],
                                   in_=S["dts"][:, par, c * 128 + 1:c * 128 + 129],
                                   identity=S["id32"][:]),
        reads=[dtP, "id32"], writes=["pmm0"], tag=f"tw2{i}_{c}")
    add("V", lambda e: e.tensor_copy(out=S["Mt"][:], in_=PS["pmm0"][:, 384:416]),
        reads=["pmm0", "Mt"], writes=["Mt"], tag=f"Mt{i}_{c}")
    add("V", lambda e: e.tensor_copy(out=S["l1t"][:], in_=PS["pmm0"][:, 416:448]),
        reads=["pmm0", "l1t"], writes=["l1t"], tag=f"l1t{i}_{c}")
    add("S", lambda e: e.activation(out=S["w1t"][:], in_=PS["pmm0"][:, 448:480],
                                    func=AF.Copy, scale=0.5),
        reads=["pmm0", "w1t"], writes=["w1t"], tag=f"w1t{i}_{c}")
    add("S", lambda e: e.activation(out=S["w2t"][:], in_=PS["pmm0"][:, 480:512],
                                    func=AF.Copy, scale=0.5),
        reads=["pmm0", "w2t"], writes=["w2t"], tag=f"w2t{i}_{c}")
    add("V", lambda e: e.tensor_tensor(out=S["Mcol1"][:], in0=S["Mt"][:],
                                       in1=S["l1t"][:], op=ALU.subtract),
        reads=["Mt", "l1t", "Mcol1"], writes=["Mcol1"], tag=f"Mc1{i}_{c}")
    add("V", lambda e: e.reciprocal(out=S["exS"][:], in_=S["w1t"][:]),
        reads=["w1t", "exS"], writes=["exS"], tag=f"rw1{i}_{c}")
    add("V", lambda e: e.tensor_tensor(out=S["ratio"][:], in0=S["w2t"][:],
                                       in1=S["exS"][:], op=ALU.mult),
        reads=["w2t", "exS", "ratio"], writes=["ratio"], tag=f"ratio{i}_{c}")
    add("V", lambda e: e.tensor_tensor(out=S["l1t"][:], in0=S["w1t"][:],
                                       in1=S["w2t"][:], op=ALU.add),
        reads=["w1t", "w2t", "l1t"], writes=["l1t"], tag=f"wsum{i}_{c}")
    add("S", lambda e: e.activation(out=S["exS"][:], in_=S["Mt"][:], func=AF.Exp,
                                    scale=-1.0),
        reads=["Mt", "exS"], writes=["exS"], tag=f"exS{i}_{c}")
    add("V", lambda e: e.tensor_tensor(out=S["ccol"][:], in0=S["l1t"][:],
                                       in1=S["exS"][:], op=ALU.mult),
        reads=["l1t", "exS", "ccol"], writes=["ccol"], tag=f"ccol{i}_{c}")
    add("V", lambda e: e.tensor_tensor(
            out=S["X2"][:].rearrange("p (h q) -> p h q", h=32),
            in0=S["xr"][:, c, :].rearrange("p (h q) -> p h q", h=32),
            in1=S["ccol"][:].unsqueeze(2).broadcast_to([128, 32, 64]),
            op=ALU.mult),
        reads=["xr", f"xr{i}_{c}", "ccol", "X2"], writes=["X2"], tag=f"X2{i}_{c}")
    for rnd in range(2):
        for k in range(8):
            mc = rnd * 8 + k
            add("T", lambda e, m=mc, kk=k: e.transpose(
                    out=PS["pt"][:, kk * 128:(kk + 1) * 128],
                    in_=S["bufC"][:, m, cs], identity=S["id_f16"][:]),
                reads=["bufC", "id_f16", "pt"], writes=["pt"],
                tag=f"btm{i}_{c}_{mc}")
        add("S", lambda e, r=rnd: e.copy(out=S["Btm"][:, r * 1024:(r + 1) * 1024],
                                         in_=PS["pt"][:]),
            reads=["pt", "Btm"], writes=["Btm"], tag=f"btmev{i}_{c}_{rnd}")

    if sub < 2:
        return
    if not first_of_seq:
        add("V", lambda e: e.tensor_scalar(
                out=S["dcol"][:, 0:1], in0=S["Lc"][:, c * 128 + 128:c * 128 + 129],
                scalar1=S["Lc"][:, c * 128:c * 128 + 1], scalar2=S["sc32"][:, 2:3],
                op0=ALU.subtract, op1=ALU.mult),
            reads=["Lc", "A", "dcol"], writes=["dcol"], tag=f"lndch{i}_{c}")
        add("V", lambda e: e.tensor_scalar(out=S["dtmp"][:, 0:128], in0=S["Pq"][:],
                                           scalar1=S["dcol"][:, 0:1], scalar2=None,
                                           op0=ALU.add),
            reads=["Pq", "dcol", "dtmp"], writes=["dtmp"], tag=f"deca{i}_{c}")
        for pc in range(8):
            add("T", lambda e, p=pc: e.matmul(
                    out=PS["pe"][:, p * 128:(p + 1) * 128],
                    lhsT=S["seldec_sb"][:, p * 128:(p + 1) * 128],
                    rhs=S["dtmp"][:, 0:128], start=True, stop=True),
                reads=["dtmp", "seldec_sb", "pe"], writes=["pe"],
                tag=f"decmm{i}_{c}_{pc}")
        add("S", lambda e: e.activation(
                out=S["decr"][:].rearrange("p c n -> p (c n)"), in_=PS["pe"][:],
                func=AF.Exp),
            reads=["pe", "decr"], writes=["decr"], tag=f"decev{i}_{c}")
        add("V", lambda e: e.tensor_tensor(
                out=S["Cdec"][:],
                in0=S["bufA"][:, :, cs].rearrange("p (b c) n -> p b c n", b=2),
                in1=S["decr"][:].unsqueeze(1).broadcast_to([128, 2, 8, 128]),
                op=ALU.mult),
            reads=["bufA", "decr", "Cdec"], writes=["Cdec"], tag=f"cdec{i}_{c}")
        add("S", lambda e: e.activation(out=S["dcol"][:, 1:2], in_=S["dcol"][:, 0:1],
                                        func=AF.Exp),
            reads=["dcol"], writes=["dcol"], tag=f"dch{i}_{c}")
        add("V", lambda e: e.tensor_scalar(out=S["rhsd"][:], in0=S["i2dch_sb"][:],
                                           scalar1=S["dcol"][:, 1:2], scalar2=None,
                                           op0=ALU.mult),
            reads=["dcol", "i2dch_sb", "rhsd"], writes=["rhsd"], tag=f"rhsd{i}_{c}")
        add("T", lambda e: e.matmul(out=PS["pmm1"][:, 504:512],
                                    lhsT=S["seldch_sb"][:], rhs=S["rhsd"][:],
                                    start=True, stop=True),
            reads=["rhsd", "seldch_sb", "pmm1"], writes=["pmm1"],
            tag=f"dchmm{i}_{c}")

    if sub < 3:
        return
    for q in range(4):
        _qA(X, i, c, q)
        if sub >= 4 and q > 0:
            _qD(X, i, c, q - 1)
        _qB(X, i, c, q)
        if sub >= 4:
            _qC(X, i, c, q, cs)
        if sub >= 5 and q > 0:
            _qE(X, i, c, q - 1, first_of_seq, boff)
    if sub >= 4:
        _qD(X, i, c, 3)
    if sub >= 5:
        _qE(X, i, c, 3, first_of_seq, boff)
    if sub < 6:
        return

    hsl = S["H"][:, boff:boff + 1024]
    hbsl = S["Hb"][:, boff:boff + 1024]
    if not first_of_seq:
        add("V", lambda e: e.tensor_tensor(
                out=S["scr4k"][:].rearrange("p (a b) -> p a b", a=8),
                in0=hsl.rearrange("p (a b) -> p a b", a=8),
                in1=PS["pmm1"][:, 504:512].unsqueeze(2).broadcast_to([128, 8, 128]),
                op=ALU.mult),
            reads=["H", "pmm1", "scr4k"], writes=["scr4k"], tag=f"hdec{i}_{c}")
    for qq in range(4):
        for mc in range(8):
            h = qq * 8 + mc
            for piece in range(2):
                cb = mc * 128 + (piece * 1024)
                fo = mc * 128 + piece * 64
                add("T", lambda e, c0=cb, f=fo, hh_=h: e.matmul(
                        out=PS["psc" if f < 512 else "psc2"]
                        [:, f % 512:f % 512 + 64],
                        lhsT=S["Btm"][:, c0:c0 + 128],
                        rhs=S["X2"][:, hh_ * 64:hh_ * 64 + 64],
                        start=True, stop=True),
                    reads=["Btm", "X2", "psc", "psc2"], writes=["psc", "psc2"],
                    tag=f"st{i}_{c}_{h}_{piece}")
        if first_of_seq:
            add("V", lambda e, b=qq: e.tensor_copy(
                    out=hsl[32 * b:32 * b + 32, 0:512],
                    in_=PS["psc"][32 * b:32 * b + 32, :]),
                reads=["psc", "H"], writes=["H"], tag=f"hb0{i}_{c}_{qq}")
            add("S", lambda e, b=qq: e.copy(
                    out=hsl[32 * b:32 * b + 32, 512:1024],
                    in_=PS["psc2"][32 * b:32 * b + 32, :]),
                reads=["psc2", "H"], writes=["H"], tag=f"hb1{i}_{c}_{qq}")
        else:
            add("V", lambda e, b=qq: e.tensor_tensor(
                    out=hsl[32 * b:32 * b + 32, 0:512],
                    in0=S["scr4k"][32 * b:32 * b + 32, 0:512],
                    in1=PS["psc"][32 * b:32 * b + 32, :], op=ALU.add),
                reads=["psc", "H", "scr4k"], writes=["H"], tag=f"hb0{i}_{c}_{qq}")
            add("V", lambda e, b=qq: e.tensor_tensor(
                    out=hsl[32 * b:32 * b + 32, 512:1024],
                    in0=S["scr4k"][32 * b:32 * b + 32, 512:1024],
                    in1=PS["psc2"][32 * b:32 * b + 32, :], op=ALU.add),
                reads=["psc2", "H", "scr4k"], writes=["H"], tag=f"hb1{i}_{c}_{qq}")
    add("G", lambda e: e.tensor_copy(out=hbsl, in_=hsl),
        reads=["H", "Hb"], writes=["Hb"], tag=f"hb{i}_{c}")
    X.tap(f"H{i}_{c}", lambda: S["H"][:, :], ["H"])


def _qA(X, i, c, q):
    S, PS, add = X.S, X.PS, X.P.add
    qb = 32 * q
    add("QA", lambda e: e.dma_start(out=S["Pflat"][qb:qb + 1, :],
                                    in_=S["Psplit"][q * 8:q * 8 + 8, :]),
        reads=["Psplit", "Pflat"], writes=["Pflat"], tag=f"pf0{i}_{c}_{q}")
    add("QA", lambda e: e.dma_start(out=S["Pflat"][qb + 1:qb + 2, :],
                                    in_=S["Psplit"][32 + q * 8:32 + q * 8 + 8, :]),
        reads=["Psplit", "Pflat"], writes=["Pflat"], tag=f"pf1{i}_{c}_{q}")
    for hf in range(2):
        sl = slice(hf * 512, (hf + 1) * 512)
        add("T", lambda e, s=sl: e.matmul(out=PS["pe"][:, s], lhsT=S["id_bf"][:],
                                          rhs=S["tri4"][:, 0:512], start=True,
                                          stop=False),
            reads=["tri4", "id_bf", "pe"], writes=["pe"], tag=f"et{i}_{c}_{q}_{hf}")
        add("T", lambda e, s=sl: e.matmul(out=PS["pe"][:, s],
                                          lhsT=S["ones2"][qb:qb + 2, :],
                                          rhs=S["Pflat"][qb:qb + 2, s], start=False,
                                          stop=True, tile_position=(qb, 0)),
            reads=["Pflat", "ones2", "pe"], writes=["pe"], tag=f"em{i}_{c}_{q}_{hf}")


def _qB(X, i, c, q):
    S, PS, add = X.S, X.PS, X.P.add
    add("V", lambda e: e.tensor_tensor(
            out=S["e1b"][:].rearrange("p (h t) -> p h t", h=8),
            in0=PS["pe"][:].rearrange("p (h t) -> p h t", h=8),
            in1=S["Mcol1"][:, q * 8:q * 8 + 8].unsqueeze(2)
            .broadcast_to([128, 8, 128]),
            op=ALU.subtract),
        reads=["pe", "Mcol1", "e1b"], writes=["e1b"], tag=f"e1{i}_{c}_{q}")
    add("S", lambda e: e.activation(out=S["M1b"][:], in_=S["e1b"][:], func=AF.Exp),
        reads=["e1b", "M1b"], writes=["M1b"], tag=f"M1{i}_{c}_{q}")
    if q == 0 and c == 0:
        X.tap(f"M1_{i}", lambda: S["M1b"][:], ["M1b"])


def _qC(X, i, c, q, cs):
    S, PS, add = X.S, X.PS, X.P.add
    import os as _os
    hh4max = int(_os.environ.get("KHH4", "4"))
    for g in range(2):
        pg = "psc" if g == 0 else "psc2"
        for hh4 in range(hh4max):
            hh = g * 4 + hh4
            h = q * 8 + hh
            m = q
            mcx0 = h % 8
            fo = hh4 * 128
            add("T", lambda e, mm=m, mc=mcx0, f=fo, p=pg, s=cs: e.matmul(
                    out=PS[p][:, f:f + 128],
                    lhsT=S["bufC"][32 * mm:32 * mm + 32, mc, s],
                    rhs=S["bufA"][32 * mm:32 * mm + 32, mc, s],
                    start=True, stop=False, tile_position=(32 * mm, 0)),
                reads=["bufA", "bufC", pg], writes=[pg], tag=f"sc0{i}_{c}_{h}")
            add("T", lambda e, mm=m, mc=8 + mcx0, f=fo, p=pg, s=cs: e.matmul(
                    out=PS[p][:, f:f + 128],
                    lhsT=S["bufC"][32 * mm:32 * mm + 32, mc, s],
                    rhs=S["bufA"][32 * mm:32 * mm + 32, mc, s],
                    start=False, stop=True, tile_position=(32 * mm, 0)),
                reads=["bufA", "bufC"], writes=[pg], tag=f"sc1{i}_{c}_{h}")


def _qD(X, i, c, q):
    S, PS, add = X.S, X.PS, X.P.add
    m2t = S["e1b"][:, 0:512].bitcast(FP16)
    add("V", lambda e: e.tensor_tensor(
            out=m2t.rearrange("p (h t) -> p h t", h=8),
            in0=S["M1b"][:].rearrange("p (h t) -> p h t", h=8),
            in1=S["ratio"][:, q * 8:q * 8 + 8].unsqueeze(2)
            .broadcast_to([128, 8, 128]),
            op=ALU.mult),
        reads=["M1b", "ratio"], writes=["e1b"], tag=f"M2a{i}_{c}_{q}")
    add("V", lambda e: e.tensor_tensor(
            out=S["M2b"][:].rearrange("p (h t) -> p h t", h=8),
            in0=m2t.rearrange("p (h t) -> p h t", h=8),
            in1=S["antiI"][:].unsqueeze(1).broadcast_to([128, 8, 128]),
            op=ALU.mult),
        reads=["e1b", "antiI", "M2b"], writes=["M2b"], tag=f"M2b{i}_{c}_{q}")
    for g in range(2):
        pg = "psc" if g == 0 else "psc2"
        add("V", lambda e, p=pg, gg=g: e.tensor_tensor(
                out=S["Sm1"][:, gg * 512:(gg + 1) * 512], in0=PS[p][:],
                in1=S["M1b"][:, gg * 512:(gg + 1) * 512], op=ALU.mult),
            reads=[pg, "M1b", "Sm1"], writes=["Sm1"], tag=f"sm1{i}_{c}_{q}_{g}")
        add("V", lambda e, p=pg, gg=g: e.tensor_tensor(
                out=S["Sm2"][:, gg * 512:(gg + 1) * 512], in0=PS[p][:],
                in1=S["M2b"][:, gg * 512:(gg + 1) * 512], op=ALU.mult),
            reads=[pg, "M2b", "Sm2"], writes=["Sm2"], tag=f"sm2{i}_{c}_{q}_{g}")
    if q == 0 and c == 0:
        X.tap(f"Sm1_{i}", lambda: S["Sm1"][:], ["Sm1"])
        X.tap(f"Sm2_{i}", lambda: S["Sm2"][:], ["Sm2"])


def _qE(X, i, c, q, first_of_seq, boff):
    S, PS, add = X.S, X.PS, X.P.add
    m = q
    for hh in range(8):
        h = q * 8 + hh
        m = q
        mcx0 = h % 8
        fo = hh * 64
        smo = hh * 128
        add("T", lambda e, f=fo, so=smo, hh_=h: e.matmul(
                out=PS["py"][:, f:f + 64],
                lhsT=S["Sm1"][:, so:so + 128],
                rhs=S["xr"][:, c, hh_ * 64:hh_ * 64 + 64],
                start=True, stop=False),
            reads=["Sm1", "xr", f"xr{i}_{c}", "py"], writes=["py"],
            tag=f"yi1{i}_{c}_{h}")
        add("T", lambda e, f=fo, so=smo, hh_=h: e.matmul(
                out=PS["py"][:, f:f + 64],
                lhsT=S["Sm2"][:, so:so + 128],
                rhs=S["xr"][:, c, hh_ * 64:hh_ * 64 + 64],
                start=False, stop=first_of_seq),
            reads=["Sm2"], writes=["py"], tag=f"yi2{i}_{c}_{h}")
        if not first_of_seq:
            add("T", lambda e, f=fo, mm=m, mc=mcx0: e.matmul(
                    out=PS["py"][:, f:f + 64],
                    lhsT=S["Cdec"][32 * mm:32 * mm + 32, 0, mc, :],
                    rhs=S["Hb"][32 * mm:32 * mm + 32,
                                boff + mc * 128:boff + mc * 128 + 64],
                    start=False, stop=False, tile_position=(32 * mm, 0)),
                reads=["Hb", "Cdec"], writes=["py"], tag=f"yn0{i}_{c}_{h}")
            add("T", lambda e, f=fo, mm=m, mc=mcx0: e.matmul(
                    out=PS["py"][:, f:f + 64],
                    lhsT=S["Cdec"][32 * mm:32 * mm + 32, 1, mc, :],
                    rhs=S["Hb"][32 * mm:32 * mm + 32,
                                boff + mc * 128 + 64:boff + mc * 128 + 128],
                    start=False, stop=True, tile_position=(32 * mm, 0)),
                reads=["Hb", "Cdec"], writes=["py"], tag=f"yn1{i}_{c}_{h}")
    add("V", lambda e: e.tensor_tensor(
            out=S["yg"][:, c, q * 512:(q + 1) * 512], in0=PS["py"][:],
            in1=S["zs"][:, c, q * 512:(q + 1) * 512], op=ALU.mult),
        reads=["py", "zs", "yg"], writes=["yg"], tag=f"ygev{i}_{c}_{q}")


def _epilogue(X, i, par):
    S, PS, D, add = X.S, X.PS, X.D, X.P.add
    X.tap(f"yg{i}", lambda: S["yg"][:].rearrange("p c n -> p (c n)"), ["yg"])
    for c in range(NCH):
        add("S", lambda e, cc=c: e.activation(out=S["e1b"][:],
                                              in_=S["yg"][:, cc, 0:1024],
                                              func=AF.Square),
            reads=["yg", "e1b"], writes=["e1b"], tag=f"rsq0{i}_{c}")
        add("V", lambda e: e.tensor_reduce(out=S["lnmv"][:, 0:1], in_=S["e1b"][:],
                                           axis=mybir.AxisListType.X, op=ALU.add),
            reads=["e1b", "lnmv"], writes=["lnmv"], tag=f"rr0{i}_{c}", lagged=True)
        add("S", lambda e, cc=c: e.activation(out=S["e1b"][:],
                                              in_=S["yg"][:, cc, 1024:2048],
                                              func=AF.Square),
            reads=["yg", "e1b"], writes=["e1b"], tag=f"rsq1{i}_{c}")
        add("V", lambda e: e.tensor_reduce(out=S["lnmv"][:, 1:2], in_=S["e1b"][:],
                                           axis=mybir.AxisListType.X, op=ALU.add),
            reads=["e1b", "lnmv"], writes=["lnmv"], tag=f"rr1{i}_{c}", lagged=True)
        add("V", lambda e: e.tensor_tensor(out=S["lnsd"][:], in0=S["lnmv"][:, 0:1],
                                           in1=S["lnmv"][:, 1:2], op=ALU.add),
            reads=["lnmv", "lnsd"], writes=["lnsd"], tag=f"rms{i}_{c}")
        add("S", lambda e: e.activation(out=S["lnrs"][:], in_=S["lnsd"][:],
                                        func=AF.Ln, scale=1.0 / 2048.0,
                                        bias=S["scals"][:, 6:7]),
            reads=["lnsd", "epsrms", "lnrs"], writes=["lnrs"], tag=f"rln{i}_{c}")
        add("S", lambda e: e.activation(out=S["lnmb"][:], in_=S["lnrs"][:],
                                        func=AF.Exp, scale=-0.5),
            reads=["lnrs", "lnmb"], writes=["lnmb"], tag=f"rex{i}_{c}")
        add("V", lambda e, cc=c: e.tensor_scalar(out=S["Btm"][:],
                                                 in0=S["yg"][:, cc, :],
                                                 scalar1=S["lnmb"][:, 0:1],
                                                 scalar2=None, op0=ALU.mult),
            reads=["yg", "lnmb", "Btm"], writes=["Btm"], tag=f"ynm{i}_{c}")
        for rnd in range(2):
            for k in range(8):
                fb = rnd * 8 + k
                add("T", lambda e, f=fb, kk=k: e.transpose(
                        out=PS["pt"][:, kk * 128:(kk + 1) * 128],
                        in_=S["Btm"][:, f * 128:(f + 1) * 128],
                        identity=S["id_f16"][:]),
                    reads=["Btm", "id_f16", "pt"], writes=["pt"],
                    tag=f"ynt{i}_{c}_{fb}")
            add("S", lambda e, r=rnd, cc=c: e.copy(
                    out=S["bufB"][:, r * 8:(r + 1) * 8, cc * 128:(cc + 1) * 128],
                    in_=PS["pt"][:].rearrange("p (a b) -> p a b", a=8)),
                reads=["pt", "bufB"], writes=["bufB", "yn"], tag=f"ynev{i}_{c}_{rnd}")
    X.tap(f"yn{i}", lambda: S["bufB"][:].rearrange("p c n -> p (c n)"),
          ["bufB", "yn"])
    for hf in range(2):
        add("QS", lambda e, h=hf: e.dma_start(
                out=S["Wbuf"][:, h * 8192:(h + 1) * 8192],
                in_=D["scr_out"][:, h * 8192:(h + 1) * 8192]),
            writes=["WbufA" if hf == 0 else "WbufB"], tag=f"ldwo{i}_{hf}")
    for tc in range(2):
        for ng in range(2):
            pm = "pmm0" if ng == 0 else "pmm1"
            for kc in range(16):
                add("T", lambda e, t=tc, n=ng, k=kc, p=pm: e.matmul(
                        out=PS[p][:, 0:512],
                        lhsT=S["bufB"][:, k, t * 128:(t + 1) * 128],
                        rhs=S["Wbuf"][:, k * 1024 + n * 512:k * 1024 + n * 512 + 512],
                        start=(k == 0), stop=(k == 15)),
                    reads=["WbufA" if kc < 8 else "WbufB", "bufB", "yn", pm],
                    writes=[pm], tag=f"op{i}_{tc}_{ng}_{kc}")
            add("V", lambda e, t=tc, n=ng, p=pm: e.tensor_tensor(
                    out=S["sgb"][:, n * 512:(n + 1) * 512], in0=PS[p][:],
                    in1=S["gbuf"][:, t, n * 512:(n + 1) * 512], op=ALU.mult),
                reads=[pm, "gbuf", "sgb"], writes=["sgb"], tag=f"sg{i}_{tc}_{ng}")
        add("V", lambda e, t=tc: e.tensor_tensor(
                out=S["outb"][:], in0=S["sgb"][:],
                in1=S["xt"][:, par, t, :], op=ALU.add),
            reads=["sgb", f"xtP{par}", "outb"], writes=["outb"], tag=f"ob{i}_{tc}")
        add("QS", lambda e, t=tc: e.dma_start(
                out=D["out"][i * 256 + t * 128:i * 256 + t * 128 + 128, :],
                in_=S["outb"][:]),
            reads=["outb"], writes=[f"outdma{i}" if tc == 1 else f"outdmaa{i}"],
            tag=f"st_out{i}_{tc}")
    X.tap(f"s{i}", lambda: S["sgb"][:], ["sgb"])



def kernel(**inputs):
    in_maps = setup_core_inputs(inputs)
    nc = build_nc()
    res = run_bass_kernel_spmd(nc, in_maps, core_ids=list(range(N_CORES)))
    out = np.concatenate(
        [np.asarray(r["out"]).reshape(2, 2048, 1024) for r in res.results], axis=0)
    return np.ascontiguousarray(out.reshape(16, 2048, 1024)).astype(np.float32)


# version 06
# speedup vs baseline: 1.1524x; 1.0500x over previous
import contextlib
import math

import numpy as np
import ml_dtypes

import concourse.bass as bass
import concourse.mybir as mybir
from concourse.bass_utils import run_bass_kernel_spmd

F32 = mybir.dt.float32
BF16 = mybir.dt.bfloat16
FP16 = mybir.dt.float16
FP8 = mybir.dt.float8e4
AF = mybir.ActivationFunctionType
ALU = mybir.AluOpType

N_CORES = 8
SEQ = 2048
T = 256
Q = 128
N_TILES = 16
TILES_PER_SEQ = SEQ // T
NCH = T // Q

MAGIC = 12582912.0
TWO_PI = float(2.0 * math.pi)
INV_2PI = float(1.0 / TWO_PI)
LN_EPS = 1e-5
RMS_EPS = 1e-6

STREAM_OF = {"T": "tensor", "V": "vector", "S": "scalar", "G": "gpsimd",
             "QS": "sync", "QG": "gpsimd", "QA": "scalar"}
DMA_ENGS = ("QS", "QG", "QA")


class Op:
    __slots__ = ("eng", "emit", "deps", "idx", "eidx", "needed", "inc_ord",
                 "tag", "lagged")

    def __init__(self, eng, emit, deps, idx, eidx, tag, lagged):
        self.eng, self.emit, self.deps = eng, emit, deps
        self.idx, self.eidx = idx, eidx
        self.needed, self.inc_ord, self.tag = False, 0, tag
        self.lagged = lagged


class Sched:
    def __init__(self):
        self.ops = []
        self.eng_ops = {e: [] for e in STREAM_OF}
        self.res = {}

    def add(self, eng, emit, reads=(), writes=(), tag="", lagged=False):
        deps = set()
        for t in reads:
            e = self.res.get(t)
            if e and e["w"] is not None:
                deps.add(e["w"])
        for t in writes:
            e = self.res.get(t)
            if e:
                if e["w"] is not None:
                    deps.add(e["w"])
                deps.update(e["r"])
        idx = len(self.ops)
        op = Op(eng, emit, deps, idx, len(self.eng_ops[eng]), tag, lagged)
        self.ops.append(op)
        self.eng_ops[eng].append(op)
        for t in reads:
            self.res.setdefault(t, {"w": None, "r": []})["r"].append(idx)
        for t in writes:
            self.res[t] = {"w": idx, "r": []}
        return idx

    def finalize(self):
        SAFE_DIST = 12
        for op in self.ops:
            best = {}
            for d in op.deps:
                dop = self.ops[d]
                same_stream = STREAM_OF[dop.eng] == STREAM_OF[op.eng]
                if same_stream and not dop.lagged:
                    if dop.eng in DMA_ENGS and op.eng in DMA_ENGS:
                        continue
                    if dop.eng == "T" and op.eng == "T":
                        continue
                    if (dop.eng == op.eng and dop.eng not in DMA_ENGS
                            and op.eidx - dop.eidx >= SAFE_DIST):
                        continue
                    if dop.eng not in DMA_ENGS and op.eng in DMA_ENGS:
                        pass
                e = dop.eng
                if e not in best or self.ops[best[e]].eidx < dop.eidx:
                    best[e] = d
            op.deps = set(best.values())
            for d in op.deps:
                self.ops[d].needed = True
        for e, ops in self.eng_ops.items():
            k = 0
            for op in ops:
                if e in DMA_ENGS or op.needed:
                    k += 1
                    op.inc_ord = k

    @staticmethod
    def _dma_sem(eng, ord_):
        if eng == "QS":
            return f"QS{(ord_ - 1) % 4}", 16 * ((ord_ - 1) // 4 + 1)
        return eng, 16 * ord_

    def emit_stream(self, stream, engobj, sems):
        hw = {}
        for op in self.ops:
            if STREAM_OF[op.eng] != stream:
                continue
            for d in sorted(op.deps, key=lambda i: self.ops[i].eng):
                dop = self.ops[d]
                if dop.eng in DMA_ENGS:
                    key, val = self._dma_sem(dop.eng, dop.inc_ord)
                else:
                    key, val = dop.eng, dop.inc_ord
                if hw.get(key, -1) >= val:
                    continue
                hw[key] = val
                engobj.wait_ge(sems[key], val)
            if op.eng in DMA_ENGS:
                if op.eng == "QS":
                    if op.inc_ord > 4:
                        key, val = self._dma_sem("QS", op.inc_ord - 4)
                    else:
                        key, val = None, 0
                elif op.inc_ord > 1:
                    key, val = op.eng, 16 * (op.inc_ord - 1)
                else:
                    key, val = None, 0
                if key is not None and hw.get(key, -1) < val:
                    hw[key] = val
                    engobj.wait_ge(sems[key], val)
            inst = op.emit(engobj)
            if op.inc_ord:
                assert inst is not None, f"op {op.tag} must return its instruction"
                if op.eng in DMA_ENGS:
                    key, _ = self._dma_sem(op.eng, op.inc_ord)
                    inst.then_inc(sems[key], 16)
                else:
                    inst.then_inc(sems[op.eng], 1)



def _perm_bc():
    p = np.empty(2048, dtype=np.int64)
    r = np.arange(1024)
    pc, m, i = r // 128, (r % 128) // 32, r % 32
    h = m * 8 + pc
    p[:1024] = h * 64 + 2 * i
    p[1024:] = h * 64 + 2 * i + 1
    return p


def _consts():
    c = {}
    ident = np.eye(128, dtype=np.float32)
    c["ident_bf"] = ident.astype(ml_dtypes.bfloat16)
    c["ident_f16"] = ident.astype(np.float16)
    c["ident_f32_32"] = np.eye(32, dtype=np.float32)
    tri = np.where(np.arange(128)[None, :] >= np.arange(128)[:, None],
                   0.0, -1e30).astype(np.float32)
    c["tri_rep4"] = np.tile(tri, (1, 4)).astype(ml_dtypes.bfloat16)
    c["anti_ident"] = (1.0 - np.eye(128, dtype=np.float32)).astype(np.float16)
    invfreq = (1.0 / (10000.0 ** (np.arange(32) / 32.0))).astype(np.float32)
    selth = np.zeros((32, 1024), dtype=np.float32)
    seldec = np.zeros((32, 1024), dtype=np.float32)
    rr = np.arange(1024)
    hh_of_r = ((rr % 128) // 32) * 8 + rr // 128
    selth[hh_of_r, rr] = invfreq[rr % 32]
    seldec[hh_of_r, rr] = 1.0
    c["selth"] = selth
    c["seldec"] = seldec
    seldch = np.zeros((32, 128), dtype=np.float32)
    for h in range(32):
        seldch[h, 32 * (h // 8):32 * (h // 8) + 32] = 1.0
    c["seldch"] = seldch
    i2d = np.zeros((32, 8), dtype=np.float32)
    i2d[np.arange(32), np.arange(32) % 8] = 1.0
    c["i2dch"] = i2d
    c["ones2_bf"] = np.ones((128, 128), dtype=ml_dtypes.bfloat16)
    c["ones1_f32"] = np.ones((1, 128), dtype=np.float32)
    c["onesdt"] = np.ones((32, 256), dtype=np.float32)
    return c


def setup_core_inputs(inputs):
    x = np.ascontiguousarray(
        np.asarray(inputs["L0_attn_hidden"]).reshape(16, 2048, 1024))
    W_in = np.asarray(inputs["L0_attn_W_in"])
    perm = _perm_bc()
    consts = _consts()
    shared = dict(
        Wz=np.ascontiguousarray(W_in[:, 0:2048]),
        Wxr=np.ascontiguousarray(W_in[:, 2048:4096]),
        WB=np.ascontiguousarray(W_in[:, 4096:6144][:, perm]),
        WC=np.ascontiguousarray(W_in[:, 6144:8192][:, perm]),
        Wdt=np.ascontiguousarray(
            np.concatenate([W_in[:, 8192:8224],
                            np.zeros((1024, 96), dtype=W_in.dtype)], axis=1)
            .reshape(8, 128, 128).transpose(1, 0, 2)),
        Wout=np.ascontiguousarray(np.asarray(inputs["L0_attn_W_out"])),
        ffnk=np.ascontiguousarray(
            np.asarray(inputs["L0_ffn_kernel"]).reshape(8, 128, 1024)
            .transpose(1, 0, 2)),
        ffnb_rep=np.ascontiguousarray(np.broadcast_to(
            np.asarray(inputs["L0_ffn_bias"])[None, :], (128, 1024))),
        s_in_col=np.ascontiguousarray(np.broadcast_to(
            np.asarray(inputs["L0_ffn_input_scale"])[:, None], (128, 1))),
        s_k_col=np.ascontiguousarray(np.broadcast_to(
            np.asarray(inputs["L0_ffn_kernel_scale"])[:, None], (128, 1))),
        dtb_col=np.ascontiguousarray(
            np.asarray(inputs["L0_attn_dt_bias"])[:, None]),
        alog_col=np.ascontiguousarray(
            np.asarray(inputs["L0_attn_A_log"])[:, None]),
        nw_fm=np.ascontiguousarray(
            np.asarray(inputs["L0_attn_norm_weight"]).reshape(16, 128).T),
        **consts,
    )
    return [dict(shared, x=np.ascontiguousarray(
        x[2 * ci:2 * ci + 2].reshape(4096, 1024))) for ci in range(N_CORES)]



class Ctx:
    pass


def build_nc(taps=(), n_tiles=N_TILES):
    nc = bass.Bass("TRN2", target_bir_lowering=False)
    P = Sched()
    X = Ctx()
    X.nc, X.P, X.n_tiles = nc, P, n_tiles
    X.taps = {}

    D = {}
    def din(name, shape, dt):
        D[name] = nc.dram_tensor(name, list(shape), dt, kind="ExternalInput").ap()
    def dout(name, shape, dt):
        D[name] = nc.dram_tensor(name, list(shape), dt, kind="ExternalOutput").ap()
    def dscr(name, shape, dt):
        D[name] = nc.dram_tensor(name, list(shape), dt, kind="Internal").ap()

    din("x", (4096, 1024), F32)
    for w in ("Wz", "Wxr", "WB", "WC"):
        din(w, (1024, 2048), F32)
    din("Wdt", (128, 8, 128), F32)
    din("Wout", (2048, 1024), F32)
    din("ffnk", (128, 8, 1024), BF16)
    din("ffnb_rep", (128, 1024), BF16)
    din("s_in_col", (128, 1), F32)
    din("s_k_col", (128, 1), F32)
    din("dtb_col", (32, 1), F32)
    din("alog_col", (32, 1), F32)
    din("nw_fm", (128, 16), F32)
    din("ident_bf", (128, 128), BF16)
    din("ident_f16", (128, 128), FP16)
    din("ident_f32_32", (32, 32), F32)
    din("tri_rep4", (128, 512), BF16)
    din("anti_ident", (128, 128), FP16)
    din("selth", (32, 1024), F32)
    din("seldec", (32, 1024), F32)
    din("seldch", (32, 128), F32)
    din("i2dch", (32, 8), F32)
    din("ones2_bf", (128, 128), BF16)
    din("ones1_f32", (1, 128), F32)
    din("onesdt", (32, 256), F32)
    dout("out", (4096, 1024), F32)
    for name, shape, dt in taps:
        dout("tap_" + name, list(shape), dt)
        X.taps[name] = D["tap_" + name]
    for s in ("scr_z", "scr_B", "scr_C", "scr_xr", "scr_out"):
        dscr(s, (128, 16384), BF16)

    es = contextlib.ExitStack()
    S = {}
    def asb(name, shape, dt):
        S[name] = es.enter_context(nc.sbuf_tensor(name, list(shape), dt))
    PS = {}
    def aps(name, shape, dt):
        PS[name] = es.enter_context(nc.psum_tensor(name, list(shape), dt))

    asb("Wdt_sb", (128, 8, 128), BF16)
    asb("ffn_q", (128, 8, 1024), FP8)
    asb("ffnb", (128, 1024), BF16)
    asb("H", (128, 2048), F32)
    asb("Hb", (128, 2048), FP16)
    asb("id_bf", (128, 128), BF16)
    asb("id_f16", (128, 128), FP16)
    asb("id32", (32, 32), F32)
    asb("tri4", (128, 512), BF16)
    asb("antiI", (128, 128), FP16)
    asb("selth_sb", (32, 1024), F32)
    asb("seldec_sb", (32, 1024), F32)
    asb("seldch_sb", (32, 128), F32)
    asb("i2dch_sb", (32, 8), F32)
    asb("ones2", (128, 128), BF16)
    asb("ones1f", (1, 128), F32)
    asb("onesdt_sb", (32, 256), F32)
    asb("nw_sb", (128, 16), F32)
    asb("scals", (128, 8), F32)
    asb("sc32", (32, 8), F32)
    asb("Wbuf", (128, 16384), BF16)
    asb("xt", (128, 2, 2, 1024), F32)
    asb("xn_tm", (128, 1024), BF16)
    asb("xn_fm", (128, 2, 8, 256), BF16)
    asb("xn_q", (128, 8, 256), FP8)
    asb("dts", (32, 2, 257), F32)
    asb("Lc", (32, 257), F32)
    asb("ldt", (32, 257), F32)
    asb("zs", (128, 2, 2048), FP16)
    asb("yg", (128, 2, 2048), FP16)
    asb("bufA", (128, 16, 256), FP16)
    asb("bufB", (128, 16, 256), FP16)
    asb("bufC", (128, 16, 256), FP16)
    asb("xr", (128, 2, 2048), FP16)
    asb("cth", (128, 8, 256), FP16)
    asb("sth", (128, 8, 256), FP16)
    asb("Btm", (128, 2048), FP16)
    asb("X2", (128, 2048), FP16)
    asb("e1b", (128, 1024), F32)
    asb("scr4k", (128, 1024), F32)
    asb("M1b", (128, 1024), FP16)
    asb("M2b", (128, 1024), FP16)
    asb("Sm1", (128, 1024), FP16)
    asb("Sm2", (128, 1024), FP16)
    asb("decr", (128, 8, 128), FP16)
    asb("Cdec", (128, 2, 8, 128), FP16)
    asb("Pq", (32, 128), F32)
    asb("Psplit", (64, 128), BF16)
    asb("Pflat", (128, 1024), BF16)
    asb("Mt", (128, 32), F32)
    asb("Mcol1", (128, 32), F32)
    asb("l1t", (128, 32), F32)
    asb("w1t", (128, 32), F32)
    asb("w2t", (128, 32), F32)
    asb("ratio", (128, 32), F32)
    asb("exS", (128, 32), F32)
    asb("ccol", (128, 32), F32)
    asb("rhsd", (32, 8), F32)
    asb("gbuf", (128, 2, 1024), BF16)
    asb("sgb", (128, 1024), F32)
    asb("outb", (128, 1024), F32)
    asb("lnstat", (128, 12), F32)
    asb("lnmv", (128, 2), F32)
    asb("lnsd", (128, 1), F32)
    asb("lnrs", (128, 1), F32)
    asb("lnmb", (128, 1), F32)
    asb("dtmp", (32, 256), F32)
    asb("dtmp2", (32, 256), F32)
    asb("dcol", (32, 2), F32)

    aps("pmm0", (128, 512), F32)
    aps("pmm1", (128, 512), F32)
    aps("pe", (128, 1024), F32)
    aps("psc", (128, 512), F32)
    aps("psc2", (128, 512), F32)
    aps("py", (128, 512), F32)
    aps("pt", (128, 1024), FP16)

    X.D, X.S, X.PS = D, S, PS
    _build(X)
    P.finalize()

    sems = {e: es.enter_context(nc.semaphore("sem_" + e)) for e in STREAM_OF}
    for j in range(4):
        sems[f"QS{j}"] = es.enter_context(nc.semaphore(f"sem_QS{j}"))
    with es:
        blk = es.enter_context(nc.Block())
        @blk.sync
        def _(eng): P.emit_stream("sync", eng, sems)
        @blk.tensor
        def _(eng): P.emit_stream("tensor", eng, sems)
        @blk.vector
        def _(eng): P.emit_stream("vector", eng, sems)
        @blk.scalar
        def _(eng): P.emit_stream("scalar", eng, sems)
        @blk.gpsimd
        def _(eng): P.emit_stream("gpsimd", eng, sems)
    return nc



def _build(X):
    D, S, PS, add = X.D, X.S, X.PS, X.P.add

    def tap(name, ap_fn, reads):
        if name in X.taps:
            dst = X.taps[name]
            add("QG", lambda e: e.dma_start(out=dst, in_=ap_fn()),
                reads=list(reads), writes=["tapw_" + name], tag="tap_" + name)
    X.tap = tap

    for nm, dst in [("ident_bf", "id_bf"), ("ident_f16", "id_f16"),
                    ("ident_f32_32", "id32"), ("tri_rep4", "tri4"),
                    ("anti_ident", "antiI"), ("selth", "selth_sb"),
                    ("seldec", "seldec_sb"), ("seldch", "seldch_sb"),
                    ("i2dch", "i2dch_sb"), ("ones2_bf", "ones2"),
                    ("ones1_f32", "ones1f"), ("onesdt", "onesdt_sb"),
                    ("nw_fm", "nw_sb"), ("ffnb_rep", "ffnb")]:
        add("QS", lambda e, a=D[nm], b=S[dst]: e.dma_start(out=b[:], in_=a),
            writes=[dst], tag="ld_" + nm)
    add("QG", lambda e: e.dma_start(out=S["Wdt_sb"][:], in_=D["Wdt"]),
        writes=["Wdt_sb"], tag="ld_Wdt")
    add("QS", lambda e: e.dma_start(out=S["scals"][:, 0:1], in_=D["s_in_col"]),
        writes=["sin0"], tag="ld_sin")
    add("QS", lambda e: e.dma_start(out=S["scals"][:, 1:2], in_=D["s_k_col"]),
        writes=["sk0"], tag="ld_sk")
    add("QS", lambda e: e.dma_start(out=S["sc32"][:, 0:1], in_=D["dtb_col"]),
        writes=["dtb"], tag="ld_dtb")
    add("QS", lambda e: e.dma_start(out=S["sc32"][:, 1:2], in_=D["alog_col"]),
        writes=["alog"], tag="ld_alog")
    add("V", lambda e: e.memset(S["scals"][:, 5:6], LN_EPS), writes=["epsln"], tag="epsln")
    add("V", lambda e: e.memset(S["scals"][:, 6:7], RMS_EPS), writes=["epsrms"], tag="epsrms")
    add("V", lambda e: e.memset(S["scals"][:, 7:8], math.pi / 2), writes=["pih"], tag="pih")
    add("V", lambda e: e.reciprocal(out=S["scals"][:, 2:3], in_=S["scals"][:, 0:1]),
        reads=["sin0"], writes=["rsin"], tag="rsin")
    add("V", lambda e: e.reciprocal(out=S["scals"][:, 3:4], in_=S["scals"][:, 1:2]),
        reads=["sk0"], writes=["rsk"], tag="rsk")
    add("V", lambda e: e.tensor_tensor(out=S["scals"][:, 4:5], in0=S["scals"][:, 0:1],
                                       in1=S["scals"][:, 1:2], op=ALU.mult),
        reads=["sin0", "sk0"], writes=["sprod"], tag="sprod")
    add("S", lambda e: e.activation(out=S["sc32"][:, 3:4], in_=S["sc32"][:, 1:2],
                                    func=AF.Exp),
        reads=["alog"], writes=["aexp"], tag="aexp")
    add("V", lambda e: e.tensor_scalar(out=S["sc32"][:, 2:3], in0=S["sc32"][:, 3:4],
                                       scalar1=-1.0, scalar2=None, op0=ALU.mult),
        reads=["aexp"], writes=["A"], tag="A")
    add("QS", lambda e: e.dma_start(
            out=S["Wbuf"][:, 0:8192].rearrange("p (c n) -> p c n", c=8),
            in_=D["ffnk"]),
        writes=["WbufA"], tag="ld_ffnk")
    add("V", lambda e: e.tensor_scalar(
            out=S["ffn_q"][:].rearrange("p c n -> p (c n)"),
            in0=S["Wbuf"][:, 0:8192], scalar1=S["scals"][:, 3:4],
            scalar2=None, op0=ALU.mult),
        reads=["WbufA", "rsk"], writes=["ffn_q"], tag="q_ffnk")
    def _nw_fold():
        for kc2 in range(16):
            for hf in range(2):
                sl = slice(kc2 * 1024 + hf * 512, kc2 * 1024 + hf * 512 + 512)
                add("V", lambda e, s=sl, k=kc2: e.tensor_scalar(
                        out=S["xn_tm"][:, 0:512],
                        in0=S["Wbuf"][:, s], scalar1=S["nw_sb"][:, k:k + 1],
                        scalar2=None, op0=ALU.mult),
                    reads=["WbufA", "WbufB", "nw_sb", "xn_tm"], writes=["xn_tm"],
                    tag=f"nwf{kc2}_{hf}")
                add("V", lambda e, s=sl: e.tensor_copy(
                        out=S["Wbuf"][:, s], in_=S["xn_tm"][:, 0:512]),
                    reads=["xn_tm", "WbufA", "WbufB"],
                    writes=["WbufA" if kc2 < 8 else "WbufB"], tag=f"nwc{kc2}_{hf}")

    import os as _os2
    wlist = [] if _os2.environ.get("KLNONLY") else [
        ("Wz", "scr_z", 8), ("WB", "scr_B", 8),
        ("WC", "scr_C", 8), ("Wxr", "scr_xr", 8),
        ("Wout", "scr_out", 16)]
    for wsrc, wscr, kc in wlist:
        add("QG", lambda e, a=D[wsrc], kk=kc: e.dma_start(
                out=S["Wbuf"][:].rearrange("p (c n) -> p c n", c=kk),
                in_=a.rearrange("(c p) n -> p c n", p=128)),
            reads=["ffn_q"], writes=["WbufA", "WbufB"], tag="cast_" + wsrc)
        if wsrc == "Wout":
            _nw_fold()
        if kc == 8:
            for hf2 in range(2):
                add("QG", lambda e, b=D[wscr], h=hf2: e.dma_start(
                        out=b[:, h * 8192:(h + 1) * 8192],
                        in_=S["Wbuf"][:].rearrange("p (c n) -> p c n", c=8)
                        [:, :, h * 1024:(h + 1) * 1024]),
                    reads=["WbufA", "WbufB"], writes=[wscr], tag=f"st_{wscr}{hf2}")
        else:
            add("QG", lambda e, b=D[wscr]: e.dma_start(out=b, in_=S["Wbuf"][:]),
                reads=["WbufA", "WbufB"], writes=[wscr], tag="st_" + wscr)

    import os as _os
    if _os.environ.get("KLNONLY"):
        _ln_dt(X, 0)
        return
    for i in range(X.n_tiles):
        _tile(X, i)

    import os
    if int(os.environ.get("KPHASES", "99")) >= 6:
        add("G", lambda e: e.nop(),
            reads=[f"outdma{i}" for i in range(X.n_tiles)], tag="final")


def _ln_dt(X, j):
    S, PS, D, add = X.S, X.PS, X.D, X.P.add
    pj = j % 2
    xP = f"xtP{pj}"
    fmP = f"xnfmP{pj}"
    dtP = f"dtP{pj}"
    xtj = S["xt"][:, pj]
    add("QS", lambda e: e.dma_start(
            out=xtj, in_=D["x"][j * 256:(j + 1) * 256, :]
            .rearrange("(tc p) n -> p tc n", p=128)),
        writes=[xP], tag=f"ldx{j}")
    X.tap(f"xt{j}", lambda: xtj.rearrange("p a n -> p (a n)"), [xP])
    for tc in range(2):
        xs = xtj[:, tc, :]
        add("V", lambda e, a=xs: e.tensor_reduce(out=S["lnmv"][:, 0:1], in_=a,
                                                 axis=mybir.AxisListType.X,
                                                 op=ALU.add),
            reads=[xP, "lnmv"], writes=["lnmv"], tag=f"lsum_{j}_{tc}", lagged=True)
        add("S", lambda e, a=xs: e.activation(out=S["xn_tm"][:], in_=a,
                                              func=AF.Square),
            reads=[xP, "xn_tm"], writes=["xn_tm"], tag=f"lsq_{j}_{tc}")
        add("V", lambda e: e.tensor_reduce(out=S["lnsd"][:, 0:1],
                                           in_=S["xn_tm"][:].bitcast(BF16),
                                           axis=mybir.AxisListType.X, op=ALU.add),
            reads=["xn_tm", "lnsd"], writes=["lnsd"], tag=f"lsqr_{j}_{tc}",
            lagged=True)
        add("V", lambda e: e.tensor_scalar(out=S["lnstat"][:, 0:1],
                                           in0=S["lnmv"][:, 0:1],
                                           scalar1=1.0 / 1024, scalar2=None,
                                           op0=ALU.mult),
            reads=["lnmv", "lnstat"], writes=["lnstat"], tag=f"lmean_{j}_{tc}")
        add("V", lambda e: e.tensor_scalar(out=S["lnstat"][:, 1:2],
                                           in0=S["lnsd"][:, 0:1],
                                           scalar1=1.0 / 1024, scalar2=None,
                                           op0=ALU.mult),
            reads=["lnsd", "lnstat"], writes=["lnstat"], tag=f"lms_{j}_{tc}")
        add("V", lambda e: e.tensor_tensor(out=S["lnstat"][:, 2:3],
                                           in0=S["lnstat"][:, 0:1],
                                           in1=S["lnstat"][:, 0:1],
                                           op=ALU.mult),
            reads=["lnstat"], writes=["lnstat"], tag=f"lm2_{j}_{tc}")
        add("V", lambda e: e.tensor_tensor(out=S["lnmv"][:, 1:2],
                                           in0=S["lnstat"][:, 1:2],
                                           in1=S["lnstat"][:, 2:3],
                                           op=ALU.subtract),
            reads=["lnstat", "lnmv"], writes=["lnmv"], tag=f"lvar_{j}_{tc}")
        add("S", lambda e: e.activation(out=S["lnsd"][:], in_=S["lnmv"][:, 1:2],
                                        func=AF.Sqrt, bias=S["scals"][:, 5:6]),
            reads=["lnmv", "epsln", "lnsd"], writes=["lnsd"], tag=f"lnsq_{j}_{tc}")
        add("V", lambda e: e.reciprocal(out=S["lnrs"][:], in_=S["lnsd"][:]),
            reads=["lnsd", "lnrs"], writes=["lnrs"], tag=f"lnrs_{j}_{tc}")
        if tc == 0:
            X.tap(f"lnstat{j}", lambda: S["lnstat"][:, :], ["lnstat"])
            X.tap(f"lnsum{j}", lambda: S["lnmv"][:, :], ["lnmv"])
            X.tap(f"lnsq{j}", lambda: S["lnsd"][:, :], ["lnsd"])
        add("V", lambda e: e.tensor_tensor(out=S["lnmb"][:], in0=S["lnstat"][:, 0:1],
                                           in1=S["lnrs"][:], op=ALU.mult),
            reads=["lnstat", "lnrs", "lnmb"], writes=["lnmb"], tag=f"lnmb_{j}_{tc}")
        add("V", lambda e, a=xs: e.tensor_scalar(
                out=S["xn_tm"][:], in0=a, scalar1=S["lnrs"][:, 0:1],
                scalar2=S["lnmb"][:, 0:1], op0=ALU.mult, op1=ALU.subtract),
            reads=[xP, "lnrs", "lnmb"], writes=["xn_tm"], tag=f"lnap_{j}_{tc}")
        for kc in range(8):
            add("T", lambda e, k=kc: e.transpose(
                    out=PS["pt"][:, k * 128:(k + 1) * 128].bitcast(BF16),
                    in_=S["xn_tm"][:, k * 128:(k + 1) * 128],
                    identity=S["id_bf"][:]),
                reads=["xn_tm", "id_bf"], writes=["pt"], tag=f"xnt_{j}_{tc}_{kc}")
        add("S", lambda e, t=tc: e.copy(
                out=S["xn_fm"][:, pj, :, t * 128:(t + 1) * 128],
                in_=PS["pt"][:].bitcast(BF16).rearrange("p (c n) -> p c n", c=8)),
            reads=["pt"], writes=[fmP], tag=f"xnev_{j}_{tc}")
    for kc in range(8):
        add("T", lambda e, k=kc: e.matmul(
                out=PS["pmm0"][:, 0:256], lhsT=S["Wdt_sb"][:, k, :],
                rhs=S["xn_fm"][:, pj, k, :], start=(k == 0), stop=(k == 7)),
            reads=[fmP, "Wdt_sb"], writes=["pmm0"], tag=f"dtmm_{j}_{kc}")
    add("S", lambda e: e.activation(out=S["dtmp"][:], in_=PS["pmm0"][0:32, 0:256],
                                    func=AF.Abs, bias=S["sc32"][:, 0:1]),
        reads=["pmm0", "dtb"], writes=["dtmp"], tag=f"dta_{j}")
    add("S", lambda e: e.activation(out=S["dtmp2"][:], in_=S["dtmp"][:],
                                    func=AF.Exp, scale=-1.0),
        reads=["dtmp"], writes=["dtmp2"], tag=f"dte_{j}")
    add("V", lambda e: e.tensor_scalar(out=S["dtmp"][:], in0=S["dtmp2"][:],
                                       scalar1=1.0, scalar2=None, op0=ALU.add),
        reads=["dtmp2"], writes=["dtmp"], tag=f"dtp1_{j}")
    add("S", lambda e: e.activation(out=S["dtmp2"][:], in_=S["dtmp"][:], func=AF.Ln),
        reads=["dtmp"], writes=["dtmp2"], tag=f"dtl_{j}")
    add("V", lambda e: e.tensor_scalar(out=S["dtmp"][:], in0=PS["pmm0"][0:32, 0:256],
                                       scalar1=S["sc32"][:, 0:1], scalar2=0.0,
                                       op0=ALU.add, op1=ALU.max),
        reads=["pmm0", "dtb"], writes=["dtmp"], tag=f"dtr_{j}")
    add("V", lambda e: e.tensor_tensor(out=S["dts"][:, pj, 0:256], in0=S["dtmp"][:],
                                       in1=S["dtmp2"][:], op=ALU.add),
        reads=["dtmp", "dtmp2"], writes=[dtP], tag=f"dt_{j}")
    X.tap(f"dt{j}", lambda pj_=pj: S["dts"][:, pj_, :], [dtP])
    X.tap(f"xnfm{j}", lambda pj_=pj: S["xn_fm"][:, pj_].rearrange("p c n -> p (c n)"),
          [fmP])


def _tile(X, i):
    S, add = X.S, X.P.add
    par = i % 2
    seq_start = (i % TILES_PER_SEQ) == 0

    if i == 0:
        _ln_dt(X, 0)
    if i + 1 < X.n_tiles:
        _ln_dt(X, i + 1)
        add("V", lambda e: e.tensor_copy(out=S["dts"][:, par, 256:257],
                                         in_=S["dts"][:, 1 - par, 0:1]),
            reads=[f"dtP{1 - par}", f"dtP{par}"], writes=[f"dtP{par}"],
            tag=f"dtx{i}")
    else:
        add("V", lambda e: e.memset(S["dts"][:, par, 256:257], 1.0),
            reads=[f"dtP{par}"], writes=[f"dtP{par}"], tag=f"dtx{i}")

    add("S", lambda e: e.activation(out=S["ldt"][:], in_=S["dts"][:, par],
                                    func=AF.Ln, scale=0.5),
        reads=[f"dtP{par}"], writes=["ldt"], tag=f"ldt{i}")
    if seq_start:
        add("V", lambda e: e.memset(S["Lc"][:, 0:1], 0.0),
            reads=["Lc"], writes=["Lc"], tag=f"Lz{i}")
    else:
        add("V", lambda e: e.tensor_copy(out=S["Lc"][:, 0:1], in_=S["Lc"][:, 256:257]),
            reads=["Lc"], writes=["Lc"], tag=f"Lcp{i}")
    add("V", lambda e: e.tensor_tensor_scan(
            out=S["Lc"][:, 1:257], data0=S["onesdt_sb"][:],
            data1=S["dts"][:, par, 0:256], initial=S["Lc"][:, 0:1],
            op0=ALU.mult, op1=ALU.add),
        reads=[f"dtP{par}", "onesdt_sb", "Lc"], writes=["Lc"], tag=f"Lsc{i}")
    X.tap(f"L{i}", lambda: S["Lc"][:, :], ["Lc"])

    import os
    ph = int(os.environ.get("KPHASES", "99"))
    if ph >= 1:
        _ffn(X, i, par)
    if ph >= 2:
        _inproj(X, i, par)
    if ph >= 3:
        _angles(X, i, par)
    if ph >= 4:
        _rope(X, i, par)
    if ph >= 5:
        for c in range(NCH):
            _chunk(X, i, c, par, seq_start and c == 0)
    if ph >= 6:
        _epilogue(X, i, par)


def _ffn(X, i, par):
    S, PS, add = X.S, X.PS, X.P.add
    fmP = f"xnfmP{par}"
    add("V", lambda e: e.tensor_scalar(
            out=S["xn_q"][:].rearrange("p c n -> p (c n)"),
            in0=S["xn_fm"][:, par].rearrange("p c n -> p (c n)"),
            scalar1=S["scals"][:, 2:3], scalar2=None, op0=ALU.mult),
        reads=[fmP, "rsin"], writes=["xn_q"], tag=f"xq{i}")
    X.tap(f"xnq{i}", lambda: S["xn_q"][:].rearrange("p c n -> p (c n)"), ["xn_q"])
    for tc in range(2):
        for ng in range(2):
            pm = "pmm0" if ng == 0 else "pmm1"
            for kc in range(8):
                add("T", lambda e, k=kc, t=tc, n=ng, p=pm: e.matmul(
                        out=PS[p][:, 0:512],
                        lhsT=S["xn_q"][:, k, t * 128:(t + 1) * 128],
                        rhs=S["ffn_q"][:, k, n * 512:(n + 1) * 512],
                        start=(k == 0), stop=(k == 7)),
                    reads=["xn_q", "ffn_q"], writes=[pm], tag=f"ffmm{i}_{tc}_{ng}_{kc}")
            add("S", lambda e, p=pm: e.activation(
                    out=S["xn_tm"][:, 0:512].bitcast(BF16), in_=PS[p][:, 0:512],
                    func=AF.Copy, scale=S["scals"][:, 4:5]),
                reads=[pm, "sprod", "xn_tm"], writes=["xn_tm"], tag=f"ffc{i}_{tc}_{ng}")
            add("V", lambda e, n=ng: e.tensor_tensor(
                    out=S["xn_tm"][:, 512:1024].bitcast(BF16),
                    in0=S["xn_tm"][:, 0:512].bitcast(BF16),
                    in1=S["ffnb"][:, n * 512:(n + 1) * 512], op=ALU.add),
                reads=["xn_tm", "ffnb"], writes=["xn_tm"], tag=f"ffv{i}_{tc}_{ng}")
            add("S", lambda e, t=tc, n=ng: e.activation(
                    out=S["gbuf"][:, t, n * 512:(n + 1) * 512],
                    in_=S["xn_tm"][:, 512:1024].bitcast(BF16), func=AF.Silu),
                reads=["xn_tm"], writes=["gbuf"], tag=f"ffs{i}_{tc}_{ng}")
    X.tap(f"g{i}", lambda: S["gbuf"][:].rearrange("p c n -> p (c n)"), ["gbuf"])


def _inproj(X, i, par):
    S, PS, D, add = X.S, X.PS, X.D, X.P.add
    fmP = f"xnfmP{par}"
    for scr, dstb, kind in [("scr_B", "bufA", "B"), ("scr_C", "bufB", "C")]:
        for hf in range(2):
            add("QS", lambda e, a=D[scr], h=hf: e.dma_start(
                    out=S["Wbuf"][:, h * 8192:(h + 1) * 8192],
                    in_=a[:, h * 8192:(h + 1) * 8192]),
                writes=["WbufA" if hf == 0 else "WbufB"], tag=f"ldw{kind}{i}_{hf}")
        for mc in range(16):
            pm = "pmm0" if (mc % 2 == 0) else "pmm1"
            half = 0 if mc < 8 else 1
            for kc in range(8):
                add("T", lambda e, m=mc, k=kc, p=pm, h=half: e.matmul(
                        out=PS[p][:, 0:256],
                        lhsT=S["Wbuf"][:, h * 8192 + k * 1024 + (m % 8) * 128:
                                        h * 8192 + k * 1024 + (m % 8) * 128 + 128],
                        rhs=S["xn_fm"][:, par, k, :],
                        start=(k == 0), stop=(k == 7)),
                    reads=["WbufA" if half == 0 else "WbufB", fmP],
                    writes=[pm], tag=f"ip{kind}{i}_{mc}_{kc}")
            if mc % 2 == 0:
                add("S", lambda e, m=mc, p=pm, d=dstb: e.copy(
                        out=S[d][:, m, :], in_=PS[p][:, 0:256]),
                    reads=[pm], writes=[dstb, f"{dstb}{i}_{mc}"],
                    tag=f"{kind}ev{i}_{mc}")
            else:
                add("V", lambda e, m=mc, p=pm, d=dstb: e.tensor_copy(
                        out=S[d][:, m, :], in_=PS[p][:, 0:256]),
                    reads=[pm], writes=[dstb, f"{dstb}{i}_{mc}"],
                    tag=f"{kind}ev{i}_{mc}")
    for hf in range(2):
        add("QS", lambda e, h=hf: e.dma_start(
                out=S["Wbuf"][:, h * 8192:(h + 1) * 8192],
                in_=D["scr_z"][:, h * 8192:(h + 1) * 8192]),
            writes=["WbufA" if hf == 0 else "WbufB"], tag=f"ldwz{i}_{hf}")
    for tc in range(2):
        for ng in range(4):
            pm = "pmm0" if (ng % 2 == 0) else "pmm1"
            half = 0 if ng < 2 else 1
            for kc in range(8):
                add("T", lambda e, t=tc, n=ng, k=kc, p=pm, h=half: e.matmul(
                        out=PS[p][:, 0:512],
                        lhsT=S["xn_fm"][:, par, k, t * 128:(t + 1) * 128],
                        rhs=S["Wbuf"][:, h * 8192 + k * 1024 + (n % 2) * 512:
                                       h * 8192 + k * 1024 + (n % 2) * 512 + 512],
                        start=(k == 0), stop=(k == 7)),
                    reads=["WbufA" if half == 0 else "WbufB", fmP],
                    writes=[pm], tag=f"zm{i}_{tc}_{ng}_{kc}")
            add("S", lambda e, t=tc, n=ng, p=pm: e.activation(
                    out=S["zs"][:, t, n * 512:(n + 1) * 512], in_=PS[p][:, 0:512],
                    func=AF.Silu),
                reads=[pm], writes=["zs"], tag=f"zev{i}_{tc}_{ng}")
    for hf in range(2):
        add("QS", lambda e, h=hf: e.dma_start(
                out=S["Wbuf"][:, h * 8192:(h + 1) * 8192],
                in_=D["scr_xr"][:, h * 8192:(h + 1) * 8192]),
            writes=["WbufA" if hf == 0 else "WbufB"], tag=f"ldwxr{i}_{hf}")
    for tc in range(2):
        for ng in range(4):
            pm = "pmm0" if (ng % 2 == 0) else "pmm1"
            half = 0 if ng < 2 else 1
            for kc in range(8):
                add("T", lambda e, t=tc, n=ng, k=kc, p=pm, h=half: e.matmul(
                        out=PS[p][:, 0:512],
                        lhsT=S["xn_fm"][:, par, k, t * 128:(t + 1) * 128],
                        rhs=S["Wbuf"][:, h * 8192 + k * 1024 + (n % 2) * 512:
                                       h * 8192 + k * 1024 + (n % 2) * 512 + 512],
                        start=(k == 0), stop=(k == 7)),
                    reads=["WbufA" if half == 0 else "WbufB", fmP],
                    writes=[pm], tag=f"xrm{i}_{tc}_{ng}_{kc}")
            add("S", lambda e, t=tc, n=ng, p=pm: e.copy(
                    out=S["xr"][:, t, n * 512:(n + 1) * 512], in_=PS[p][:, 0:512]),
                reads=[pm], writes=["xr", f"xr{i}_{tc}"], tag=f"xrev{i}_{tc}_{ng}")
    X.tap(f"zs{i}", lambda: S["zs"][:].rearrange("p c n -> p (c n)"), ["zs"])
    X.tap(f"zsm{i}", lambda: S["zs"][:].rearrange("p c n -> p (c n)"), ["zs"])
    X.tap(f"Braw{i}", lambda: S["bufA"][:].rearrange("p c n -> p (c n)"),
          ["bufA"] + [f"bufA{i}_{m}" for m in range(16)])
    X.tap(f"xr{i}", lambda: S["xr"][:].rearrange("p c n -> p (c n)"),
          ["xr", f"xr{i}_0", f"xr{i}_1"])


def _angles(X, i, par):
    S, PS, add = X.S, X.PS, X.P.add
    for hf in range(2):
        for pc in range(4):
            g = 4 * hf + pc
            add("T", lambda e, gg=g, p=pc: e.matmul(
                    out=PS["pe"][:, p * 256:(p + 1) * 256],
                    lhsT=S["selth_sb"][:, gg * 128:(gg + 1) * 128],
                    rhs=S["Lc"][:, 1:257], start=True, stop=True),
                reads=["Lc", "selth_sb"], writes=["pe"], tag=f"thmm{i}_{g}")
        add("V", lambda e: e.tensor_scalar(out=S["e1b"][:], in0=PS["pe"][:],
                                           scalar1=INV_2PI, scalar2=MAGIC,
                                           op0=ALU.mult, op1=ALU.add),
            reads=["pe"], writes=["e1b"], tag=f"thu{i}_{hf}")
        add("V", lambda e: e.tensor_scalar(out=S["scr4k"][:], in0=S["e1b"][:],
                                           scalar1=MAGIC, scalar2=None,
                                           op0=ALU.subtract),
            reads=["e1b"], writes=["scr4k"], tag=f"thk{i}_{hf}")
        add("V", lambda e: e.scalar_tensor_tensor(
                out=S["e1b"][:], in0=S["scr4k"][:], scalar=-TWO_PI,
                in1=PS["pe"][:], op0=ALU.mult, op1=ALU.add),
            reads=["scr4k", "pe", "e1b"], writes=["e1b"], tag=f"thr{i}_{hf}")
        add("S", lambda e, h=hf: e.activation(
                out=S["sth"][:, 4 * h:4 * h + 4, :].rearrange("p c n -> p (c n)"),
                in_=S["e1b"][:], func=AF.Sin),
            reads=["e1b"], writes=["sth"], tag=f"sin{i}_{hf}")
        add("S", lambda e: e.activation(out=S["scr4k"][:], in_=S["e1b"][:],
                                        func=AF.Abs),
            reads=["e1b", "scr4k"], writes=["scr4k"], tag=f"tha{i}_{hf}")
        add("S", lambda e, h=hf: e.activation(
                out=S["cth"][:, 4 * h:4 * h + 4, :].rearrange("p c n -> p (c n)"),
                in_=S["scr4k"][:], func=AF.Sin, scale=-1.0, bias=S["scals"][:, 7:8]),
            reads=["scr4k", "pih"], writes=["cth"], tag=f"cos{i}_{hf}")
    X.tap(f"cth{i}", lambda: S["cth"][:].rearrange("p c n -> p (c n)"), ["cth"])
    X.tap(f"sth{i}", lambda: S["sth"][:].rearrange("p c n -> p (c n)"), ["sth"])


def _rope(X, i, par):
    S, add = X.S, X.P.add

    def rot(srcb, dstb, src_reads):
        src, dst = S[srcb], S[dstb]
        x0 = src[:, 0:8, :].rearrange("p c n -> p (c n)")
        x1 = src[:, 8:16, :].rearrange("p c n -> p (c n)")
        o0 = dst[:, 0:8, :].rearrange("p c n -> p (c n)")
        o1 = dst[:, 8:16, :].rearrange("p c n -> p (c n)")
        ct = S["cth"][:].rearrange("p c n -> p (c n)")
        st = S["sth"][:].rearrange("p c n -> p (c n)")
        t1, t2 = S["Btm"][:], S["X2"][:]
        sr = src_reads + ["cth", "sth"]
        add("V", lambda e: e.tensor_tensor(out=t1, in0=x0, in1=ct, op=ALU.mult),
            reads=sr + ["Btm"], writes=["Btm"], tag=f"rp1{dstb}{i}")
        add("G", lambda e: e.tensor_tensor(out=t2, in0=x1, in1=st, op=ALU.mult),
            reads=sr + ["X2"], writes=["X2"], tag=f"rp2{dstb}{i}")
        add("V", lambda e: e.tensor_tensor(out=o0, in0=t1, in1=t2, op=ALU.subtract),
            reads=["Btm", "X2", dstb], writes=[dstb], tag=f"rp3{dstb}{i}")
        add("V", lambda e: e.tensor_tensor(out=t1, in0=x0, in1=st, op=ALU.mult),
            reads=sr + ["Btm"], writes=["Btm"], tag=f"rp4{dstb}{i}")
        add("G", lambda e: e.tensor_tensor(out=t2, in0=x1, in1=ct, op=ALU.mult),
            reads=sr + ["X2"], writes=["X2"], tag=f"rp5{dstb}{i}")
        add("V", lambda e: e.tensor_tensor(out=o1, in0=t1, in1=t2, op=ALU.add),
            reads=["Btm", "X2", dstb], writes=[dstb], tag=f"rp6{dstb}{i}")

    rot("bufA", "bufC", ["bufA"] + [f"bufA{i}_{m}" for m in range(16)])
    rot("bufB", "bufA", ["bufB"] + [f"bufB{i}_{m}" for m in range(16)])
    X.tap(f"Brot{i}", lambda: S["bufC"][:].rearrange("p c n -> p (c n)"), ["bufC"])
    X.tap(f"Crot{i}", lambda: S["bufA"][:].rearrange("p c n -> p (c n)"), ["bufA"])


def _chunk(X, i, c, par, first_of_seq):
    import os
    sub = int(os.environ.get("KCHSUB", "9"))
    S, PS, add = X.S, X.PS, X.P.add
    cs = slice(c * 128, (c + 1) * 128)
    b_active = i // TILES_PER_SEQ
    boff = b_active * 1024
    dtP = f"dtP{par}"

    add("V", lambda e: e.tensor_scalar(
            out=S["Pq"][:], in0=S["Lc"][:, 1 + c * 128:1 + c * 128 + 128],
            scalar1=S["Lc"][:, c * 128 + 128:c * 128 + 129],
            scalar2=S["sc32"][:, 2:3], op0=ALU.subtract, op1=ALU.mult),
        reads=["Lc", "A", "Pq"], writes=["Pq"], tag=f"P{i}_{c}")
    add("V", lambda e: e.tensor_copy(out=S["Psplit"][0:32, :], in_=S["Pq"][:]),
        reads=["Pq", "Psplit"], writes=["Psplit"], tag=f"Phi{i}_{c}")
    add("V", lambda e: e.tensor_tensor(out=S["Psplit"][32:64, :], in0=S["Pq"][:],
                                       in1=S["Psplit"][0:32, :], op=ALU.subtract),
        reads=["Pq", "Psplit"], writes=["Psplit"], tag=f"Plo{i}_{c}")
    add("T", lambda e: e.transpose(out=PS["pmm0"][:, 384:416], in_=S["Pq"][:],
                                   identity=S["id32"][:]),
        reads=["Pq", "id32", "pmm0"], writes=["pmm0"], tag=f"tM{i}_{c}")
    add("T", lambda e: e.transpose(out=PS["pmm0"][:, 416:448],
                                   in_=S["ldt"][:, c * 128:c * 128 + 128],
                                   identity=S["id32"][:]),
        reads=["ldt", "id32"], writes=["pmm0"], tag=f"tl1{i}_{c}")
    add("T", lambda e: e.transpose(out=PS["pmm0"][:, 448:480],
                                   in_=S["dts"][:, par, c * 128:c * 128 + 128],
                                   identity=S["id32"][:]),
        reads=[dtP, "id32"], writes=["pmm0"], tag=f"tw1{i}_{c}")
    add("T", lambda e: e.transpose(out=PS["pmm0"][:, 480:512],
                                   in_=S["dts"][:, par, c * 128 + 1:c * 128 + 129],
                                   identity=S["id32"][:]),
        reads=[dtP, "id32"], writes=["pmm0"], tag=f"tw2{i}_{c}")
    add("V", lambda e: e.tensor_copy(out=S["Mt"][:], in_=PS["pmm0"][:, 384:416]),
        reads=["pmm0", "Mt"], writes=["Mt"], tag=f"Mt{i}_{c}")
    add("V", lambda e: e.tensor_copy(out=S["l1t"][:], in_=PS["pmm0"][:, 416:448]),
        reads=["pmm0", "l1t"], writes=["l1t"], tag=f"l1t{i}_{c}")
    add("S", lambda e: e.activation(out=S["w1t"][:], in_=PS["pmm0"][:, 448:480],
                                    func=AF.Copy, scale=0.5),
        reads=["pmm0", "w1t"], writes=["w1t"], tag=f"w1t{i}_{c}")
    add("S", lambda e: e.activation(out=S["w2t"][:], in_=PS["pmm0"][:, 480:512],
                                    func=AF.Copy, scale=0.5),
        reads=["pmm0", "w2t"], writes=["w2t"], tag=f"w2t{i}_{c}")
    add("V", lambda e: e.tensor_tensor(out=S["Mcol1"][:], in0=S["Mt"][:],
                                       in1=S["l1t"][:], op=ALU.subtract),
        reads=["Mt", "l1t", "Mcol1"], writes=["Mcol1"], tag=f"Mc1{i}_{c}")
    add("V", lambda e: e.reciprocal(out=S["exS"][:], in_=S["w1t"][:]),
        reads=["w1t", "exS"], writes=["exS"], tag=f"rw1{i}_{c}")
    add("V", lambda e: e.tensor_tensor(out=S["ratio"][:], in0=S["w2t"][:],
                                       in1=S["exS"][:], op=ALU.mult),
        reads=["w2t", "exS", "ratio"], writes=["ratio"], tag=f"ratio{i}_{c}")
    add("V", lambda e: e.tensor_tensor(out=S["l1t"][:], in0=S["w1t"][:],
                                       in1=S["w2t"][:], op=ALU.add),
        reads=["w1t", "w2t", "l1t"], writes=["l1t"], tag=f"wsum{i}_{c}")
    add("S", lambda e: e.activation(out=S["exS"][:], in_=S["Mt"][:], func=AF.Exp,
                                    scale=-1.0),
        reads=["Mt", "exS"], writes=["exS"], tag=f"exS{i}_{c}")
    add("V", lambda e: e.tensor_tensor(out=S["ccol"][:], in0=S["l1t"][:],
                                       in1=S["exS"][:], op=ALU.mult),
        reads=["l1t", "exS", "ccol"], writes=["ccol"], tag=f"ccol{i}_{c}")
    add("V", lambda e: e.tensor_tensor(
            out=S["X2"][:].rearrange("p (h q) -> p h q", h=32),
            in0=S["xr"][:, c, :].rearrange("p (h q) -> p h q", h=32),
            in1=S["ccol"][:].unsqueeze(2).broadcast_to([128, 32, 64]),
            op=ALU.mult),
        reads=["xr", f"xr{i}_{c}", "ccol", "X2"], writes=["X2"], tag=f"X2{i}_{c}")
    for rnd in range(2):
        for k in range(8):
            mc = rnd * 8 + k
            add("T", lambda e, m=mc, kk=k: e.transpose(
                    out=PS["pt"][:, kk * 128:(kk + 1) * 128],
                    in_=S["bufC"][:, m, cs], identity=S["id_f16"][:]),
                reads=["bufC", "id_f16", "pt"], writes=["pt"],
                tag=f"btm{i}_{c}_{mc}")
        add("S", lambda e, r=rnd: e.copy(out=S["Btm"][:, r * 1024:(r + 1) * 1024],
                                         in_=PS["pt"][:]),
            reads=["pt", "Btm"], writes=["Btm"], tag=f"btmev{i}_{c}_{rnd}")

    if sub < 2:
        return
    if not first_of_seq:
        add("V", lambda e: e.tensor_scalar(
                out=S["dcol"][:, 0:1], in0=S["Lc"][:, c * 128 + 128:c * 128 + 129],
                scalar1=S["Lc"][:, c * 128:c * 128 + 1], scalar2=S["sc32"][:, 2:3],
                op0=ALU.subtract, op1=ALU.mult),
            reads=["Lc", "A", "dcol"], writes=["dcol"], tag=f"lndch{i}_{c}")
        add("V", lambda e: e.tensor_scalar(out=S["dtmp"][:, 0:128], in0=S["Pq"][:],
                                           scalar1=S["dcol"][:, 0:1], scalar2=None,
                                           op0=ALU.add),
            reads=["Pq", "dcol", "dtmp"], writes=["dtmp"], tag=f"deca{i}_{c}")
        for pc in range(8):
            add("T", lambda e, p=pc: e.matmul(
                    out=PS["pe"][:, p * 128:(p + 1) * 128],
                    lhsT=S["seldec_sb"][:, p * 128:(p + 1) * 128],
                    rhs=S["dtmp"][:, 0:128], start=True, stop=True),
                reads=["dtmp", "seldec_sb", "pe"], writes=["pe"],
                tag=f"decmm{i}_{c}_{pc}")
        add("S", lambda e: e.activation(
                out=S["decr"][:].rearrange("p c n -> p (c n)"), in_=PS["pe"][:],
                func=AF.Exp),
            reads=["pe", "decr"], writes=["decr"], tag=f"decev{i}_{c}")
        add("V", lambda e: e.tensor_tensor(
                out=S["Cdec"][:],
                in0=S["bufA"][:, :, cs].rearrange("p (b c) n -> p b c n", b=2),
                in1=S["decr"][:].unsqueeze(1).broadcast_to([128, 2, 8, 128]),
                op=ALU.mult),
            reads=["bufA", "decr", "Cdec"], writes=["Cdec"], tag=f"cdec{i}_{c}")
        add("S", lambda e: e.activation(out=S["dcol"][:, 1:2], in_=S["dcol"][:, 0:1],
                                        func=AF.Exp),
            reads=["dcol"], writes=["dcol"], tag=f"dch{i}_{c}")
        add("V", lambda e: e.tensor_scalar(out=S["rhsd"][:], in0=S["i2dch_sb"][:],
                                           scalar1=S["dcol"][:, 1:2], scalar2=None,
                                           op0=ALU.mult),
            reads=["dcol", "i2dch_sb", "rhsd"], writes=["rhsd"], tag=f"rhsd{i}_{c}")
        add("T", lambda e: e.matmul(out=PS["pmm1"][:, 504:512],
                                    lhsT=S["seldch_sb"][:], rhs=S["rhsd"][:],
                                    start=True, stop=True),
            reads=["rhsd", "seldch_sb", "pmm1"], writes=["pmm1"],
            tag=f"dchmm{i}_{c}")

    if sub < 3:
        return
    for q in range(4):
        _qA(X, i, c, q)
        if sub >= 4 and q > 0:
            _qD(X, i, c, q - 1)
        _qB(X, i, c, q)
        if sub >= 4:
            _qC(X, i, c, q, cs)
        if sub >= 5 and q > 0:
            _qE(X, i, c, q - 1, first_of_seq, boff)
    if sub >= 4:
        _qD(X, i, c, 3)
    if sub >= 5:
        _qE(X, i, c, 3, first_of_seq, boff)
    if sub < 6:
        return

    hsl = S["H"][:, boff:boff + 1024]
    hbsl = S["Hb"][:, boff:boff + 1024]
    if not first_of_seq:
        add("V", lambda e: e.tensor_tensor(
                out=S["scr4k"][:].rearrange("p (a b) -> p a b", a=8),
                in0=hsl.rearrange("p (a b) -> p a b", a=8),
                in1=PS["pmm1"][:, 504:512].unsqueeze(2).broadcast_to([128, 8, 128]),
                op=ALU.mult),
            reads=["H", "pmm1", "scr4k"], writes=["scr4k"], tag=f"hdec{i}_{c}")
    for qq in range(4):
        if qq % 2 == 0:
            bk = lambda f: ("psc" if f < 512 else "psc2", f % 512)
            rb = ["psc", "psc2"]
            lo = lambda b: PS["psc"][32 * b:32 * b + 32, :]
            hi = lambda b: PS["psc2"][32 * b:32 * b + 32, :]
        else:
            bk = lambda f: ("pe", f)
            rb = ["pe"]
            lo = lambda b: PS["pe"][32 * b:32 * b + 32, 0:512]
            hi = lambda b: PS["pe"][32 * b:32 * b + 32, 512:1024]
        for mc in range(8):
            h = qq * 8 + mc
            for piece in range(2):
                cb = mc * 128 + (piece * 1024)
                fo = mc * 128 + piece * 64
                pgn, pf = bk(fo)
                add("T", lambda e, c0=cb, f=pf, hh_=h, p=pgn: e.matmul(
                        out=PS[p][:, f:f + 64],
                        lhsT=S["Btm"][:, c0:c0 + 128],
                        rhs=S["X2"][:, hh_ * 64:hh_ * 64 + 64],
                        start=True, stop=True),
                    reads=["Btm", "X2"] + rb, writes=rb,
                    tag=f"st{i}_{c}_{h}_{piece}")
        if first_of_seq:
            add("V", lambda e, b=qq, f=lo: e.tensor_copy(
                    out=hsl[32 * b:32 * b + 32, 0:512], in_=f(b)),
                reads=rb + ["H"], writes=["H"], tag=f"hb0{i}_{c}_{qq}")
            add("S", lambda e, b=qq, f=hi: e.copy(
                    out=hsl[32 * b:32 * b + 32, 512:1024], in_=f(b)),
                reads=rb + ["H"], writes=["H"], tag=f"hb1{i}_{c}_{qq}")
        else:
            add("V", lambda e, b=qq, f=lo: e.tensor_tensor(
                    out=hsl[32 * b:32 * b + 32, 0:512],
                    in0=S["scr4k"][32 * b:32 * b + 32, 0:512],
                    in1=f(b), op=ALU.add),
                reads=rb + ["H", "scr4k"], writes=["H"], tag=f"hb0{i}_{c}_{qq}")
            add("V", lambda e, b=qq, f=hi: e.tensor_tensor(
                    out=hsl[32 * b:32 * b + 32, 512:1024],
                    in0=S["scr4k"][32 * b:32 * b + 32, 512:1024],
                    in1=f(b), op=ALU.add),
                reads=rb + ["H", "scr4k"], writes=["H"], tag=f"hb1{i}_{c}_{qq}")
    add("G", lambda e: e.tensor_copy(out=hbsl, in_=hsl),
        reads=["H", "Hb"], writes=["Hb"], tag=f"hb{i}_{c}")
    X.tap(f"H{i}_{c}", lambda: S["H"][:, :], ["H"])


def _qA(X, i, c, q):
    S, PS, add = X.S, X.PS, X.P.add
    qb = 32 * q
    add("QA", lambda e: e.dma_start(out=S["Pflat"][qb:qb + 1, :],
                                    in_=S["Psplit"][q * 8:q * 8 + 8, :]),
        reads=["Psplit", "Pflat"], writes=["Pflat"], tag=f"pf0{i}_{c}_{q}")
    add("QA", lambda e: e.dma_start(out=S["Pflat"][qb + 1:qb + 2, :],
                                    in_=S["Psplit"][32 + q * 8:32 + q * 8 + 8, :]),
        reads=["Psplit", "Pflat"], writes=["Pflat"], tag=f"pf1{i}_{c}_{q}")
    for hf in range(2):
        sl = slice(hf * 512, (hf + 1) * 512)
        add("T", lambda e, s=sl: e.matmul(out=PS["pe"][:, s], lhsT=S["id_bf"][:],
                                          rhs=S["tri4"][:, 0:512], start=True,
                                          stop=False),
            reads=["tri4", "id_bf", "pe"], writes=["pe"], tag=f"et{i}_{c}_{q}_{hf}")
        add("T", lambda e, s=sl: e.matmul(out=PS["pe"][:, s],
                                          lhsT=S["ones2"][qb:qb + 2, :],
                                          rhs=S["Pflat"][qb:qb + 2, s], start=False,
                                          stop=True, tile_position=(qb, 0)),
            reads=["Pflat", "ones2", "pe"], writes=["pe"], tag=f"em{i}_{c}_{q}_{hf}")


def _qB(X, i, c, q):
    S, PS, add = X.S, X.PS, X.P.add
    add("V", lambda e: e.tensor_tensor(
            out=S["e1b"][:].rearrange("p (h t) -> p h t", h=8),
            in0=PS["pe"][:].rearrange("p (h t) -> p h t", h=8),
            in1=S["Mcol1"][:, q * 8:q * 8 + 8].unsqueeze(2)
            .broadcast_to([128, 8, 128]),
            op=ALU.subtract),
        reads=["pe", "Mcol1", "e1b"], writes=["e1b"], tag=f"e1{i}_{c}_{q}")
    add("S", lambda e: e.activation(out=S["M1b"][:], in_=S["e1b"][:], func=AF.Exp),
        reads=["e1b", "M1b"], writes=["M1b"], tag=f"M1{i}_{c}_{q}")
    m2t = S["e1b"][:, 0:512].bitcast(FP16)
    add("G", lambda e: e.tensor_tensor(
            out=m2t.rearrange("p (h t) -> p h t", h=8),
            in0=S["M1b"][:].rearrange("p (h t) -> p h t", h=8),
            in1=S["ratio"][:, q * 8:q * 8 + 8].unsqueeze(2)
            .broadcast_to([128, 8, 128]),
            op=ALU.mult),
        reads=["M1b", "ratio"], writes=["e1b"], tag=f"M2a{i}_{c}_{q}")
    if q == 0 and c == 0:
        X.tap(f"M1_{i}", lambda: S["M1b"][:], ["M1b"])


def _qC(X, i, c, q, cs):
    S, PS, add = X.S, X.PS, X.P.add
    import os as _os
    hh4max = int(_os.environ.get("KHH4", "4"))
    for g in range(2):
        pg = "psc" if g == 0 else "psc2"
        for hh4 in range(hh4max):
            hh = g * 4 + hh4
            h = q * 8 + hh
            m = q
            mcx0 = h % 8
            fo = hh4 * 128
            add("T", lambda e, mm=m, mc=mcx0, f=fo, p=pg, s=cs: e.matmul(
                    out=PS[p][:, f:f + 128],
                    lhsT=S["bufC"][32 * mm:32 * mm + 32, mc, s],
                    rhs=S["bufA"][32 * mm:32 * mm + 32, mc, s],
                    start=True, stop=False, tile_position=(32 * mm, 0)),
                reads=["bufA", "bufC", pg], writes=[pg], tag=f"sc0{i}_{c}_{h}")
            add("T", lambda e, mm=m, mc=8 + mcx0, f=fo, p=pg, s=cs: e.matmul(
                    out=PS[p][:, f:f + 128],
                    lhsT=S["bufC"][32 * mm:32 * mm + 32, mc, s],
                    rhs=S["bufA"][32 * mm:32 * mm + 32, mc, s],
                    start=False, stop=True, tile_position=(32 * mm, 0)),
                reads=["bufA", "bufC"], writes=[pg], tag=f"sc1{i}_{c}_{h}")


def _qD(X, i, c, q):
    S, PS, add = X.S, X.PS, X.P.add
    m2t = S["e1b"][:, 0:512].bitcast(FP16)
    add("V", lambda e: e.tensor_tensor(
            out=S["M2b"][:].rearrange("p (h t) -> p h t", h=8),
            in0=m2t.rearrange("p (h t) -> p h t", h=8),
            in1=S["antiI"][:].unsqueeze(1).broadcast_to([128, 8, 128]),
            op=ALU.mult),
        reads=["e1b", "antiI", "M2b"], writes=["M2b"], tag=f"M2b{i}_{c}_{q}")
    for g in range(2):
        pg = "psc" if g == 0 else "psc2"
        add("V", lambda e, p=pg, gg=g: e.tensor_tensor(
                out=S["Sm1"][:, gg * 512:(gg + 1) * 512], in0=PS[p][:],
                in1=S["M1b"][:, gg * 512:(gg + 1) * 512], op=ALU.mult),
            reads=[pg, "M1b", "Sm1"], writes=["Sm1"], tag=f"sm1{i}_{c}_{q}_{g}")
        add("V", lambda e, p=pg, gg=g: e.tensor_tensor(
                out=S["Sm2"][:, gg * 512:(gg + 1) * 512], in0=PS[p][:],
                in1=S["M2b"][:, gg * 512:(gg + 1) * 512], op=ALU.mult),
            reads=[pg, "M2b", "Sm2"], writes=["Sm2"], tag=f"sm2{i}_{c}_{q}_{g}")
    if q == 0 and c == 0:
        X.tap(f"Sm1_{i}", lambda: S["Sm1"][:], ["Sm1"])
        X.tap(f"Sm2_{i}", lambda: S["Sm2"][:], ["Sm2"])


def _qE(X, i, c, q, first_of_seq, boff):
    S, PS, add = X.S, X.PS, X.P.add
    m = q
    for hh in range(8):
        h = q * 8 + hh
        m = q
        mcx0 = h % 8
        fo = hh * 64
        smo = hh * 128
        add("T", lambda e, f=fo, so=smo, hh_=h: e.matmul(
                out=PS["py"][:, f:f + 64],
                lhsT=S["Sm1"][:, so:so + 128],
                rhs=S["xr"][:, c, hh_ * 64:hh_ * 64 + 64],
                start=True, stop=False),
            reads=["Sm1", "xr", f"xr{i}_{c}", "py"], writes=["py"],
            tag=f"yi1{i}_{c}_{h}")
        add("T", lambda e, f=fo, so=smo, hh_=h: e.matmul(
                out=PS["py"][:, f:f + 64],
                lhsT=S["Sm2"][:, so:so + 128],
                rhs=S["xr"][:, c, hh_ * 64:hh_ * 64 + 64],
                start=False, stop=first_of_seq),
            reads=["Sm2"], writes=["py"], tag=f"yi2{i}_{c}_{h}")
        if not first_of_seq:
            add("T", lambda e, f=fo, mm=m, mc=mcx0: e.matmul(
                    out=PS["py"][:, f:f + 64],
                    lhsT=S["Cdec"][32 * mm:32 * mm + 32, 0, mc, :],
                    rhs=S["Hb"][32 * mm:32 * mm + 32,
                                boff + mc * 128:boff + mc * 128 + 64],
                    start=False, stop=False, tile_position=(32 * mm, 0)),
                reads=["Hb", "Cdec"], writes=["py"], tag=f"yn0{i}_{c}_{h}")
            add("T", lambda e, f=fo, mm=m, mc=mcx0: e.matmul(
                    out=PS["py"][:, f:f + 64],
                    lhsT=S["Cdec"][32 * mm:32 * mm + 32, 1, mc, :],
                    rhs=S["Hb"][32 * mm:32 * mm + 32,
                                boff + mc * 128 + 64:boff + mc * 128 + 128],
                    start=False, stop=True, tile_position=(32 * mm, 0)),
                reads=["Hb", "Cdec"], writes=["py"], tag=f"yn1{i}_{c}_{h}")
    add("V", lambda e: e.tensor_tensor(
            out=S["yg"][:, c, q * 512:(q + 1) * 512], in0=PS["py"][:],
            in1=S["zs"][:, c, q * 512:(q + 1) * 512], op=ALU.mult),
        reads=["py", "zs", "yg"], writes=["yg"], tag=f"ygev{i}_{c}_{q}")


def _epilogue(X, i, par):
    S, PS, D, add = X.S, X.PS, X.D, X.P.add
    X.tap(f"yg{i}", lambda: S["yg"][:].rearrange("p c n -> p (c n)"), ["yg"])
    for c in range(NCH):
        add("S", lambda e, cc=c: e.activation(out=S["e1b"][:],
                                              in_=S["yg"][:, cc, 0:1024],
                                              func=AF.Square),
            reads=["yg", "e1b"], writes=["e1b"], tag=f"rsq0{i}_{c}")
        add("V", lambda e: e.tensor_reduce(out=S["lnmv"][:, 0:1], in_=S["e1b"][:],
                                           axis=mybir.AxisListType.X, op=ALU.add),
            reads=["e1b", "lnmv"], writes=["lnmv"], tag=f"rr0{i}_{c}", lagged=True)
        add("S", lambda e, cc=c: e.activation(out=S["e1b"][:],
                                              in_=S["yg"][:, cc, 1024:2048],
                                              func=AF.Square),
            reads=["yg", "e1b"], writes=["e1b"], tag=f"rsq1{i}_{c}")
        add("V", lambda e: e.tensor_reduce(out=S["lnmv"][:, 1:2], in_=S["e1b"][:],
                                           axis=mybir.AxisListType.X, op=ALU.add),
            reads=["e1b", "lnmv"], writes=["lnmv"], tag=f"rr1{i}_{c}", lagged=True)
        add("V", lambda e: e.tensor_tensor(out=S["lnsd"][:], in0=S["lnmv"][:, 0:1],
                                           in1=S["lnmv"][:, 1:2], op=ALU.add),
            reads=["lnmv", "lnsd"], writes=["lnsd"], tag=f"rms{i}_{c}")
        add("S", lambda e: e.activation(out=S["lnrs"][:], in_=S["lnsd"][:],
                                        func=AF.Ln, scale=1.0 / 2048.0,
                                        bias=S["scals"][:, 6:7]),
            reads=["lnsd", "epsrms", "lnrs"], writes=["lnrs"], tag=f"rln{i}_{c}")
        add("S", lambda e: e.activation(out=S["lnmb"][:], in_=S["lnrs"][:],
                                        func=AF.Exp, scale=-0.5),
            reads=["lnrs", "lnmb"], writes=["lnmb"], tag=f"rex{i}_{c}")
        add("V", lambda e, cc=c: e.tensor_scalar(out=S["Btm"][:],
                                                 in0=S["yg"][:, cc, :],
                                                 scalar1=S["lnmb"][:, 0:1],
                                                 scalar2=None, op0=ALU.mult),
            reads=["yg", "lnmb", "Btm"], writes=["Btm"], tag=f"ynm{i}_{c}")
        for rnd in range(2):
            for k in range(8):
                fb = rnd * 8 + k
                add("T", lambda e, f=fb, kk=k: e.transpose(
                        out=PS["pt"][:, kk * 128:(kk + 1) * 128],
                        in_=S["Btm"][:, f * 128:(f + 1) * 128],
                        identity=S["id_f16"][:]),
                    reads=["Btm", "id_f16", "pt"], writes=["pt"],
                    tag=f"ynt{i}_{c}_{fb}")
            add("S", lambda e, r=rnd, cc=c: e.copy(
                    out=S["bufB"][:, r * 8:(r + 1) * 8, cc * 128:(cc + 1) * 128],
                    in_=PS["pt"][:].rearrange("p (a b) -> p a b", a=8)),
                reads=["pt", "bufB"], writes=["bufB", "yn"], tag=f"ynev{i}_{c}_{rnd}")
    X.tap(f"yn{i}", lambda: S["bufB"][:].rearrange("p c n -> p (c n)"),
          ["bufB", "yn"])
    for hf in range(2):
        add("QS", lambda e, h=hf: e.dma_start(
                out=S["Wbuf"][:, h * 8192:(h + 1) * 8192],
                in_=D["scr_out"][:, h * 8192:(h + 1) * 8192]),
            writes=["WbufA" if hf == 0 else "WbufB"], tag=f"ldwo{i}_{hf}")
    for tc in range(2):
        for ng in range(2):
            pm = "pmm0" if ng == 0 else "pmm1"
            for kc in range(16):
                add("T", lambda e, t=tc, n=ng, k=kc, p=pm: e.matmul(
                        out=PS[p][:, 0:512],
                        lhsT=S["bufB"][:, k, t * 128:(t + 1) * 128],
                        rhs=S["Wbuf"][:, k * 1024 + n * 512:k * 1024 + n * 512 + 512],
                        start=(k == 0), stop=(k == 15)),
                    reads=["WbufA" if kc < 8 else "WbufB", "bufB", "yn", pm],
                    writes=[pm], tag=f"op{i}_{tc}_{ng}_{kc}")
            add("V", lambda e, t=tc, n=ng, p=pm: e.tensor_tensor(
                    out=S["sgb"][:, n * 512:(n + 1) * 512], in0=PS[p][:],
                    in1=S["gbuf"][:, t, n * 512:(n + 1) * 512], op=ALU.mult),
                reads=[pm, "gbuf", "sgb"], writes=["sgb"], tag=f"sg{i}_{tc}_{ng}")
        add("V", lambda e, t=tc: e.tensor_tensor(
                out=S["outb"][:], in0=S["sgb"][:],
                in1=S["xt"][:, par, t, :], op=ALU.add),
            reads=["sgb", f"xtP{par}", "outb"], writes=["outb"], tag=f"ob{i}_{tc}")
        add("QS", lambda e, t=tc: e.dma_start(
                out=D["out"][i * 256 + t * 128:i * 256 + t * 128 + 128, :],
                in_=S["outb"][:]),
            reads=["outb"], writes=[f"outdma{i}" if tc == 1 else f"outdmaa{i}"],
            tag=f"st_out{i}_{tc}")
    X.tap(f"s{i}", lambda: S["sgb"][:], ["sgb"])



def kernel(**inputs):
    in_maps = setup_core_inputs(inputs)
    nc = build_nc()
    res = run_bass_kernel_spmd(nc, in_maps, core_ids=list(range(N_CORES)))
    out = np.concatenate(
        [np.asarray(r["out"]).reshape(2, 2048, 1024) for r in res.results], axis=0)
    return np.ascontiguousarray(out.reshape(16, 2048, 1024)).astype(np.float32)
